# version 01
# baseline (speedup 1.0000x reference)
import math
import numpy as np
import ml_dtypes
import concourse.bass as bass
import concourse.mybir as mybir
from concourse.bass_utils import run_bass_kernel_spmd
from contextlib import ExitStack

F32 = mybir.dt.float32
I32 = mybir.dt.int32
BF16 = mybir.dt.bfloat16
AF = mybir.ActivationFunctionType
ALU = mybir.AluOpType
AX = mybir.AxisListType
BF16NP = ml_dtypes.bfloat16

ENG_NAMES = ("tensor", "vector", "scalar", "gpsimd", "sync")
REDUCE_WAITS = True



class _Op:
    __slots__ = ("engine", "emit", "reads", "writes", "dma_ch", "gid",
                 "inc_marked", "inc_index", "edges", "tag", "eng_pos")

    def __init__(self, engine, emit, reads, writes, dma_ch, gid, tag):
        self.engine = engine
        self.emit = emit
        self.reads = reads
        self.writes = writes
        self.dma_ch = dma_ch
        self.gid = gid
        self.inc_marked = False
        self.inc_index = None
        self.edges = []
        self.tag = tag
        self.eng_pos = None


class Region:
    __slots__ = ("name", "p0", "p1", "b0", "b1")

    def __init__(self, name, p0, p1, b0, b1):
        self.name, self.p0, self.p1, self.b0, self.b1 = name, p0, p1, b0, b1

    def overlaps(self, o):
        return (self.name == o.name and self.p0 < o.p1 and o.p0 < self.p1
                and self.b0 < o.b1 and o.b0 < self.b1)

    def covers(self, o):
        return (self.name == o.name and self.p0 <= o.p0 and o.p1 <= self.p1
                and self.b0 <= o.b0 and o.b1 <= self.b1)


PSUM_BANK_B = 2048


def _ap_region(ap):
    t = ap.tensor
    reg = _ap_region_raw(ap)
    if type(t).__name__ == "PSumTensorHandle":
        reg.b0 = (reg.b0 // PSUM_BANK_B) * PSUM_BANK_B
        reg.b1 = ((reg.b1 + PSUM_BANK_B - 1) // PSUM_BANK_B) * PSUM_BANK_B
        reg.p0, reg.p1 = 0, 256
    return reg


def _ap_region_raw(ap):
    t = ap.tensor
    name = t.name
    dts = mybir.dt.size(ap.dtype)
    shape = list(t.shape)
    is_sb = hasattr(t, "base_partition")
    dims = [list(d) for d in ap.ap]
    off_b = ap.offset * dts
    if not is_sb:
        lo = hi = off_b
        for step, count in dims:
            sb = step * dts
            if count <= 0:
                count = 1
            if sb >= 0:
                hi += sb * (count - 1)
            else:
                lo += sb * (count - 1)
        return Region(name, 0, 1, lo, hi + dts)
    tdts = mybir.dt.size(t.dtype)
    row_b = tdts
    for s in shape[1:]:
        row_b *= s
    base_p = getattr(t, "base_partition", 0) or 0
    if dims and dims[0][0] * dts == row_b and dims[0][0] > 0:
        p0 = off_b // row_b
        c0 = off_b % row_b
        lo = hi = c0
        for step, count in dims[1:]:
            sb = step * dts
            if count <= 0:
                count = 1
            if sb >= 0:
                hi += sb * (count - 1)
            else:
                lo += sb * (count - 1)
        if lo < 0 or hi + dts > row_b:
            return Region(name, 0, 256, 0, row_b)
        return Region(name, base_p + p0, base_p + p0 + dims[0][1],
                      lo, hi + dts)
    lo = hi = off_b % row_b
    p0 = off_b // row_b
    ok = True
    for step, count in dims:
        sb = step * dts
        if count <= 0:
            count = 1
        if sb >= 0:
            hi += sb * (count - 1)
        else:
            lo += sb * (count - 1)
    if lo >= 0 and hi + dts <= row_b:
        return Region(name, base_p + p0, base_p + p0 + 1, lo, hi + dts)
    return Region(name, 0, 256, 0, row_b)


def _aap(x):
    if x is None or isinstance(x, (int, float, bass.AP)):
        return x
    return x[:]


CMAX = 64


class Builder:
    def __init__(self, nc, paranoid=False, consts=True):
        self.nc = nc
        self.ops = []
        self.es = ExitStack()
        self.paranoid = paranoid
        self._uid = 0
        self._dma_counts = {}
        self._batch_chs = set()
        self._sb_names = set()
        self._const_vals = []
        self._const_idx = {}
        if consts:
            cd = nc.dram_tensor("c_scalars", [128, CMAX], F32,
                                kind="ExternalInput").ap()
            idd = nc.dram_tensor("c_ident", [128, 128], BF16,
                                 kind="ExternalInput").ap()
            onb = nc.dram_tensor("c_onesb", [128, 512], BF16,
                                 kind="ExternalInput").ap()
            onf = nc.dram_tensor("c_onesf", [128, 512], F32,
                                 kind="ExternalInput").ap()
            self.csb = self.sb("csb", [128, CMAX], F32)
            self.ident = self.sb("ident_sb", [128, 128], BF16)
            self.onesb = self.sb("onesb_sb", [128, 512], BF16)
            self.onesf = self.sb("onesf_sb", [128, 512], F32)
            self.dma(self.csb, cd, ch="consts", batch=True, tag="consts")
            self.dma(self.ident, idd, ch="consts", batch=True, tag="consts")
            self.dma(self.onesb, onb, ch="consts", batch=True, tag="consts")
            self.dma(self.onesf, onf, ch="consts", batch=True, tag="consts")

    def cs(self, value):
        v = float(value)
        if v not in self._const_idx:
            assert len(self._const_vals) < CMAX
            self._const_idx[v] = len(self._const_vals)
            self._const_vals.append(v)
        k = self._const_idx[v]
        return self.csb[:, k:k + 1]

    def host_consts(self):
        c = np.zeros((128, CMAX), np.float32)
        for i, v in enumerate(self._const_vals):
            c[:, i] = v
        return {
            "c_scalars": c,
            "c_ident": np.eye(128, dtype=BF16NP),
            "c_onesb": np.ones((128, 512), BF16NP),
            "c_onesf": np.ones((128, 512), np.float32),
        }

    def sb(self, name, shape, dtype):
        assert name not in self._sb_names, name
        self._sb_names.add(name)
        return self.es.enter_context(self.nc.sbuf_tensor(name, shape, dtype))

    def ps(self, name, shape, dtype=F32):
        return self.es.enter_context(self.nc.psum_tensor(name, shape, dtype))

    def uid(self):
        self._uid += 1
        return self._uid

    def op(self, engine, emit, reads, writes, dma_ch=None, tag=""):
        r = [x if isinstance(x, Region) else _ap_region(x) for x in reads]
        w = [x if isinstance(x, Region) else _ap_region(x) for x in writes]
        o = _Op(engine, emit, r, w, dma_ch, len(self.ops), tag)
        if dma_ch is not None:
            self._dma_counts[dma_ch] = self._dma_counts.get(dma_ch, 0) + 1
            o.inc_index = self._dma_counts[dma_ch]
        self.ops.append(o)
        return o

    def dma(self, out, in_, ch="ld", on="sync", batch=False, tag="dma"):
        out, in_ = _aap(out), _aap(in_)
        if batch:
            self._batch_chs.add(ch)

        def emit(eng):
            return eng.dma_start(out=out, in_=in_)
        return self.op(on, emit, [in_], [out], dma_ch=ch, tag=tag)

    def mm_group(self, psum_ap, pairs, start=True, stop=True, tag="mm"):
        psum_ap = _aap(psum_ap)
        pairs = [(_aap(l), _aap(r)) for (l, r) in pairs]
        n = len(pairs)

        def emit(eng):
            last = None
            for i, (l, r) in enumerate(pairs):
                last = eng.matmul(psum_ap, l, r,
                                  start=(start and i == 0),
                                  stop=(stop and i == n - 1))
            return last
        reads = []
        for l, r in pairs:
            reads.append(l)
            reads.append(r)
        return self.op("tensor", emit, reads, [psum_ap], tag=tag)

    def transpose(self, psum_ap, in_ap, ident_ap, tag="tr"):
        psum_ap, in_ap, ident_ap = _aap(psum_ap), _aap(in_ap), _aap(ident_ap)
        def emit(eng):
            return eng.transpose(psum_ap, in_ap, ident_ap)
        return self.op("tensor", emit, [in_ap, ident_ap], [psum_ap], tag=tag)

    def act(self, out, in_, func, bias=0.0, scale=1.0, accum_out=None,
            tag="act"):
        out, in_, bias, scale, accum_out = (_aap(out), _aap(in_), _aap(bias),
                                            _aap(scale), _aap(accum_out))
        extra_r = [b for b in (bias, scale) if isinstance(b, bass.AP)]
        outs = [out] + ([accum_out] if accum_out is not None else [])

        def emit(eng):
            return eng.activation(out=out, in_=in_, func=func, bias=bias,
                                  scale=scale, accum_out=accum_out)
        return self.op("scalar", emit, [in_] + extra_r, outs, tag=tag)

    def tt(self, out, a, b, alu, engine="vector", tag="tt"):
        out, a, b = _aap(out), _aap(a), _aap(b)
        def emit(eng):
            return eng.tensor_tensor(out, a, b, alu)
        return self.op(engine, emit, [a, b], [out], tag=tag)

    def ts(self, out, in_, s1, s2, op0, op1=ALU.bypass, engine="vector",
           accum_out=None, tag="ts"):
        out, in_, s1, s2, accum_out = (_aap(out), _aap(in_), _aap(s1),
                                       _aap(s2), _aap(accum_out))
        extra = [s for s in (s1, s2) if isinstance(s, bass.AP)]
        outs = [out] + ([accum_out] if accum_out is not None else [])

        def emit(eng):
            return eng.tensor_scalar(out, in_, s1, s2, op0, op1,
                                     accum_out=accum_out)
        return self.op(engine, emit, [in_] + extra, outs, tag=tag)

    def stt(self, out, in0, scalar, in1, op0, op1, engine="vector", tag="stt"):
        out, in0, scalar, in1 = _aap(out), _aap(in0), _aap(scalar), _aap(in1)
        extra = [scalar] if isinstance(scalar, bass.AP) else []

        def emit(eng):
            return eng.scalar_tensor_tensor(out, in0, scalar, in1, op0, op1)
        return self.op(engine, emit, [in0, in1] + extra, [out], tag=tag)

    def red(self, out, in_, alu, negate=False, engine="vector", tag="red"):
        out, in_ = _aap(out), _aap(in_)
        def emit(eng):
            return eng.tensor_reduce(out, in_, AX.X, alu, negate=negate)
        return self.op(engine, emit, [in_], [out], tag=tag)

    def copy(self, out, in_, engine="vector", tag="cp"):
        out, in_ = _aap(out), _aap(in_)
        def emit(eng):
            return eng.tensor_copy(out, in_)
        return self.op(engine, emit, [in_], [out], tag=tag)

    def memset(self, out, val, engine="vector", tag="memset"):
        out = _aap(out)
        def emit(eng):
            return eng.memset(out, val)
        return self.op(engine, emit, [], [out], tag=tag)

    def recip(self, out, in_, tag="recip"):
        out, in_ = _aap(out), _aap(in_)
        def emit(eng):
            return eng.reciprocal(out, in_)
        return self.op("vector", emit, [in_], [out], tag=tag)

    def select(self, out, mask, on_true, on_false, tag="sel"):
        out, mask, on_true, on_false = (_aap(out), _aap(mask), _aap(on_true),
                                        _aap(on_false))
        def emit(eng):
            return eng.select(out, mask, on_true, on_false)
        return self.op("vector", emit, [mask, on_true, on_false], [out],
                       tag=tag)

    def raw(self, engine, emit, reads, writes, tag="raw"):
        reads = [_aap(r) for r in reads]
        writes = [_aap(w) for w in writes]
        return self.op(engine, emit, reads, writes, tag=tag)

    def _compute_edges(self):
        last_writes = {}
        reads_since = {}
        for o in self.ops:
            deps = set()
            for r in o.reads:
                for reg, w in last_writes.get(r.name, ()):
                    if r.overlaps(reg):
                        deps.add(w)
            for wreg in o.writes:
                for reg, w in last_writes.get(wreg.name, ()):
                    if wreg.overlaps(reg):
                        deps.add(w)
                for reg, rd in reads_since.get(wreg.name, ()):
                    if wreg.overlaps(reg):
                        deps.add(rd)
            deps.discard(o)
            o.edges = [d for d in deps]
            for wreg in o.writes:
                lw = last_writes.setdefault(wreg.name, [])
                lw[:] = [(reg, w) for (reg, w) in lw if not wreg.covers(reg)]
                lw.append((wreg, o))
                rs = reads_since.setdefault(wreg.name, [])
                rs[:] = [(reg, rr) for (reg, rr) in rs
                         if not wreg.covers(reg)]
            for r in o.reads:
                reads_since.setdefault(r.name, []).append((r, o))

    def _mark_incs(self):
        for o in self.ops:
            for p in o.edges:
                if p.dma_ch is None:
                    p.inc_marked = True
        counters = {e: 0 for e in ENG_NAMES}
        pos = {e: 0 for e in ENG_NAMES}
        for o in self.ops:
            o.eng_pos = pos[o.engine]
            pos[o.engine] += 1
            if o.dma_ch is None and o.inc_marked:
                counters[o.engine] += 1
                o.inc_index = counters[o.engine]
        self._final_counts = counters

    def finalize(self):
        self._compute_edges()
        self._mark_incs()
        nc = self.nc
        sems = {}
        for e in ENG_NAMES:
            sems[e] = self.es.enter_context(nc.semaphore(f"sem_{e}"))
        for c in self._dma_counts:
            sems["dma_" + c] = self.es.enter_context(
                nc.semaphore(f"sem_dma_{c}"))

        import bisect
        ch_gids = {}
        for o in self.ops:
            if o.dma_ch is not None:
                ch_gids.setdefault(o.dma_ch, []).append(o.gid)

        waits = []
        for o in self.ops:
            wl = {}
            for p in o.edges:
                if p.dma_ch is not None:
                    key = "dma_" + p.dma_ch
                    if p.dma_ch in self._batch_chs:
                        val = 16 * bisect.bisect_left(ch_gids[p.dma_ch],
                                                      o.gid)
                    else:
                        val = 16 * p.inc_index
                else:
                    key = p.engine
                    val = p.inc_index
                if wl.get(key, -1) < val:
                    wl[key] = val
            waits.append(wl)
        marks = {e: {} for e in ENG_NAMES}
        for o, wl in zip(self.ops, waits):
            wm = marks[o.engine]
            red = {}
            for k, v in wl.items():
                if k == "tensor" and o.engine == "tensor":
                    continue
                if REDUCE_WAITS and wm.get(k, 0) >= v:
                    continue
                red[k] = v
                wm[k] = max(wm.get(k, 0), v)
            waits[o.gid] = red

        per_engine = {e: [] for e in ENG_NAMES}
        for o in self.ops:
            per_engine[o.engine].append(o)

        block = self.es.enter_context(nc.Block())

        def make_body(ename):
            ops_list = per_engine[ename]

            def body(eng):
                for o in ops_list:
                    for k, v in sorted(waits[o.gid].items()):
                        eng.wait_ge(sems[k], v)
                    inst = o.emit(eng)
                    if o.dma_ch is not None:
                        inst.then_inc(sems["dma_" + o.dma_ch], 16)
                    elif o.inc_marked:
                        inst.then_inc(sems[o.engine], 1)
                if ename == "sync":
                    for c, cnt in sorted(self._dma_counts.items()):
                        if cnt:
                            eng.wait_ge(sems["dma_" + c], 16 * cnt)
            return body

        block.tensor(make_body("tensor"))
        block.vector(make_body("vector"))
        block.scalar(make_body("scalar"))
        block.gpsimd(make_body("gpsimd"))
        block.sync(make_body("sync"))
        self.es.close()
        return nc



T_SEQ = 2048
NT = 16
D_MODEL = 1024


class Arena:

    def __init__(self, B, name, f32_cols):
        self.t = B.sb(name, [128, f32_cols], F32)
        self.cols = f32_cols

    def f32(self, c0, cols):
        assert c0 + cols <= self.cols, "arena overflow"
        return self.t[:, c0:c0 + cols]

    def bf16(self, c0_f32, bf_cols):
        assert 2 * c0_f32 + bf_cols <= 2 * self.cols, "arena overflow"
        return self.t[:, :].bitcast(BF16)[:, 2 * c0_f32:2 * c0_f32 + bf_cols]


class FMT:

    def __init__(self, B, name, C, T, dtype, ap=None):
        self.C, self.T = C, T
        self.nk = (C + 127) // 128
        self.sb = ap if ap is not None else B.sb(name, [128, self.nk * T],
                                                 dtype)

    def rows(self, k):
        return min(128, self.C - 128 * k)

    def chunk(self, k, t0=0, t1=None):
        t1 = self.T if t1 is None else t1
        return self.sb[0:self.rows(k), k * self.T + t0:k * self.T + t1]


class TMT:

    def __init__(self, B, name, T, C, dtype, ap=None):
        self.T, self.C = T, C
        self.nt = (T + 127) // 128
        self.sb = ap if ap is not None else B.sb(name, [128, self.nt * C],
                                                 dtype)

    def tile(self, t, c0=0, c1=None):
        c1 = self.C if c1 is None else c1
        return self.sb[:, t * self.C + c0:t * self.C + c1]


class PsumPool:
    def __init__(self, B, n_f32=6, n_bf16=2, pref="pp"):
        self.f32s = [B.ps(f"{pref}_f{i}", [128, 512], F32)
                     for i in range(n_f32)]
        self.b16s = [B.ps(f"{pref}_b{i}", [128, 1024], BF16)
                     for i in range(n_bf16)]
        self.fi = 0
        self.bi = 0

    def f32(self):
        t = self.f32s[self.fi % len(self.f32s)]
        self.fi += 1
        return t

    def bf16(self):
        t = self.b16s[self.bi % len(self.b16s)]
        self.bi += 1
        return t


def load_x(B, x_dram, name="x"):
    x = TMT(B, name, T_SEQ, D_MODEL, F32)
    B.dma(x.sb.rearrange("p (t c) -> p t c", t=NT),
          x_dram.rearrange("(t p) c -> p t c", p=128), ch=name + "_in",
          batch=True)
    return x


def store_x(B, x, x_dram):
    B.dma(x_dram.rearrange("(t p) c -> p t c", p=128),
          x.sb.rearrange("p (t c) -> p t c", t=x.nt), ch="x_out",
          on="scalar", batch=True)


def pre_ln_tile(B, x_ap, out_ap, cols4, sq, eps=1e-5, pref="ln"):
    C = x_ap.shape[-1]
    negmu = cols4[:, 0:1]
    vs = cols4[:, 1:2]
    std = cols4[:, 2:3]
    rstd = cols4[:, 3:4]
    B.red(negmu, x_ap, ALU.add, negate=True, tag=pref)
    B.ts(negmu, negmu, 1.0 / C, None, ALU.mult, tag=pref)
    B.act(sq[:, 0:C], x_ap, AF.Square, bias=negmu, accum_out=vs, tag=pref)
    B.act(std, vs, AF.Sqrt, bias=B.cs(eps), scale=1.0 / C, tag=pref)
    B.recip(rstd, std)
    B.ts(out_ap, x_ap, negmu, rstd, ALU.add, ALU.mult, tag=pref)


def pre_ln(B, x, out, pref="ln", eps=1e-5, t_tiles=None, sq=None):
    C = x.C
    cols = B.sb(pref + "_cols", [128, 4 * x.nt], F32)
    if sq is None:
        sq = B.sb(pref + "_sq", [128, C], F32)
    for t in (t_tiles if t_tiles is not None else range(x.nt)):
        pre_ln_tile(B, x.tile(t), out.tile(t), cols[:, 4 * t:4 * t + 4], sq,
                    eps=eps, pref=pref)


def tm_to_fm(B, tm, fm, pool, t_tiles=None, k_chunks=None):
    ts_list = list(t_tiles if t_tiles is not None else range(tm.nt))
    ks = list(k_chunks if k_chunks is not None else range((tm.C + 127) // 128))
    for k in ks:
        rows = min(128, tm.C - 128 * k)
        for g0 in range(0, len(ts_list), 8):
            grp = ts_list[g0:g0 + 8]
            bank = pool.bf16()
            for j, t in enumerate(grp):
                B.transpose(bank[0:rows, j * 128:(j + 1) * 128],
                            tm.tile(t, 128 * k, 128 * k + rows),
                            B.ident, tag="tr")
            if grp == list(range(grp[0], grp[0] + len(grp))):
                B.copy(fm.sb[0:rows,
                             k * fm.T + grp[0] * 128:
                             k * fm.T + (grp[0] + len(grp)) * 128],
                       bank[0:rows, 0:len(grp) * 128], tag="trev")
            else:
                for j, t in enumerate(grp):
                    B.copy(fm.chunk(k, t * 128, (t + 1) * 128),
                           bank[0:rows, j * 128:(j + 1) * 128], tag="trev")


def load_weight(B, name, dram_ap, C, N, stage, out_dtype=BF16):
    nk = (C + 127) // 128
    w = FMT(B, "w_" + name, C, N, out_dtype)
    for k in range(nk):
        rows = min(128, C - 128 * k)
        half = (k % 2) * (stage.shape[1] // 2)
        st = stage[0:rows, half:half + N]
        B.dma(st, dram_ap[128 * k:128 * k + rows, :],
              ch=f"w_{name}_{k % 2}")
        B.copy(w.chunk(k), st, tag="wconv")
    return w


def load_vec_cols(B, name, dram_ap, n):
    v = B.sb("v_" + name, [128, n], F32)
    B.dma(v, dram_ap, ch="v_" + name, batch=True)
    return v


def load_rows(B, name, dram_ap, n):
    r = B.sb("r_" + name, [1, n], F32)
    B.dma(r, dram_ap, ch="r_" + name, batch=True)
    return r


def mmA(B, pool, xT, W, t, n0, nn, kcs=None, bias_row=None, extra=None,
        tag="mmA"):
    kcs = kcs if kcs is not None else range(xT.nk)
    bank = pool.f32()
    pairs = []
    for k in kcs:
        pairs.append((xT.chunk(k, t * 128, (t + 1) * 128),
                      W.chunk(k, n0, n0 + nn)))
    if bias_row is not None:
        pairs.append((B.onesf[0:1, 0:128], bias_row[0:1, n0:n0 + nn]))
    if extra:
        pairs.extend(extra)
    B.mm_group(bank[:, 0:nn], pairs, tag=tag)
    return bank[:, 0:nn]


def mmB(B, pool, W, xT, m, t0, nn, kcs=None, bias_row=None, tag="mmB"):
    kcs = kcs if kcs is not None else range(W.nk)
    mrows = min(128, W.T - m * 128)
    bank = pool.f32()
    pairs = []
    for k in kcs:
        pairs.append((W.chunk(k, m * 128, m * 128 + mrows),
                      xT.chunk(k, t0, t0 + nn)))
    if bias_row is not None:
        pairs.append((bias_row[0:1, m * 128:m * 128 + mrows],
                      B.onesf[0:1, 0:nn]))
    B.mm_group(bank[0:mrows, 0:nn], pairs, tag=tag)
    return bank[0:mrows, 0:nn]



PI = math.pi
TWO_PI = 2 * math.pi
SIN_OFF = PI + 1024 * TWO_PI


def prep_l0_attn(inp):
    f32 = np.float32
    rows = np.zeros((1, 4096), f32)
    rows[0, 0:1024] = inp["L0_attn_bq"]
    rows[0, 1024:2048] = inp["L0_attn_bk"]
    rows[0, 2048:3072] = inp["L0_attn_bv"]
    rows[0, 3072:4096] = inp["L0_attn_bo"]
    proj = np.asarray(inp["L0_attn_proj_mat"], f32)
    return {
        "wqT": np.ascontiguousarray(np.asarray(inp["L0_attn_Wq"], f32).T),
        "wkT": np.ascontiguousarray(np.asarray(inp["L0_attn_Wk"], f32).T),
        "wvT": np.ascontiguousarray(np.asarray(inp["L0_attn_Wv"], f32).T),
        "woT": np.ascontiguousarray(np.asarray(inp["L0_attn_Wo"], f32).T),
        "projdup": np.concatenate([proj, proj], 0),
        "rows": rows,
    }


def build_l0_attn(nc):
    xin = nc.dram_tensor("xin", [T_SEQ, D_MODEL], F32,
                         kind="ExternalInput").ap()
    xout = nc.dram_tensor("xout", [T_SEQ, D_MODEL], F32,
                          kind="ExternalOutput").ap()
    wqd = nc.dram_tensor("wqT", [1024, 1024], F32, kind="ExternalInput").ap()
    wkd = nc.dram_tensor("wkT", [1024, 1024], F32, kind="ExternalInput").ap()
    wvd = nc.dram_tensor("wvT", [1024, 1024], F32, kind="ExternalInput").ap()
    wod = nc.dram_tensor("woT", [1024, 1024], F32, kind="ExternalInput").ap()
    projd = nc.dram_tensor("projdup", [128, 128], F32,
                           kind="ExternalInput").ap()
    rowsd = nc.dram_tensor("rows", [1, 4096], F32, kind="ExternalInput").ap()

    B = Builder(nc)
    pool = PsumPool(B, n_f32=4)
    A = Arena(B, "A", 42496)
    x = TMT(B, "x", T_SEQ, D_MODEL, F32, ap=A.f32(0, 16384))
    B.dma(x.sb.rearrange("p (t c) -> p t c", t=NT),
          xin.rearrange("(t p) c -> p t c", p=128), ch="x_in", batch=True)
    h = TMT(B, "h", T_SEQ, D_MODEL, BF16, ap=A.bf16(16384, 16384))
    rr0 = B.sb("rr0", [128, 4 * 512], F32)
    rr = [rr0, rr0]
    pre_ln(B, x, h, sq=rr0[:, 0:1024])
    hT = FMT(B, "hT", D_MODEL, T_SEQ, BF16, ap=A.bf16(24576, 16384))
    tm_to_fm(B, h, hT, pool)

    stage = A.f32(16384, 2048)
    Wq = FMT(B, None, 1024, 1024, BF16, ap=A.bf16(0, 8192))
    Wk = FMT(B, None, 1024, 1024, BF16, ap=A.bf16(4096, 8192))
    Wv = FMT(B, None, 1024, 1024, BF16, ap=A.bf16(8192, 8192))
    Wo = FMT(B, None, 1024, 1024, BF16, ap=A.bf16(12288, 8192))
    for nm, wd_, W in (("q", wqd, Wq), ("k", wkd, Wk), ("v", wvd, Wv),
                       ("o", wod, Wo)):
        for k in range(8):
            st = stage[:, (k % 2) * 1024:(k % 2) * 1024 + 1024]
            B.dma(st, wd_[128 * k:128 * (k + 1), :], ch=f"w{nm}{k % 2}")
            B.copy(W.chunk(k), st, tag="wconv")
    projsb = B.sb("projsb", [128, 128], BF16)
    projst = B.sb("projst", [128, 128], F32)
    B.dma(projst, projd, ch="proj", batch=True)
    B.ts(projsb, projst, 1.0 / (16.0 * PI), None, ALU.mult, tag="wconv")
    rows = load_rows(B, "l0", rowsd, 4096)

    outT = FMT(B, "outT", D_MODEL, T_SEQ, BF16, ap=A.bf16(32768, 16384))
    qTb = [A.bf16(18432 + i * 1024, 2048) for i in range(2)]
    kTb = [A.bf16(20480 + i * 1024, 2048) for i in range(2)]
    vb = [TMT(B, None, T_SEQ, 128, BF16, ap=A.bf16(22528 + i * 1024, 2048))
          for i in range(2)]
    kp = [A.bf16(40960 + i * 128, 256) for i in range(2)]
    qpT = [FMT(B, None, 256, 512, BF16, ap=A.bf16(41216 + i * 512, 1024))
           for i in range(2)]
    kvp = [B.ps(f"kvp{i}", [128, 512], F32) for i in range(2)]
    kvsb = [B.sb(f"kv{i}", [128, 128], BF16) for i in range(2)]
    otk = [B.sb(f"otk{i}", [128, 64], BF16) for i in range(2)]

    def fourier(s, psum, dst_cos, dst_sin):
        n = psum.shape[1]
        M = 12582912.0
        r1 = rr0[:, 0:n]
        t2 = rr0[:, 512:512 + n]
        r2 = rr0[:, 1024:1024 + n]
        f2 = rr0[:, 1536:1536 + n]
        B.ts(r1, psum, M, None, ALU.add, tag="four")
        B.stt(r1, r1, M, psum, ALU.subtract, ALU.subtract, tag="four")
        B.act(dst_sin, r1, AF.Sin, scale=TWO_PI, tag="four")
        B.ts(t2, psum, 0.25, None, ALU.add, tag="four")
        B.ts(r2, t2, M, None, ALU.add, tag="four")
        B.stt(f2, r2, M, t2, ALU.subtract, ALU.subtract, tag="four")
        B.act(dst_cos, f2, AF.Sin, scale=TWO_PI, tag="four")

    for hp in range(8):
        s = hp % 2
        for t0 in range(0, T_SEQ, 512):
            ps = mmB(B, pool, Wq, hT, hp, t0, 512,
                     bias_row=rows[0:1, 0:1024], tag="qkv")
            B.copy(qTb[s][:, t0:t0 + 512], ps, tag="qkv_ev")
            ps = mmB(B, pool, Wk, hT, hp, t0, 512,
                     bias_row=rows[0:1, 1024:2048], tag="qkv")
            B.copy(kTb[s][:, t0:t0 + 512], ps, tag="qkv_ev")
        for t in range(NT):
            ps = mmA(B, pool, hT, Wv, t, hp * 128, 128,
                     bias_row=rows[0:1, 2048:3072], tag="qkv")
            B.copy(vb[s].tile(t), ps, tag="qkv_ev")
        for h_ in (0, 1):
            head = 2 * hp + h_
            base = h_ * 64
            for t in range(NT):
                sk = t % 2
                pk = pool.f32()
                B.mm_group(pk[:, 0:128],
                           [(kTb[s][base:base + 64, t * 128:(t + 1) * 128],
                             projsb[base:base + 64, :])], tag="pk")
                fourier(sk, pk[:, 0:128], kp[sk][:, 0:128],
                        kp[sk][:, 128:256])
                for m in range(2):
                    B.mm_group(kvp[m][:, 0:64],
                               [(kp[sk][:, m * 128:(m + 1) * 128],
                                 vb[s].tile(t, base, base + 64))],
                               start=(t == 0), stop=(t == NT - 1), tag="kv")
            for m in range(2):
                B.copy(kvsb[s][:, m * 64:(m + 1) * 64], kvp[m][:, 0:64],
                       tag="kv_ev")
            for t0 in range(0, T_SEQ, 512):
                sq_ = (t0 // 512) % 2
                pq = pool.f32()
                B.mm_group(pq[:, 0:512],
                           [(projsb[base:base + 64, :],
                             qTb[s][base:base + 64, t0:t0 + 512])], tag="pq")
                fourier(sq_, pq[:, 0:512], qpT[sq_].chunk(0),
                        qpT[sq_].chunk(1))
                bank = pool.bf16()
                for j in range(4):
                    t = t0 // 128 + j
                    ops_ = pool.f32()
                    B.mm_group(ops_[:, 0:64],
                               [(qpT[sq_].chunk(0, j * 128, (j + 1) * 128),
                                 kvsb[s][:, 0:64]),
                                (qpT[sq_].chunk(1, j * 128, (j + 1) * 128),
                                 kvsb[s][:, 64:128])], tag="outh")
                    B.ts(otk[s], ops_[:, 0:64], 1.0 / 128.0, None, ALU.mult,
                         tag="outh_ev")
                    B.transpose(bank[base:base + 64, j * 128:(j + 1) * 128],
                                otk[s], B.ident, tag="outh_tr")
                B.copy(outT.sb[base:base + 64,
                               hp * T_SEQ + t0:hp * T_SEQ + t0 + 512],
                       bank[base:base + 64, 0:512], tag="outh_trev")

    xr = A.f32(16384, 4096)
    res = A.f32(20480, 4096)
    for t in range(NT):
        xt = xr[:, (t % 2) * 2048:(t % 2) * 2048 + 1024]
        rt = res[:, (t % 2) * 2048:(t % 2) * 2048 + 1024]
        B.dma(xt, xin.rearrange("(t p) c -> t p c", p=128)[t],
              ch=f"xr{t % 2}")
        for n0 in (0, 512):
            ps = mmA(B, pool, outT, Wo, t, n0, 512,
                     bias_row=rows[0:1, 3072:4096], tag="wo")
            B.tt(rt[:, n0:n0 + 512], ps, xt[:, n0:n0 + 512], ALU.add,
                 tag="resid")
        B.dma(xout.rearrange("(t p) c -> t p c", p=128)[t], rt,
              ch=f"xo{t % 2}", on="scalar")
    B.finalize()
    return B



def _causal_masks_l0f():
    p = np.arange(128)[:, None]
    c = np.arange(512)[None, :]
    m1 = np.stack([np.where(c <= 128 * j + p, 0.0, -1e30) for j in range(4)])
    mT = np.stack([np.where(c >= 128 * j + p, 0.0, -1e30) for j in range(4)])
    return (np.concatenate([m1.transpose(1, 0, 2).reshape(128, 2048),
                            mT.transpose(1, 0, 2).reshape(128, 2048)], 1)
            .astype(BF16NP))


def prep_l0_ffn(inp):
    f32 = np.float32
    qkvw = np.concatenate([np.asarray(inp["L0_ffn_q_proj"], f32),
                           np.asarray(inp["L0_ffn_k_proj"], f32),
                           np.asarray(inp["L0_ffn_v_proj"], f32)], axis=2)
    return {
        "qkvw": qkvw,
        "ow": np.asarray(inp["L0_ffn_o_proj"], f32),
        "simw": np.asarray(inp["L0_ffn_sim_matrix"], f32),
        "gates": np.asarray(inp["L0_ffn_gates"], f32).reshape(1, 8),
        "masksd": _causal_masks_l0f(),
    }


def build_l0_ffn(nc):
    xin = nc.dram_tensor("xin", [T_SEQ, D_MODEL], F32,
                         kind="ExternalInput").ap()
    xout = nc.dram_tensor("xout", [T_SEQ, D_MODEL], F32,
                          kind="ExternalOutput").ap()
    qkvd = nc.dram_tensor("qkvw", [8, 1024, 384], F32,
                          kind="ExternalInput").ap()
    owd = nc.dram_tensor("ow", [8, 128, 1024], F32, kind="ExternalInput").ap()
    simd = nc.dram_tensor("simw", [1024, 8], F32, kind="ExternalInput").ap()
    gated = nc.dram_tensor("gates", [1, 8], F32, kind="ExternalInput").ap()
    maskd = nc.dram_tensor("masksd", [128, 4096], BF16,
                           kind="ExternalInput").ap()

    B = Builder(nc)
    pool = PsumPool(B, n_f32=4)
    A = Arena(B, "A", 25600)
    xs2 = A.f32(23552, 2048)
    xind = xin.rearrange("(t p) c -> t p c", p=128)
    xoutd = xout.rearrange("(t p) c -> t p c", p=128)
    h = TMT(B, "h", T_SEQ, D_MODEL, BF16, ap=A.bf16(0, 16384))
    lncols = B.sb("lncols", [128, 4 * NT], F32)
    sq = B.sb("sqscr", [128, 1024], F32)
    for t in range(NT):
        xt = xs2[:, (t % 2) * 1024:(t % 2) * 1024 + 1024]
        B.dma(xt, xind[t], ch=f"x{t % 2}")
        pre_ln_tile(B, xt, h.tile(t), lncols[:, 4 * t:4 * t + 4], sq)
    hT = FMT(B, "hT", D_MODEL, T_SEQ, BF16, ap=A.bf16(8192, 16384))
    tm_to_fm(B, h, hT, pool)
    rnorm = B.sb("rnorm", [128, NT], F32)
    rrec = B.sb("rrec", [128, NT], F32)
    hsq = B.sb("hsqscr", [128, 1024], F32)
    for t in range(NT):
        B.act(hsq, h.tile(t), AF.Square, accum_out=rnorm[:, t:t + 1],
              tag="rn")
        B.act(rnorm[:, t:t + 1], rnorm[:, t:t + 1], AF.Sqrt, tag="rn")
        B.ts(rnorm[:, t:t + 1], rnorm[:, t:t + 1], 1e-12, None, ALU.max,
             tag="rn")
        B.recip(rrec[:, t:t + 1], rnorm[:, t:t + 1])

    simsb = FMT(B, "simsb", 1024, 8, BF16)
    simst = B.sb("simst", [128, 64], F32)
    B.dma(simst.rearrange("p (k e) -> p k e", k=8),
          simd.rearrange("(k p) e -> p k e", p=128), ch="sim", batch=True)
    B.copy(simsb.sb, simst, tag="sim")
    simsq = B.sb("simsq", [128, 64], F32)
    B.act(simsq, simst, AF.Square, tag="sim")
    csps = pool.f32()
    B.mm_group(csps[0:1, 0:8],
               [(B.onesf[:, 0:1], simsq[:, 8 * k:8 * k + 8])
                for k in range(8)], tag="simn")
    csr = B.sb("csr", [1, 8], F32)
    B.act(csr, csps[0:1, 0:8], AF.Sqrt, tag="sim")
    B.ts(csr, csr, 1e-12, None, ALU.max, tag="sim")
    B.recip(csr, csr)
    csrb = B.sb("csrb", [128, 8], F32)
    csb16 = B.sb("csb16", [1, 8], BF16)
    B.copy(csb16, csr, tag="sim")
    csps2 = pool.f32()
    B.mm_group(csps2[:, 0:8], [(B.onesb[0:1, 0:128], csb16[0:1, :])],
               tag="simb")
    B.copy(csrb, csps2[:, 0:8], tag="sim")
    gsb = B.sb("gsb", [1, 8], F32)
    B.dma(gsb, gated, ch="gates", batch=True)
    sgate = B.sb("sgate", [1, 8], BF16)
    B.act(sgate, gsb, AF.Sigmoid, tag="gate")
    sgb = B.sb("sgb", [128, 8], F32)
    csps3 = pool.f32()
    B.mm_group(csps3[:, 0:8], [(B.onesb[0:1, 0:128], sgate[0:1, :])],
               tag="gateb")
    B.copy(sgb, csps3[:, 0:8], tag="gate")

    mcols = B.sb("mcols", [128, 8 * NT], F32)
    wcols = B.sb("wcols", [128, 8 * NT], F32)
    rt = B.sb("rtscr", [128, 64], F32)
    for t in range(NT):
        lg = rt[:, 0:8]
        gt_ = rt[:, 8:16]
        fb1 = rt[:, 16:24]
        m2s = rt[:, 24:32]
        ml = rt[:, 32:40]
        ex = rt[:, 40:48]
        col = rt[:, 48:49]
        col2 = rt[:, 49:50]
        mk = mcols[:, 8 * t:8 * t + 8]
        wk = wcols[:, 8 * t:8 * t + 8]
        ps = mmA(B, pool, hT, simsb, t, 0, 8, tag="logit")
        B.stt(lg, ps, rrec[:, t:t + 1], csrb, ALU.mult, ALU.mult, tag="rt")
        B.tt(lg, lg, sgb, ALU.subtract, tag="rt")
        B.ts(gt_, lg, 0.0, None, ALU.max, tag="rt")
        B.ts(mk, lg, 0.0, None, ALU.is_gt, tag="rt")
        B.red(col, mk, ALU.add, tag="rt")
        B.ts(col, col, 0.0, None, ALU.is_equal, tag="rt")
        B.red(col2, lg, ALU.max, tag="rt")
        B.ts(fb1, lg, col2, None, ALU.is_ge, tag="rt")
        B.stt(m2s, fb1, -2e9, lg, ALU.mult, ALU.add, tag="rt")
        B.red(col2, m2s, ALU.max, tag="rt")
        B.ts(m2s, m2s, col2, None, ALU.is_ge, tag="rt")
        B.tt(fb1, fb1, m2s, ALU.add, tag="rt")
        B.stt(mk, fb1, col, mk, ALU.mult, ALU.max, tag="rt")
        B.tt(ml, gt_, mk, ALU.mult, tag="rt")
        B.ts(ex, mk, -1.0, 1e9, ALU.add, ALU.mult, tag="rt")
        B.tt(ml, ml, ex, ALU.add, tag="rt")
        B.red(col2, ml, ALU.max, negate=True, tag="rt")
        B.act(ex, ml, AF.Exp, bias=col2, accum_out=col, tag="rt")
        B.recip(col, col)
        B.stt(wk, ex, col, mk, ALU.mult, ALU.mult, tag="rt")
    qkv = TMT(B, "qkv", T_SEQ, 384, F32, ap=A.f32(0, 6144))
    wstage = B.sb("wstage", [128, 2 * 3072], F32)
    qkvw_sb = [FMT(B, f"qkvw{i}", 1024, 384, BF16) for i in range(2)]
    for e in range(8):
        se = e % 2
        st = wstage[:, se * 3072:se * 3072 + 3072]
        B.dma(st.rearrange("p (k n) -> p k n", k=8),
              qkvd[e].rearrange("(k p) n -> p k n", p=128), ch=f"qkv{se}")
        B.copy(qkvw_sb[se].sb, st, tag="wconv")
        for t in range(NT):
            ps = mmA(B, pool, hT, qkvw_sb[se], t, 0, 384, tag="eqkv")
            mc = mcols[:, 8 * t + e:8 * t + e + 1]
            if e == 0:
                B.ts(qkv.tile(t), ps, mc, None, ALU.mult, tag="eacc")
            else:
                B.stt(qkv.tile(t), ps, mc, qkv.tile(t), ALU.mult, ALU.add,
                      tag="eacc")
    qT = FMT(B, "qT", 128, T_SEQ, BF16, ap=A.bf16(6144, 2048))
    kT = FMT(B, "kT", 128, T_SEQ, BF16, ap=A.bf16(7168, 2048))
    vb = TMT(B, "vb", T_SEQ, 128, BF16, ap=A.bf16(21504, 2048))
    qsc = B.sb("qsc", [128, 128], BF16)
    for t in range(NT):
        B.ts(qsc, qkv.tile(t, 0, 128), 1.0 / math.sqrt(128.0), None,
             ALU.mult, tag="qsc")
        bank = pool.bf16()
        B.transpose(bank[:, 0:128], qsc, B.ident, tag="qtr")
        B.copy(qT.chunk(0, t * 128, (t + 1) * 128), bank[:, 0:128],
               tag="qtr")
        B.copy(qsc, qkv.tile(t, 128, 256), tag="ksc")
        bank2 = pool.bf16()
        B.transpose(bank2[:, 128:256], qsc, B.ident, tag="ktr")
        B.copy(kT.chunk(0, t * 128, (t + 1) * 128), bank2[:, 128:256],
               tag="ktr")
        B.copy(vb.tile(t), qkv.tile(t, 256, 384), tag="vsc")
    masks = B.sb("masks", [128, 4096], BF16)
    B.dma(masks, maskd, ch="masks", batch=True)
    rm = B.sb("rm", [128, NT], F32)
    rscr = B.sb("rscr", [128, 2], F32)
    for t in range(NT):
        nchunks = t // 4 + 1
        for c5 in range(nchunks):
            ps = pool.f32()
            B.mm_group(ps[:, 0:512],
                       [(qT.chunk(0, t * 128, (t + 1) * 128),
                         kT.chunk(0, c5 * 512, (c5 + 1) * 512))], tag="s1")
            if c5 == t // 4:
                j = t % 4
                B.tt(ps[:, 0:512], ps[:, 0:512],
                     masks[:, 512 * j:512 * j + 512], ALU.add, tag="s1m")
            if c5 == 0:
                B.red(rm[:, t:t + 1], ps[:, 0:512], ALU.max, tag="s1r")
            else:
                B.red(rscr[:, 0:1], ps[:, 0:512], ALU.max, tag="s1r")
                B.tt(rm[:, t:t + 1], rm[:, t:t + 1], rscr[:, 0:1], ALU.max,
                     tag="s1r")
    rmb = B.sb("rmb", [128, NT], BF16)
    B.ts(rmb, rm, -1.0, None, ALU.mult, tag="rmneg")
    rmrow = B.sb("rmrow", [1, T_SEQ], BF16)
    for t in range(NT):
        bank = pool.bf16()
        B.transpose(bank[0:1, 0:128], rmb[:, t:t + 1], B.ident, tag="rmtr")
        B.copy(rmrow[0:1, 128 * t:128 * t + 128], bank[0:1, 0:128],
               tag="rmtr")
    ET = B.sb("ET", [128, 16 * 512], BF16)
    ao = TMT(B, "ao", T_SEQ, 128, BF16, ap=A.bf16(20480, 2048))
    dn = B.sb("dnrow", [1, 512], F32)
    for b in range(4):
        nblk = 4 * b + 4
        for tkb in range(nblk):
            ps = pool.f32()
            B.mm_group(ps[:, 0:512],
                       [(kT.chunk(0, tkb * 128, (tkb + 1) * 128),
                         qT.chunk(0, b * 512, (b + 1) * 512)),
                        (B.onesb[0:1, 0:128],
                         rmrow[0:1, b * 512:(b + 1) * 512])], tag="s2")
            if tkb >= 4 * b:
                j = tkb % 4
                B.tt(ps[:, 0:512], ps[:, 0:512],
                     masks[:, 2048 + 512 * j:2048 + 512 * j + 512],
                     ALU.add, tag="s2m")
            B.act(ET[:, 512 * tkb:512 * tkb + 512], ps[:, 0:512], AF.Exp,
                  tag="s2e")
        dps = pool.f32()
        B.mm_group(dps[0:1, 0:512],
                   [(B.onesb[:, 0:1], ET[:, 512 * k:512 * k + 512])
                    for k in range(nblk)], tag="dn")
        B.copy(dn, dps[0:1, 0:512], tag="dn")
        B.recip(dn, dn)
        for j in range(4):
            t = 4 * b + j
            ops_ = pool.f32()
            B.mm_group(ops_[:, 0:128],
                       [(ET[:, 512 * k + 128 * j:512 * k + 128 * j + 128],
                         vb.tile(k)) for k in range(nblk)], tag="pv")
            dcol = pool.f32()
            B.transpose(dcol[:, 0:1], dn[0:1, 128 * j:128 * j + 128],
                        B.cs(1.0)[0:1, 0:1], tag="dtr")
            B.ts(ao.tile(t), ops_[:, 0:128], dcol[:, 0:1], None, ALU.mult,
                 tag="aoev")
    aoT = FMT(B, "aoT", 128, T_SEQ, BF16, ap=A.bf16(22528, 2048))
    tm_to_fm(B, ao, aoT, pool)
    facc = TMT(B, "facc", T_SEQ, D_MODEL, F32, ap=A.f32(0, 16384))
    ostage = B.sb("ostage", [128, 2 * 1024], F32)
    ow_sb = [B.sb(f"ow{i}", [128, 1024], BF16) for i in range(2)]
    for e in range(8):
        se = e % 2
        st = ostage[:, se * 1024:se * 1024 + 1024]
        B.dma(st, owd[e], ch=f"ow{se}")
        B.copy(ow_sb[se], st, tag="wconv")
        for t in range(NT):
            wc = wcols[:, 8 * t + e:8 * t + e + 1]
            for n0 in (0, 512):
                ps = pool.f32()
                B.mm_group(ps[:, 0:512],
                           [(aoT.chunk(0, t * 128, (t + 1) * 128),
                             ow_sb[se][:, n0:n0 + 512])], tag="oproj")
                if e == 0:
                    B.ts(facc.tile(t, n0, n0 + 512), ps[:, 0:512], wc, None,
                         ALU.mult, tag="oacc")
                else:
                    B.stt(facc.tile(t, n0, n0 + 512), ps[:, 0:512], wc,
                          facc.tile(t, n0, n0 + 512), ALU.mult, ALU.add,
                          tag="oacc")
    for t in range(NT):
        xt = xs2[:, (t % 2) * 1024:(t % 2) * 1024 + 1024]
        B.dma(xt, xind[t], ch=f"x{t % 2}")
        B.tt(facc.tile(t), facc.tile(t), xt, ALU.add, tag="resid")
        B.dma(xoutd[t], facc.tile(t), ch=f"xo{t % 2}", on="scalar")
    B.finalize()
    return B



def _rope_tabs_l1a():
    d = 64
    inv = 1.0 / (10000.0 ** (np.arange(0, d, 2) / d))
    fr = np.arange(T_SEQ)[:, None] * inv[None, :]
    cos = np.repeat(np.cos(fr), 2, axis=1).T
    sin = np.repeat(np.sin(fr), 2, axis=1).T
    cos2 = np.concatenate([cos, cos], 0)
    sin2 = np.concatenate([sin, sin], 0)
    return np.concatenate([cos2, sin2], 1).astype(BF16NP)


def _rot_mat_128():
    R = np.zeros((128, 128), np.float32)
    for j in range(64):
        R[2 * j, 2 * j + 1] = -1.0
        R[2 * j + 1, 2 * j] = 1.0
    return R.T.astype(BF16NP)


def prep_l1_attn(inp):
    f32 = np.float32
    pmk = np.asarray(inp["L1_attn_pm_k"], f32)
    pmv = np.asarray(inp["L1_attn_pm_v"], f32)
    pmkT = np.zeros((128, 128), f32)
    pmv_h = np.zeros((16, 1024), f32)
    for h in range(16):
        pmkT[64 * (h % 2):64 * (h % 2) + 64, 16 * (h // 2):16 * (h // 2) + 16] \
            = pmk[h].T
        pmv_h[:, 64 * h:64 * h + 64] = pmv[h]
    vecs = np.zeros((128, 8), f32)
    vecs[:] = np.asarray(inp["L1_attn_norm_w"], f32).reshape(8, 128).T
    rows = np.asarray(inp["L1_attn_b_out"], f32).reshape(1, 1024)
    return {
        "wqkv": np.asarray(inp["L1_attn_w_qkv"], f32),
        "wout": np.asarray(inp["L1_attn_w_out"], f32),
        "pmkT": pmkT, "pmv": pmv_h,
        "vecs": vecs, "rows": rows,
        "tabs": _rope_tabs_l1a(), "rotT": _rot_mat_128(),
        "masksd": _causal_masks_l0f(),
    }


def build_l1_attn(nc):
    xin = nc.dram_tensor("xin", [T_SEQ, D_MODEL], F32,
                         kind="ExternalInput").ap()
    xout = nc.dram_tensor("xout", [T_SEQ, D_MODEL], F32,
                          kind="ExternalOutput").ap()
    wqkvd = nc.dram_tensor("wqkv", [1024, 3072], F32,
                           kind="ExternalInput").ap()
    woutd = nc.dram_tensor("wout", [1024, 1024], F32,
                           kind="ExternalInput").ap()
    pmkTd = nc.dram_tensor("pmkT", [128, 128], F32, kind="ExternalInput").ap()
    pmvd = nc.dram_tensor("pmv", [16, 1024], F32, kind="ExternalInput").ap()
    vecsd = nc.dram_tensor("vecs", [128, 8], F32, kind="ExternalInput").ap()
    rowsd = nc.dram_tensor("rows", [1, 1024], F32, kind="ExternalInput").ap()
    tabsd = nc.dram_tensor("tabs", [128, 2 * T_SEQ], BF16,
                           kind="ExternalInput").ap()
    rotd = nc.dram_tensor("rotT", [128, 128], BF16, kind="ExternalInput").ap()
    maskd = nc.dram_tensor("masksd", [128, 4096], BF16,
                           kind="ExternalInput").ap()

    B = Builder(nc)
    pool = PsumPool(B, n_f32=4)
    A = Arena(B, "A", 43008)
    xs2 = A.f32(40960, 2048)
    xind = xin.rearrange("(t p) c -> t p c", p=128)
    xoutd = xout.rearrange("(t p) c -> t p c", p=128)
    vecs = load_vec_cols(B, "l1a", vecsd, 8)
    tabs = B.sb("tabs_sb", [128, 2 * T_SEQ], BF16)
    B.dma(tabs, tabsd, ch="tabs", batch=True)
    rott = B.sb("rott", [128, 128], BF16)
    B.dma(rott, rotd, ch="rot", batch=True)
    masks = B.sb("masks", [128, 4096], BF16)
    B.dma(masks, maskd, ch="masks", batch=True)
    stage = B.sb("stage_l1a", [128, 1024], F32)
    pmk_sb = B.sb("pmk", [128, 128], BF16)
    B.dma(stage[:, 0:128], pmkTd, ch="pmk", batch=True)
    B.copy(pmk_sb, stage[:, 0:128], tag="wconv")
    pmv_sb = B.sb("pmv_sbuf", [16, 1024], BF16)
    B.dma(stage[0:16, 0:1024], pmvd, ch="pmv", batch=True)
    B.copy(pmv_sb, stage[0:16, 0:1024], tag="wconv")

    xn = TMT(B, "xn", T_SEQ, D_MODEL, BF16, ap=A.bf16(0, 16384))
    cols = B.sb("cols", [128, 8 * NT], F32)
    rsc = B.sb("rsc", [128, 2 * 512], F32)
    sq = rsc[:, 0:1024]
    for t in range(NT):
        xt = xs2[:, (t % 2) * 1024:(t % 2) * 1024 + 1024]
        B.dma(xt, xind[t], ch=f"x{t % 2}")
        c8 = cols[:, 8 * t:8 * t + 8]
        negmu, vs, std, rstd = (c8[:, 0:1], c8[:, 1:2], c8[:, 2:3], c8[:, 3:4])
        t1, rmsstd, rmsr, comb = (c8[:, 4:5], c8[:, 5:6], c8[:, 6:7],
                                  c8[:, 7:8])
        B.red(negmu, xt, ALU.add, negate=True, tag="ln")
        B.ts(negmu, negmu, 1.0 / 1024, None, ALU.mult, tag="ln")
        B.act(sq, xt, AF.Square, bias=negmu, accum_out=vs, tag="ln")
        B.act(std, vs, AF.Sqrt, bias=B.cs(1e-5), scale=1.0 / 1024, tag="ln")
        B.recip(rstd, std)
        B.tt(t1, rstd, rstd, ALU.mult, tag="rms")
        B.tt(t1, t1, vs, ALU.mult, tag="rms")
        B.act(rmsstd, t1, AF.Sqrt, bias=B.cs(1e-6), scale=1.0 / 1024,
              tag="rms")
        B.recip(rmsr, rmsstd)
        B.tt(comb, rstd, rmsr, ALU.mult, tag="rms")
        B.ts(xn.tile(t), xt, negmu, comb, ALU.add, ALU.mult, tag="ln")
    xnT = FMT(B, "xnT", D_MODEL, T_SEQ, BF16, ap=A.bf16(8192, 16384))
    tm_to_fm(B, xn, xnT, pool)

    v = TMT(B, "v", T_SEQ, D_MODEL, BF16, ap=A.bf16(32768, 16384))
    wvstage = stage
    Wv = FMT(B, "Wv", 1024, 1024, BF16, ap=A.bf16(0, 8192))
    for k in range(8):
        st = wvstage[:, 0:1024]
        B.dma(st, wqkvd[128 * k:128 * (k + 1), 2048:3072], ch=f"wv{k % 2}")
        B.ts(Wv.chunk(k), st, vecs[:, k:k + 1], None, ALU.mult, tag="wconv")
    for t in range(NT):
        for n0 in (0, 512):
            ps = mmA(B, pool, xnT, Wv, t, n0, 512, tag="vmm")
            B.copy(v.tile(t, n0, n0 + 512), ps, tag="vmm")

    qT = FMT(B, "qT", D_MODEL, T_SEQ, BF16, ap=A.bf16(16384, 16384))
    kT = FMT(B, "kT", D_MODEL, T_SEQ, BF16, ap=A.bf16(24576, 16384))
    qraw = [A.bf16(40960 + i * 1024, 2048) for i in range(2)]
    wqstage = wvstage
    wqb = [FMT(B, f"wqb{i}", 1024, 128, BF16) for i in range(2)]
    for m in range(16):
        sm = m % 2
        st = wqstage[:, 0:1024]
        B.dma(st.rearrange("p (k n) -> p k n", k=8),
              wqkvd[:, 128 * m:128 * (m + 1)]
              .rearrange("(k p) n -> p k n", p=128), ch=f"wq{sm}")
        for k in range(8):
            if m < 8:
                B.ts(wqb[sm].chunk(k), st[:, 128 * k:128 * k + 128],
                     vecs[:, k:k + 1], 0.125, ALU.mult, ALU.mult, tag="wconv")
            else:
                B.ts(wqb[sm].chunk(k), st[:, 128 * k:128 * k + 128],
                     vecs[:, k:k + 1], None, ALU.mult, tag="wconv")
        dst = qT if m < 8 else kT
        mc = m % 8
        for t0 in range(0, T_SEQ, 512):
            ps = mmB(B, pool, wqb[sm], xnT, 0, t0, 512, tag="qk")
            B.copy(qraw[sm][:, t0:t0 + 512], ps, tag="qk")
        for t0 in range(0, T_SEQ, 512):
            pr = pool.f32()
            B.mm_group(pr[:, 0:512], [(rott, qraw[sm][:, t0:t0 + 512])],
                       tag="rope")
            s1 = rsc[:, 0:512]
            s2 = rsc[:, 512:1024]
            B.tt(s1, qraw[sm][:, t0:t0 + 512], tabs[:, t0:t0 + 512],
                 ALU.mult, tag="rope")
            B.tt(s2, pr[:, 0:512], tabs[:, T_SEQ + t0:T_SEQ + t0 + 512],
                 ALU.mult, tag="rope")
            B.tt(dst.chunk(mc, t0, t0 + 512), s1, s2, ALU.add, tag="rope")

    outT = FMT(B, "outT", D_MODEL, T_SEQ, BF16, ap=A.bf16(0, 16384))
    ET = A.bf16(40960, 4 * 512)
    ETpm = A.bf16(41984, 512)[0:16, :]
    rm = B.sb("rm", [128, 2 * NT], F32)
    rmb = B.sb("rmb", [128, 1], BF16)
    rmrow = B.sb("rmrow", [1, 512], BF16)
    dn = B.sb("dnrow", [1, 512], F32)
    aotk = B.sb("aotk", [128, 64], BF16)
    for seg in range(4):
        for h in range(16):
            hc, base = h // 2, 64 * (h % 2)
            qrows = lambda t0, t1: qT.sb[base:base + 64,
                                         hc * T_SEQ + t0:hc * T_SEQ + t1]
            krows = lambda t0, t1: kT.sb[base:base + 64,
                                         hc * T_SEQ + t0:hc * T_SEQ + t1]
            s0 = seg * 512
            for j in range(4):
                t = 4 * seg + j
                pp = pool.f32()
                B.mm_group(pp[:, 0:16],
                           [(qrows(s0 + 128 * j, s0 + 128 * (j + 1)),
                             pmk_sb[base:base + 64, 16 * hc:16 * hc + 16])],
                           tag="s1pm")
                ps = pool.f32()
                B.mm_group(ps[:, 0:512],
                           [(qrows(s0 + 128 * j, s0 + 128 * (j + 1)),
                             krows(s0, s0 + 512))], tag="s1")
                B.tt(ps[:, 0:512], ps[:, 0:512],
                     masks[:, 512 * j:512 * j + 512], ALU.add, tag="s1m")
                rmc = rm[:, 2 * t:2 * t + 1]
                rs2 = rm[:, 2 * t + 1:2 * t + 2]
                B.red(rmc, pp[:, 0:16], ALU.max, tag="s1r")
                B.red(rs2, ps[:, 0:512], ALU.max, tag="s1r")
                B.tt(rmc, rmc, rs2, ALU.max, tag="s1r")
            for j in range(4):
                t = 4 * seg + j
                B.ts(rmb, rm[:, 2 * t:2 * t + 1], -1.0, None, ALU.mult,
                     tag="rmneg")
                bank = pool.bf16()
                B.transpose(bank[0:1, 0:128], rmb, B.ident, tag="rmtr")
                B.copy(rmrow[0:1, 128 * j:128 * j + 128], bank[0:1, 0:128],
                       tag="rmtr")
            pe = pool.f32()
            B.mm_group(pe[0:16, 0:512],
                       [(pmk_sb[base:base + 64, 16 * hc:16 * hc + 16],
                         qrows(s0, s0 + 512)),
                        (B.onesb[0:1, 0:16], rmrow[0:1, 0:512])], tag="s2pm")
            B.act(ETpm, pe[0:16, 0:512], AF.Exp, tag="s2e")
            for jt in range(4):
                ps = pool.f32()
                B.mm_group(ps[:, 0:512],
                           [(krows(s0 + 128 * jt, s0 + 128 * (jt + 1)),
                             qrows(s0, s0 + 512)),
                            (B.onesb[0:1, 0:128], rmrow[0:1, 0:512])],
                           tag="s2")
                B.tt(ps[:, 0:512], ps[:, 0:512],
                     masks[:, 2048 + 512 * jt:2048 + 512 * jt + 512],
                     ALU.add, tag="s2m")
                B.act(ET[:, 512 * jt:512 * jt + 512], ps[:, 0:512], AF.Exp,
                      tag="s2e")
            dps = pool.f32()
            B.mm_group(dps[0:1, 0:512],
                       [(B.onesb[0:16, 0:1], ETpm)] +
                       [(B.onesb[:, 0:1], ET[:, 512 * k:512 * k + 512])
                        for k in range(4)], tag="dn")
            B.copy(dn, dps[0:1, 0:512], tag="dn")
            B.recip(dn, dn)
            for j in range(4):
                t = 4 * seg + j
                ops_ = pool.f32()
                B.mm_group(ops_[:, 0:64],
                           [(ETpm[:, 128 * j:128 * j + 128],
                             pmv_sb[:, 64 * h:64 * h + 64])] +
                           [(ET[:, 512 * k + 128 * j:512 * k + 128 * j + 128],
                             v.tile(4 * seg + k, 64 * h, 64 * h + 64))
                            for k in range(4)], tag="pv")
                dcol = pool.f32()
                B.transpose(dcol[:, 0:1], dn[0:1, 128 * j:128 * j + 128],
                            B.cs(1.0)[0:1, 0:1], tag="dtr")
                B.ts(aotk, ops_[:, 0:64], dcol[:, 0:1], None, ALU.mult,
                     tag="aoev")
                bank = pool.bf16()
                B.transpose(bank[base:base + 64, 0:128], aotk, B.ident,
                            tag="aotr")
                B.copy(outT.sb[base:base + 64,
                               hc * T_SEQ + 128 * t:hc * T_SEQ + 128 * t + 128],
                       bank[base:base + 64, 0:128], tag="aotr")
    wostage = wvstage
    Wo = FMT(B, "Wo", 1024, 1024, BF16, ap=A.bf16(16384, 8192))
    rows = A.f32(24576, 1024)[0:1, :].__class__ if False else None
    rows = A.t[0:1, 24576:25600]
    B.dma(rows, rowsd, ch="rows_l1a", batch=True)
    for k in range(8):
        st = wostage[:, 0:1024]
        B.dma(st, woutd[128 * k:128 * (k + 1), :], ch=f"wo{k % 2}")
        B.copy(Wo.chunk(k), st, tag="wconv")
    res = A.f32(32768, 2048)
    for t in range(NT):
        xt = xs2[:, (t % 2) * 1024:(t % 2) * 1024 + 1024]
        rt = res[:, (t % 2) * 1024:(t % 2) * 1024 + 1024]
        B.dma(xt, xind[t], ch=f"x{t % 2}")
        for n0 in (0, 512):
            ps = mmA(B, pool, outT, Wo, t, n0, 512,
                     bias_row=rows[0:1, 0:1024], tag="womm")
            B.tt(rt[:, n0:n0 + 512], ps, xt[:, n0:n0 + 512], ALU.add,
                 tag="resid")
        B.dma(xoutd[t], rt, ch=f"xo{t % 2}", on="scalar")
    B.finalize()
    return B



def _local_masks(win, span):
    ntw, nts = win // 128, span // 128
    p = np.arange(128)
    out = []
    for jq in range(ntw):
        gi = (span - win) + 128 * jq + p[:, None]
        gj = np.arange(span)[None, :]
        out.append(np.where((gj <= gi) & (gi <= gj + win), 0.0, -1e30))
    for jk in range(nts):
        gj = 128 * jk + p[:, None]
        gi = (span - win) + np.arange(win)[None, :]
        out.append(np.where((gj <= gi) & (gi <= gj + win), 0.0, -1e30))
    return np.concatenate(out, 1).astype(BF16NP)


def _xpos_tabs(win):
    d = 64
    half = np.arange(0, d, 2).astype(np.float64)
    invf = 1.0 / 10000.0 ** (half / d)
    t = np.arange(T_SEQ, dtype=np.float64)
    fr = t[:, None] * invf[None, :]
    fr2 = np.concatenate([fr, fr], 1)
    base = (half + 0.4 * d) / (1.4 * d)
    power = (t - T_SEQ // 2) / (win // 2)
    sc = base[None, :] ** power[:, None]
    sc2 = np.concatenate([sc, sc], 1)
    sign = np.ones(64)
    sign[:32] = -1.0
    cq = np.cos(fr2) * sc2 * 8.0
    sq = np.sin(fr2) * sc2 * 8.0 * sign[None, :]
    ck = np.cos(fr2) / sc2
    sk = np.sin(fr2) / sc2 * sign[None, :]
    return np.concatenate([cq, sq, ck, sk], 1).astype(np.float32)


def _relay_w1(w1, b1):
    hid = w1.shape[1] // 2
    nkc = (hid + 127) // 128
    wout = np.zeros((w1.shape[0], 256 * nkc), np.float32)
    b1a = np.zeros((128, nkc), np.float32)
    b1g = np.zeros((128, nkc), np.float32)
    for j in range(nkc):
        r = min(128, hid - 128 * j)
        wout[:, 256 * j:256 * j + r] = w1[:, 128 * j:128 * j + r]
        wout[:, 256 * j + 128:256 * j + 128 + r] = \
            w1[:, hid + 128 * j:hid + 128 * j + r]
        b1a[:r, j] = b1[128 * j:128 * j + r]
        b1g[:r, j] = b1[hid + 128 * j:hid + 128 * j + r]
    return wout, b1a, b1g


def prep_local_stack(pref, inp, nlayers, win):
    f32 = np.float32
    names = {"L1": dict(wqkv="L1_ffn_wqkv", qs="L1_ffn_q_scale",
                        ks="L1_ffn_k_scale", wo="L1_ffn_wo", lg="L1_ffn_ln_g",
                        lb="L1_ffn_ln_b", w1="L1_ffn_w1", b1="L1_ffn_b1",
                        w2="L1_ffn_w2", b2="L1_ffn_b2"),
             "L3": dict(wqkv="L3_ffn_Wqkv", qs="L3_ffn_q_scale",
                        ks="L3_ffn_k_scale", wo="L3_ffn_Wo", lg="L3_ffn_ln_g",
                        lb="L3_ffn_ln_b", w1="L3_ffn_W1", b1="L3_ffn_b1",
                        w2="L3_ffn_W2", b2="L3_ffn_b2")}[pref]
    out = {"tabs": _xpos_tabs(win), "masksd": _local_masks(win, 2 * win)}
    hid = np.asarray(inp[names["w1"]], f32).shape[2] // 2
    nkc = (hid + 127) // 128
    w1r = np.zeros((nlayers, 1024, 256 * nkc), f32)
    b1a = np.zeros((nlayers, 128, nkc), f32)
    b1g = np.zeros((nlayers, 128, nkc), f32)
    for l in range(nlayers):
        w1r[l], b1a[l], b1g[l] = _relay_w1(
            np.asarray(inp[names["w1"]][l], f32),
            np.asarray(inp[names["b1"]][l], f32))
    qs = np.asarray(inp[names["qs"]], f32)
    ks = np.asarray(inp[names["ks"]], f32)
    qkrows = np.zeros((1, nlayers * 128), f32)
    for l in range(nlayers):
        qkrows[0, 128 * l:128 * l + 64] = qs[l]
        qkrows[0, 128 * l + 64:128 * l + 128] = ks[l]
    lglb = np.zeros((1, nlayers * 2048), f32)
    b2r = np.zeros((1, nlayers * 1024), f32)
    for l in range(nlayers):
        lglb[0, 2048 * l:2048 * l + 1024] = np.asarray(inp[names["lg"]][l])
        lglb[0, 2048 * l + 1024:2048 * l + 2048] = \
            np.asarray(inp[names["lb"]][l])
        b2r[0, 1024 * l:1024 * (l + 1)] = np.asarray(inp[names["b2"]][l])
    out.update({
        "wqkv": np.asarray(inp[names["wqkv"]], f32),
        "wo": np.asarray(inp[names["wo"]], f32),
        "w1r": w1r, "b1a": b1a, "b1g": b1g,
        "w2": np.asarray(inp[names["w2"]], f32),
        "qkrows": qkrows, "lglb": lglb, "b2r": b2r,
    })
    return out


class Cursor:
    def __init__(self, arena, start=0):
        self.A = arena
        self.c = start

    def f32(self, n):
        ap = self.A.f32(self.c, n)
        self.c += n
        return ap

    def bf16(self, n):
        f = (n + 1) // 2
        ap = self.A.bf16(self.c, n)
        self.c += f
        return ap


def build_local_stack(nc, nlayers, heads, win):
    H = heads
    DQ = 64 * H
    NDQ = DQ // 128
    span = 2 * win
    ntw, nts = win // 128, span // 128
    nwin = T_SEQ // win
    HID = 2730
    NKC = 22
    xin = nc.dram_tensor("xin", [T_SEQ, D_MODEL], F32,
                         kind="ExternalInput").ap()
    xout = nc.dram_tensor("xout", [T_SEQ, D_MODEL], F32,
                          kind="ExternalOutput").ap()
    wqkvd = nc.dram_tensor("wqkv", [nlayers, 1024, 3 * DQ], F32,
                           kind="ExternalInput").ap()
    wod = nc.dram_tensor("wo", [nlayers, DQ, 1024], F32,
                         kind="ExternalInput").ap()
    w1d = nc.dram_tensor("w1r", [nlayers, 1024, 256 * NKC], F32,
                         kind="ExternalInput").ap()
    b1ad = nc.dram_tensor("b1a", [nlayers, 128, NKC], F32,
                          kind="ExternalInput").ap()
    b1gd = nc.dram_tensor("b1g", [nlayers, 128, NKC], F32,
                          kind="ExternalInput").ap()
    w2d = nc.dram_tensor("w2", [nlayers, HID, 1024], F32,
                         kind="ExternalInput").ap()
    qkrd = nc.dram_tensor("qkrows", [1, nlayers * 128], F32,
                          kind="ExternalInput").ap()
    lglbd = nc.dram_tensor("lglb", [1, nlayers * 2048], F32,
                           kind="ExternalInput").ap()
    b2rd = nc.dram_tensor("b2r", [1, nlayers * 1024], F32,
                          kind="ExternalInput").ap()
    tabsd = nc.dram_tensor("tabs", [T_SEQ, 256], F32,
                           kind="ExternalInput").ap()
    maskd = nc.dram_tensor("masksd", [128, ntw * span + nts * win], BF16,
                           kind="ExternalInput").ap()

    B = Builder(nc)
    pool = PsumPool(B, n_f32=4)
    xind = xin.rearrange("(t p) c -> t p c", p=128)
    xoutd = xout.rearrange("(t p) c -> t p c", p=128)

    xi = TMT(B, "xi", T_SEQ, D_MODEL, F32)
    attn_f = (24 * DQ + NDQ * 1024 + 2 * DQ + 2 * NDQ * win
              + NDQ * nts * 128 + nts * DQ + nts * win + 2) // 2
    ffn_f = ((16384 + NKC * 512 + 2 * NKC * 128 + 2 * 8 * 256
              + 1024 + 512 + 512 + 2) // 2
             + 2 * 1024 + 11 * NT + 64)
    A = Arena(B, "A", max(attn_f, ffn_f) + 16)
    ca = Cursor(A)
    wqkv = FMT(B, None, 1024, 3 * DQ, BF16, ap=ca.bf16(8 * 3 * DQ))
    wosb = FMT(B, None, DQ, 1024, BF16, ap=ca.bf16(NDQ * 1024))
    qkl = ca.bf16(2 * DQ)
    qTf = FMT(B, None, DQ, win, BF16, ap=ca.bf16(NDQ * win))
    kTf = FMT(B, None, DQ, nts * 128, BF16, ap=ca.bf16(NDQ * nts * 128))
    vbr = [ca.bf16(DQ) for _ in range(nts)]
    outTw = FMT(B, None, DQ, win, BF16, ap=ca.bf16(NDQ * win))
    ET = ca.bf16(nts * win)
    attn_top = ca.c
    cf = Cursor(A)
    h2nT = FMT(B, None, D_MODEL, T_SEQ, BF16, ap=cf.bf16(8 * T_SEQ))
    uT = FMT(B, None, HID, 512, BF16, ap=cf.bf16(NKC * 512))
    w2b = [FMT(B, None, HID, 128, BF16, ap=cf.bf16(NKC * 128))
           for _ in range(2)]
    lgb = cf.f32(1024)
    lbb = cf.f32(1024)
    yTs = cf.bf16(512)
    h2n = cf.bf16(1024)
    w1b = [FMT(B, None, 1024, 256, BF16, ap=cf.bf16(8 * 256))
           for _ in range(2)]
    glscr = cf.bf16(512)
    fcols = cf.f32(11 * NT)
    b1ac = cf.f32(NKC)
    b1gc = cf.f32(NKC)
    b2c = cf.f32(8)
    edgec = cf.f32(5)
    assert max(attn_top, cf.c) <= A.cols, (attn_top, cf.c, A.cols)

    tabs = TMT(B, "tabs_sb", T_SEQ, 256, F32)
    B.dma(tabs.sb.rearrange("p (t c) -> p t c", t=NT),
          tabsd.rearrange("(t p) c -> p t c", p=128), ch="tabs", batch=True)
    masks = B.sb("masks_sb", [128, ntw * span + nts * win], BF16)
    B.dma(masks, maskd, ch="masks", batch=True)

    def mask1(jq):
        return masks[:, jq * span:(jq + 1) * span]

    def mask2(jk):
        return masks[:, ntw * span + jk * win:ntw * span + (jk + 1) * win]

    qkrows = load_rows(B, "qkr", qkrd, nlayers * 128)
    cols = B.sb("cols", [128, 4 * NT], F32)
    stage = B.sb("stage", [128, 3072], F32)
    qn = stage[:, 0:2 * DQ]
    sq = stage[:, 2048:3072]
    rawT = B.sb("rawT", [128, 8 * 128], BF16)
    bnd = B.sb("bnd", [128, 4], BF16)
    htu = B.sb("htu", [128, 4 * 128], BF16)
    xb = B.sb("xbscr", [128, 1024], BF16)
    qp = xb
    rstdh = B.sb("rstdh", [128, H], F32)
    qsb = B.sb("qsb", [128, 128], F32)
    rm = B.sb("rm", [128, 2], F32)
    rmb = B.sb("rmb", [128, 1], BF16)
    rmrow = B.sb("rmrow", [1, win], BF16)
    dn = B.sb("dnrow", [1, win], F32)
    aotk = B.sb("aotk", [128, 64], BF16)

    for t in range(NT):
        xt = stage[:, (t % 2) * 1024:(t % 2) * 1024 + 1024]
        B.dma(xt, xind[t], ch=f"x{t % 2}")
        pre_ln_tile(B, xt, xi.tile(t), cols[:, 4 * t:4 * t + 4], sq)

    def hview(ap_):
        return ap_.rearrange("p (h d) -> p h d", h=H)

    def l2norm_xpos(t, src, dst, tc, tsn, qk):
        B.act(qn[:, 0:DQ], src, AF.Square, tag="l2")
        B.red(rstdh, hview(qn[:, 0:DQ]), ALU.add, tag="l2")
        B.act(rstdh, rstdh, AF.Sqrt, tag="l2")
        B.ts(rstdh, rstdh, 1e-12, None, ALU.max, tag="l2")
        B.recip(rstdh, rstdh)
        rb = rstdh[:, :].unsqueeze(2).broadcast_to([128, H, 64])
        B.tt(hview(qn[:, 0:DQ]), hview(src), rb, ALU.mult, tag="l2")
        qsv = qsb[:, qk * 64:qk * 64 + 64].unsqueeze(1).broadcast_to(
            [128, H, 64])
        B.tt(hview(qn[:, DQ:2 * DQ]), hview(qn[:, 0:DQ]), qsv, ALU.mult,
             tag="l2")
        cosv = tabs.tile(t, tc, tc + 64).unsqueeze(1).broadcast_to(
            [128, H, 64])
        sinv = tabs.tile(t, tsn, tsn + 64).unsqueeze(1).broadcast_to(
            [128, H, 64])
        qn2 = qn[:, DQ:2 * DQ]
        swv = qn2.rearrange("p (h two j) -> p h two j", h=H,
                            two=2)[:, :, ::-1, :]
        B.tt(hview(qn[:, 0:DQ]), hview(qn2), cosv, ALU.mult, tag="xpos")
        B.tt(hview(dst), swv,
             sinv.rearrange("p h (two j) -> p h two j", two=2), ALU.mult,
             tag="xpos")
        B.tt(dst, qn[:, 0:DQ], dst, ALU.add, tag="xpos")

    def attn_layer(l):
        for k in range(8):
            st = stage[:, 0:3 * DQ]
            B.dma(st, wqkvd[l, 128 * k:128 * (k + 1), :], ch="wq")
            B.copy(wqkv.chunk(k), st, tag="wconv")
        for k in range(NDQ):
            st = stage[:, 0:1024]
            B.dma(st, wod[l, 128 * k:128 * (k + 1), :], ch="wod")
            B.copy(wosb.chunk(k), st, tag="wconv")
        pqs = pool.f32()
        B.mm_group(pqs[:, 0:128], [(B.onesf[0:1, 0:128],
                                    qkrows[0:1, 128 * l:128 * (l + 1)])],
                   tag="qsb")
        B.copy(qsb, pqs[:, 0:128], tag="qsb")
        for w in range(nwin):
            for jt in range(ntw):
                t = w * ntw + jt
                B.copy(xb, xi.tile(t), tag="xb")
                bank = pool.bf16()
                for k in range(8):
                    B.transpose(bank[:, 128 * k:128 * (k + 1)],
                                xb[:, 128 * k:128 * (k + 1)], B.ident,
                                tag="xtr")
                if t > 0:
                    for k in range(4):
                        B.copy(htu[:, 128 * k:128 * k + 1],
                               bnd[:, k:k + 1], tag="sh")
                else:
                    B.memset(htu[:, 0:1], 0.0, tag="sh")
                    B.memset(htu[:, 128:129], 0.0, tag="sh")
                    B.memset(htu[:, 256:257], 0.0, tag="sh")
                    B.memset(htu[:, 384:385], 0.0, tag="sh")
                B.copy(rawT, bank[:, 0:1024], tag="xtrev")
                for k in range(4):
                    B.copy(htu[:, 128 * k + 1:128 * k + 128],
                           rawT[:, 128 * (4 + k):128 * (4 + k) + 127],
                           tag="sh")
                    B.copy(bnd[:, k:k + 1],
                           rawT[:, 128 * (4 + k) + 127:128 * (4 + k) + 128],
                           tag="sh")
                for n0 in range(0, 3 * DQ, 512):
                    nn = min(512, 3 * DQ - n0)
                    bankp = pool.f32()
                    pairs = []
                    for k in range(8):
                        lhs = (rawT[:, 128 * k:128 * (k + 1)] if k < 4
                               else htu[:, 128 * (k - 4):128 * (k - 3)])
                        pairs.append((lhs, wqkv.chunk(k, n0, n0 + nn)))
                    B.mm_group(bankp[:, 0:nn], pairs, tag="qkvmm")
                    lo, hi = n0, n0 + nn
                    if lo < 2 * DQ:
                        c = min(hi, 2 * DQ) - lo
                        B.copy(qkl[:, lo:lo + c], bankp[:, 0:c], tag="qkvev")
                    if hi > 2 * DQ:
                        c0 = max(lo, 2 * DQ)
                        B.copy(vbr[t % nts][:, c0 - 2 * DQ:hi - 2 * DQ],
                               bankp[:, c0 - lo:nn], tag="qkvev")
                l2norm_xpos(t, qkl[:, 0:DQ], qp[:, 0:DQ], 0, 64, 0)
                for g0 in range(NDQ):
                    bank = pool.bf16()
                    B.transpose(bank[:, 0:128],
                                qp[:, 128 * g0:128 * (g0 + 1)], B.ident,
                                tag="qtr")
                    B.copy(qTf.chunk(g0, jt * 128, (jt + 1) * 128),
                           bank[:, 0:128], tag="qtrev")
                l2norm_xpos(t, qkl[:, DQ:2 * DQ], qp[:, 0:DQ], 128, 192, 1)
                for g0 in range(NDQ):
                    bank = pool.bf16()
                    B.transpose(bank[:, 0:128],
                                qp[:, 128 * g0:128 * (g0 + 1)], B.ident,
                                tag="ktr")
                    B.copy(kTf.chunk(g0, (t % nts) * 128,
                                     (t % nts) * 128 + 128),
                           bank[:, 0:128], tag="ktrev")
            stiles = list(range(max(0, (w - 1) * ntw), (w + 1) * ntw))
            jk0 = nts - len(stiles)
            for h in range(H):
                hc, base = h // 2, 64 * (h % 2)

                def qr(c0, c1):
                    return qTf.sb[base:base + 64, hc * win + c0:hc * win + c1]

                def kr(ti):
                    sl = (ti % nts) * 128
                    return kTf.sb[base:base + 64,
                                  hc * (nts * 128) + sl:
                                  hc * (nts * 128) + sl + 128]
                for jq in range(ntw):
                    first = True
                    for r0 in range(0, len(stiles), 4):
                        grp = stiles[r0:r0 + 4]
                        ps = pool.f32()
                        for gi, ti in enumerate(grp):
                            B.mm_group(ps[:, gi * 128:(gi + 1) * 128],
                                       [(qr(jq * 128, (jq + 1) * 128),
                                         kr(ti))], tag="s1")
                        mv = mask1(jq)[:, (jk0 + r0) * 128:
                                       (jk0 + r0 + len(grp)) * 128]
                        B.tt(ps[:, 0:len(grp) * 128],
                             ps[:, 0:len(grp) * 128], mv, ALU.add, tag="s1m")
                        if first:
                            B.red(rm[:, 0:1], ps[:, 0:len(grp) * 128],
                                  ALU.max, tag="s1r")
                            first = False
                        else:
                            B.red(rm[:, 1:2], ps[:, 0:len(grp) * 128],
                                  ALU.max, tag="s1r")
                            B.tt(rm[:, 0:1], rm[:, 0:1], rm[:, 1:2],
                                 ALU.max, tag="s1r")
                    B.ts(rmb, rm[:, 0:1], -1.0, None, ALU.mult, tag="rmneg")
                    bank = pool.bf16()
                    B.transpose(bank[0:1, 0:128], rmb, B.ident, tag="rmtr")
                    B.copy(rmrow[0:1, 128 * jq:128 * jq + 128],
                           bank[0:1, 0:128], tag="rmtr")
                for r, ti in enumerate(stiles):
                    ps = pool.f32()
                    B.mm_group(ps[:, 0:win],
                               [(kr(ti), qr(0, win)),
                                (B.onesb[0:1, 0:128], rmrow[0:1, 0:win])],
                               tag="s2")
                    B.tt(ps[:, 0:win], ps[:, 0:win], mask2(jk0 + r),
                         ALU.add, tag="s2m")
                    B.act(ET[:, r * win:(r + 1) * win], ps[:, 0:win],
                          AF.Exp, tag="s2e")
                dps = pool.f32()
                B.mm_group(dps[0:1, 0:win],
                           [(B.onesb[:, 0:1], ET[:, r * win:(r + 1) * win])
                            for r in range(len(stiles))], tag="dn")
                B.copy(dn, dps[0:1, 0:win], tag="dn")
                B.recip(dn, dn)
                for jq in range(ntw):
                    ops_ = pool.f32()
                    B.mm_group(ops_[:, 0:64],
                               [(ET[:, r * win + 128 * jq:
                                    r * win + 128 * jq + 128],
                                 vbr[ti % nts][:, 64 * h:64 * h + 64])
                                for r, ti in enumerate(stiles)], tag="pv")
                    dcol = pool.f32()
                    B.transpose(dcol[:, 0:1],
                                dn[0:1, 128 * jq:128 * jq + 128],
                                B.cs(1.0)[0:1, 0:1], tag="dtr")
                    B.ts(aotk, ops_[:, 0:64], dcol[:, 0:1], None, ALU.mult,
                         tag="aoev")
                    bank = pool.bf16()
                    B.transpose(bank[base:base + 64, 0:128], aotk, B.ident,
                                tag="aotr")
                    B.copy(outTw.sb[base:base + 64,
                                    hc * win + 128 * jq:
                                    hc * win + 128 * jq + 128],
                           bank[base:base + 64, 0:128], tag="aotr")
            for jt in range(ntw):
                t = w * ntw + jt
                for n0 in (0, 512):
                    ps = pool.f32()
                    B.mm_group(ps[:, 0:512],
                               [(outTw.chunk(k, jt * 128, (jt + 1) * 128),
                                 wosb.chunk(k, n0, n0 + 512))
                                for k in range(NDQ)], tag="womm")
                    B.tt(xi.tile(t, n0, n0 + 512), ps[:, 0:512],
                         xi.tile(t, n0, n0 + 512), ALU.add, tag="wores")

    def fc(stat, sl=None):
        if sl is None:
            return fcols[:, stat * NT:(stat + 1) * NT]
        return fcols[:, stat * NT + sl:stat * NT + sl + 1]

    def ffn_layer(l):
        B.dma(stage[0:1, 0:2048], lglbd[0:1, 2048 * l:2048 * (l + 1)],
              ch="lglb")
        for dst, off in ((lgb, 0), (lbb, 1024)):
            for n0 in (0, 512):
                pb = pool.f32()
                B.mm_group(pb[:, 0:512],
                           [(B.onesf[0:1, 0:128],
                             stage[0:1, off + n0:off + n0 + 512])],
                           tag="lgbc")
                B.copy(dst[:, n0:n0 + 512], pb[:, 0:512], tag="lgbc")
        B.dma(stage[0:1, 2048:3072], b2rd[0:1, 1024 * l:1024 * (l + 1)],
              ch="b2r")
        for k in range(8):
            pb = pool.f32()
            B.transpose(pb[:, 0:1],
                        stage[0:1, 2048 + 128 * k:2048 + 128 * (k + 1)],
                        B.cs(1.0)[0:1, 0:1], tag="b2c")
            B.copy(b2c[:, k:k + 1], pb[:, 0:1], tag="b2c")
        B.dma(b1ac, b1ad[l], ch="b1a")
        B.dma(b1gc, b1gd[l], ch="b1g")
        for t in range(NT):
            B.red(fc(0, t), xi.tile(t, 0, 512), ALU.add, tag="fst")
            B.act(sq[:, 0:512], xi.tile(t, 0, 512), AF.Square,
                  accum_out=fc(1, t), tag="fst")
            B.red(fc(2, t), xi.tile(t, 512, 1024), ALU.add, tag="fst")
            B.act(sq[:, 0:512], xi.tile(t, 512, 1024), AF.Square,
                  accum_out=fc(3, t), tag="fst")
        for cc, dc in ((2, 4), (3, 5)):
            B.dma(fc(dc)[0:1, 0:1], B.cs(0.0)[0:1, 0:1], ch="fsh",
                  batch=True)
            B.dma(fc(dc)[1:128, :], fc(cc)[0:127, :], ch="fsh", batch=True)
            B.dma(fc(dc)[0:1, 1:NT], fc(cc)[127:128, 0:NT - 1], ch="fsh", batch=True)
        B.tt(fc(6), fc(0), fc(4), ALU.add, tag="fst")
        B.ts(fc(6), fc(6), -1.0 / 1024, None, ALU.mult, tag="fst")
        B.tt(fc(7), fc(1), fc(5), ALU.add, tag="fst")
        B.ts(fc(7), fc(7), 1.0 / 1024, None, ALU.mult, tag="fst")
        B.tt(fc(10), fc(6), fc(6), ALU.mult, tag="fst")
        B.tt(fc(7), fc(7), fc(10), ALU.subtract, tag="fst")
        B.act(fc(7), fc(7), AF.Sqrt, bias=B.cs(1e-5), tag="fst")
        B.recip(fc(7), fc(7))
        for cc, dc in ((6, 8), (7, 9)):
            B.dma(fc(dc)[0:127, :], fc(cc)[1:128, :], ch="fsh", batch=True)
            B.dma(fc(dc)[127:128, 0:NT - 1], fc(cc)[0:1, 1:NT], ch="fsh",
                  batch=True)
            B.dma(fc(dc)[127:128, NT - 1:NT], B.cs(0.0)[0:1, 0:1], ch="fsh",
                  batch=True)
        pb1 = pool.f32()
        B.mm_group(pb1[:, 0:1], [(B.onesf[0:1, 0:128],
                                  fcols[0:1, 6 * NT:6 * NT + 1])], tag="edge")
        B.copy(edgec[:, 0:1], pb1[:, 0:1], tag="edge")
        pb2 = pool.f32()
        B.mm_group(pb2[:, 0:1], [(B.onesf[0:1, 0:128],
                                  fcols[0:1, 7 * NT:7 * NT + 1])], tag="edge")
        B.copy(edgec[:, 1:2], pb2[:, 0:1], tag="edge")
        B.tt(edgec[:, 2:3], edgec[:, 0:1], edgec[:, 1:2], ALU.mult,
             tag="edge")
        for t in range(NT):
            B.ts(sq[:, 0:512], xi.tile(t, 0, 512), fc(6, t), fc(7, t),
                 ALU.add, ALU.mult, tag="f2")
            B.tt(sq[:, 0:512], sq[:, 0:512], lgb[:, 0:512], ALU.mult,
                 tag="f2")
            B.tt(h2n[:, 0:512], sq[:, 0:512], lbb[:, 0:512], ALU.add,
                 tag="f2")
            B.ts(sq[:, 512:1024], xi.tile(t, 512, 1024), fc(8, t), fc(9, t),
                 ALU.add, ALU.mult, tag="f2")
            B.tt(sq[:, 512:1024], sq[:, 512:1024], lgb[:, 512:1024],
                 ALU.mult, tag="f2")
            B.tt(h2n[:, 512:1024], sq[:, 512:1024], lbb[:, 512:1024],
                 ALU.add, tag="f2")
            bank = pool.bf16()
            for k in range(8):
                B.transpose(bank[:, 128 * k:128 * (k + 1)],
                            h2n[:, 128 * k:128 * (k + 1)], B.ident,
                            tag="f2tr")
            for k in range(4):
                B.copy(h2nT.chunk(k, t * 128, (t + 1) * 128),
                       bank[:, 128 * k:128 * (k + 1)], tag="f2trev")
            for k in range(4, 8):
                ncc = 127 if t == NT - 1 else 128
                B.copy(h2nT.chunk(k, t * 128 + 1, t * 128 + 1 + ncc),
                       bank[:, 128 * k:128 * k + ncc], tag="f2trev")
        for k in range(4, 8):
            lgcolp = pool.f32()
            B.transpose(lgcolp[:, 0:1], lgb[0:1, 128 * k:128 * (k + 1)],
                        B.cs(1.0)[0:1, 0:1], tag="edge")
            lbcolp = pool.f32()
            B.transpose(lbcolp[:, 0:1], lbb[0:1, 128 * k:128 * (k + 1)],
                        B.cs(1.0)[0:1, 0:1], tag="edge")
            B.copy(edgec[:, 3:4], lgcolp[:, 0:1], tag="edge")
            B.copy(edgec[:, 4:5], lbcolp[:, 0:1], tag="edge")
            B.stt(h2nT.chunk(k, 0, 1), edgec[:, 3:4], edgec[:, 2:3],
                  edgec[:, 4:5], ALU.mult, ALU.add, tag="edge")
        for b in range(4):
            t0 = 512 * b
            for j in range(NKC):
                rj = min(128, HID - 128 * j)
                sj = j % 2
                st = stage[:, 0:2048]
                B.dma(st.rearrange("p (k n) -> p k n", k=8),
                      w1d[l, :, 256 * j:256 * (j + 1)].rearrange(
                          "(k p) n -> p k n", p=128), ch="w1")
                B.copy(w1b[sj].sb, st, tag="wconv")
                psa = mmB(B, pool, w1b[sj], h2nT, 0, t0, 512, tag="w1a")
                psg = mmB(B, pool, w1b[sj], h2nT, 1, t0, 512, tag="w1g")
                B.act(glscr[0:rj, 0:512], psg[0:rj, 0:512], AF.Gelu,
                      bias=b1gc[0:rj, j:j + 1], tag="gelu")
                B.stt(uT.chunk(j), psa[0:rj, 0:512], b1ac[0:rj, j:j + 1],
                      glscr[0:rj, 0:512], ALU.add, ALU.mult, tag="umul")
            for m in range(8):
                sm = m % 2
                st = stage[:, 0:NKC * 128]
                B.dma(st.rearrange("p (k n) -> p k n", k=NKC)[:, 0:21, :],
                      w2d[l, 0:2688, 128 * m:128 * (m + 1)].rearrange(
                          "(k p) n -> p k n", p=128), ch="w2a")
                B.dma(st[0:42, 21 * 128:21 * 128 + 128],
                      w2d[l, 2688:2730, 128 * m:128 * (m + 1)], ch="w2b")
                B.copy(w2b[sm].sb, st, tag="wconv")
                psy = pool.f32()
                pairs = [(w2b[sm].chunk(k), uT.chunk(k)) for k in range(NKC)]
                B.mm_group(psy[:, 0:512], pairs, tag="w2mm")
                B.ts(yTs[:, 0:512], psy[:, 0:512], b2c[:, m:m + 1], None,
                     ALU.add, tag="w2ev")
                for jt in range(4):
                    t = 4 * b + jt
                    bank = pool.bf16()
                    B.transpose(bank[:, 0:128],
                                yTs[:, 128 * jt:128 * (jt + 1)], B.ident,
                                tag="w2tr")
                    B.tt(xi.tile(t, 128 * m, 128 * (m + 1)),
                         bank[:, 0:128],
                         xi.tile(t, 128 * m, 128 * (m + 1)), ALU.add,
                         tag="w2res")

    for l in range(nlayers):
        attn_layer(l)
        ffn_layer(l)

    for t in range(NT):
        xt = stage[:, (t % 2) * 1024:(t % 2) * 1024 + 1024]
        rt = stage[:, 2048:3072]
        B.dma(xt, xind[t], ch=f"x{t % 2}")
        B.tt(rt, xt, xi.tile(t), ALU.add, tag="fres")
        B.dma(xoutd[t], rt, ch="xo", on="scalar")
    B.finalize()
    return B



def prep_l3_attn(inp):
    f32 = np.float32
    tri = np.triu(np.ones((128, 128), np.float32))
    return {
        "wqT": np.ascontiguousarray(np.asarray(inp["L3_attn_Wq"], f32).T),
        "wkT": np.ascontiguousarray(np.asarray(inp["L3_attn_Wk"], f32).T),
        "wvT": np.ascontiguousarray(np.asarray(inp["L3_attn_Wv"], f32).T),
        "woT": np.ascontiguousarray(np.asarray(inp["L3_attn_Wo"], f32).T),
        "rows": np.asarray(inp["L3_attn_bo"], f32).reshape(1, 1024),
        "trimask": tri.astype(BF16NP),
    }


def build_l3_attn(nc):
    xin = nc.dram_tensor("xin", [T_SEQ, D_MODEL], F32,
                         kind="ExternalInput").ap()
    xout = nc.dram_tensor("xout", [T_SEQ, D_MODEL], F32,
                          kind="ExternalOutput").ap()
    wqd = nc.dram_tensor("wqT", [1024, 1024], F32, kind="ExternalInput").ap()
    wkd = nc.dram_tensor("wkT", [1024, 1024], F32, kind="ExternalInput").ap()
    wvd = nc.dram_tensor("wvT", [1024, 1024], F32, kind="ExternalInput").ap()
    wod = nc.dram_tensor("woT", [1024, 1024], F32, kind="ExternalInput").ap()
    rowsd = nc.dram_tensor("rows", [1, 1024], F32, kind="ExternalInput").ap()
    trid = nc.dram_tensor("trimask", [128, 128], BF16,
                          kind="ExternalInput").ap()

    B = Builder(nc)
    pool = PsumPool(B, n_f32=4)
    A = Arena(B, "A", 40960)
    xind = xin.rearrange("(t p) c -> t p c", p=128)
    xoutd = xout.rearrange("(t p) c -> t p c", p=128)
    stage = B.sb("stage", [128, 1024], F32)
    tri = B.sb("tri", [128, 128], BF16)
    B.dma(tri, trid, ch="tri", batch=True)
    lncols = B.sb("lncols", [128, 4 * NT], F32)
    escr = B.sb("escr", [128, 2 * 512], F32)
    S_sb = B.sb("S_sb", [128, 8 * 65], F32)
    S_bf = B.sb("S_bf", [128, 8 * 65], BF16)
    At = B.sb("At", [128, 128], BF16)
    dcol = B.sb("dcolx", [128, 2], F32)

    xn = TMT(B, "xn", T_SEQ, D_MODEL, BF16, ap=A.bf16(0, 16384))
    for t in range(NT):
        xt = stage[:, 0:1024]
        B.dma(xt, xind[t], ch="x0")
        pre_ln_tile(B, xt, xn.tile(t), lncols[:, 4 * t:4 * t + 4],
                    escr[:, 0:1024])
    xnT = FMT(B, "xnT", D_MODEL, T_SEQ, BF16, ap=A.bf16(8192, 16384))
    tm_to_fm(B, xn, xnT, pool)

    phiqT = FMT(B, None, D_MODEL, T_SEQ, BF16, ap=A.bf16(16384, 16384))
    phikT = FMT(B, None, D_MODEL, T_SEQ, BF16, ap=A.bf16(24576, 16384))
    vT_tmp = None
    phik_roll = B.sb("phik_roll", [128, 1024], BF16)
    v = TMT(B, None, T_SEQ, D_MODEL, BF16, ap=A.bf16(32768, 16384))
    wcb = [B.sb(f"wcb{i}", [128, 8 * 128], BF16) for i in range(2)]

    def proj_convB(wd_, dst_fm, elu):
        for m in range(8):
            sm = m % 2
            st = stage[:, 0:1024]
            B.dma(st.rearrange("p (k n) -> p k n", k=8),
                  wd_[:, 128 * m:128 * (m + 1)].rearrange(
                      "(k p) n -> p k n", p=128), ch=f"w{sm}")
            B.copy(wcb[sm], st, tag="wconv")
            for t0 in range(0, T_SEQ, 512):
                ps = pool.f32()
                B.mm_group(ps[:, 0:512],
                           [(wcb[sm][:, 128 * k:128 * (k + 1)],
                             xnT.chunk(k, t0, t0 + 512)) for k in range(8)],
                           tag="proj")
                if elu:
                    B.ts(escr[:, 0:512], ps[:, 0:512], 0.0, None, ALU.min,
                         tag="elu")
                    B.act(escr[:, 512:1024], escr[:, 0:512], AF.Exp,
                          tag="elu")
                    B.stt(dst_fm.chunk(m, t0, t0 + 512), ps[:, 0:512], 0.0,
                          escr[:, 512:1024], ALU.max, ALU.add, tag="elu")
                else:
                    B.copy(dst_fm.chunk(m, t0, t0 + 512), ps[:, 0:512],
                           tag="projev")
    proj_convB(wqd, phiqT, True)
    proj_convB(wkd, phikT, True)
    proj_convB(wvd, FMT(B, None, D_MODEL, T_SEQ, BF16,
                        ap=A.bf16(0, 16384)), False)
    vT = FMT(B, None, D_MODEL, T_SEQ, BF16, ap=A.bf16(0, 16384))
    for t in range(NT):
        bank = pool.bf16()
        for k in range(8):
            B.transpose(bank[:, 128 * k:128 * (k + 1)],
                        vT.chunk(k, t * 128, (t + 1) * 128), B.ident,
                        tag="vtr")
        B.copy(v.tile(t), bank[:, 0:1024], tag="vtrev")

    B.memset(S_sb, 0.0, tag="sinit")
    B.memset(S_bf, 0.0, tag="sinit")
    ao = TMT(B, None, T_SEQ, D_MODEL, BF16, ap=A.bf16(0, 16384))
    EPS = 1e-5
    for c in range(NT):
        bank = pool.bf16()
        for k in range(8):
            B.transpose(bank[:, 128 * k:128 * (k + 1)],
                        phikT.chunk(k, c * 128, (c + 1) * 128), B.ident,
                        tag="pktr")
        B.copy(phik_roll, bank[:, 0:1024], tag="pktrev")
        for h in range(16):
            hc, base = h // 2, 64 * (h % 2)
            qrows = phiqT.sb[base:base + 64,
                             hc * T_SEQ + c * 128:hc * T_SEQ + (c + 1) * 128]
            krows = phikT.sb[base:base + 64,
                             hc * T_SEQ + c * 128:hc * T_SEQ + (c + 1) * 128]
            Ssl = S_sb[base:base + 64, 65 * hc:65 * hc + 65]
            Sbl = S_bf[base:base + 64, 65 * hc:65 * hc + 65]
            pa = pool.f32()
            B.mm_group(pa[:, 0:128], [(krows, qrows)], tag="at")
            B.tt(At, pa[:, 0:128], tri, ALU.mult, tag="atm")
            pn = pool.f32()
            B.mm_group(pn[:, 0:64],
                       [(At, v.tile(c, 64 * h, 64 * h + 64)),
                        (qrows, Sbl[:, 0:64])], tag="num")
            B.mm_group(pn[:, 64:65],
                       [(At, B.onesb[:, 0:1]),
                        (qrows, Sbl[:, 64:65])], tag="den")
            B.ts(dcol[:, 0:1], pn[:, 64:65], EPS, None, ALU.add, tag="div")
            B.recip(dcol[:, 1:2], dcol[:, 0:1])
            B.ts(ao.tile(c, 64 * h, 64 * h + 64), pn[:, 0:64],
                 dcol[:, 1:2], None, ALU.mult, tag="aoev")
            pd = pool.f32()
            B.mm_group(pd[base:base + 64, 0:64],
                       [(phik_roll[:, 64 * h:64 * h + 64],
                         v.tile(c, 64 * h, 64 * h + 64))], tag="supd")
            B.mm_group(pd[base:base + 64, 64:65],
                       [(phik_roll[:, 64 * h:64 * h + 64],
                         B.onesb[:, 0:1])], tag="supd")
            B.tt(Ssl, Ssl, pd[base:base + 64, 0:65], ALU.add, tag="sadd")
            B.copy(Sbl, Ssl, tag="sbf")
    Wo = FMT(B, None, 1024, 1024, BF16, ap=A.bf16(16384, 8192))
    for k in range(8):
        st = stage[:, 0:1024]
        B.dma(st, wod[128 * k:128 * (k + 1), :], ch=f"w{k % 2}")
        B.copy(Wo.chunk(k), st, tag="wconv")
    rows = A.t[0:1, 16384 + 4096:16384 + 4096 + 1024]
    B.dma(rows, rowsd, ch="rows", batch=True)
    aoT = FMT(B, None, D_MODEL, T_SEQ, BF16, ap=A.bf16(8192, 16384))
    tm_to_fm(B, ao, aoT, pool)
    for t in range(NT):
        xt = stage[:, 0:1024]
        rt = escr[:, 0:1024]
        B.dma(xt, xind[t], ch="x0")
        for n0 in (0, 512):
            ps = mmA(B, pool, aoT, Wo, t, n0, 512,
                     bias_row=rows[0:1, 0:1024], tag="womm")
            B.tt(rt[:, n0:n0 + 512], ps, xt[:, n0:n0 + 512], ALU.add,
                 tag="resid")
        B.dma(xoutd[t], rt, ch=f"xo{t % 2}", on="scalar")
    B.finalize()
    return B



def prep_l2_attn(inp):
    f32 = np.float32
    wx = np.asarray(inp["L2_attn_Wx"], f32)
    wxc = np.concatenate([wx[h] for h in range(16)], axis=1)
    rows = np.zeros((1, 2048), f32)
    rows[0, 0:1024] = np.asarray(inp["L2_attn_bx"], f32).reshape(-1)
    rows[0, 1024:2048] = np.asarray(inp["L2_attn_bp"], f32)
    return {
        "wxc": wxc,
        "wp": np.asarray(inp["L2_attn_Wp"], f32),
        "rows": rows,
    }


def build_l2_attn(nc):
    xin = nc.dram_tensor("xin", [T_SEQ, D_MODEL], F32,
                         kind="ExternalInput").ap()
    xout = nc.dram_tensor("xout", [T_SEQ, D_MODEL], F32,
                          kind="ExternalOutput").ap()
    wxd = nc.dram_tensor("wxc", [1024, 1024], F32, kind="ExternalInput").ap()
    wpd = nc.dram_tensor("wp", [1024, 1024], F32, kind="ExternalInput").ap()
    rowsd = nc.dram_tensor("rows", [1, 2048], F32, kind="ExternalInput").ap()

    B = Builder(nc)
    pool = PsumPool(B, n_f32=4)
    A = Arena(B, "A", 32768)
    xind = xin.rearrange("(t p) c -> t p c", p=128)
    xoutd = xout.rearrange("(t p) c -> t p c", p=128)
    stage = B.sb("stage", [128, 1024], F32)
    lncols = B.sb("lncols", [128, 4 * NT], F32)
    scr = B.sb("scr", [128, 1024], F32)
    bxrow = B.sb("bxrow", [1, 1024], F32)
    B.dma(bxrow, rowsd[0:1, 0:1024], ch="bx", batch=True)
    rall = B.sb("rall", [128, NT], F32)
    rrow = B.sb("rrow", [1, T_SEQ], BF16)
    rb = B.sb("rb", [128, T_SEQ], BF16)
    nrm = B.sb("nrm", [128, 16], F32)
    dtmp = B.sb("dtmp", [128, 4], F32)
    E2 = B.sb("E_sb", [128, 8 * T_SEQ], BF16)
    E1 = A.bf16(8192, 8 * T_SEQ)

    def Etile(a, c0, c1):
        e = E1 if a < 8 else E2
        a2 = a % 8
        return e[:, a2 * T_SEQ + c0:a2 * T_SEQ + c1]

    xn = TMT(B, "xn", T_SEQ, D_MODEL, BF16, ap=A.bf16(0, 16384))
    for t in range(NT):
        xt = stage[:, 0:1024]
        B.dma(xt, xind[t], ch="x0")
        pre_ln_tile(B, xt, xn.tile(t), lncols[:, 4 * t:4 * t + 4],
                    scr[:, 0:1024])
    xnT = FMT(B, "xnT", D_MODEL, T_SEQ, BF16, ap=A.bf16(8192, 16384))
    tm_to_fm(B, xn, xnT, pool)
    Wx = FMT(B, None, 1024, 1024, BF16, ap=A.bf16(0, 8192))
    for k in range(8):
        st = stage[:, 0:1024]
        B.dma(st, wxd[128 * k:128 * (k + 1), :], ch="w0")
        B.copy(Wx.chunk(k), st, tag="wconv")
    H = TMT(B, None, T_SEQ, D_MODEL, BF16, ap=A.bf16(16384, 16384))
    hn = B.sb("hnscr", [128, 1024], BF16)
    hnT = FMT(B, None, D_MODEL, T_SEQ, BF16, ap=A.bf16(24576, 16384))
    for t in range(NT):
        for n0 in (0, 512):
            ps = mmA(B, pool, xnT, Wx, t, n0, 512,
                     bias_row=bxrow[0:1, 0:1024], tag="hmm")
            B.copy(H.tile(t, n0, n0 + 512), ps, tag="hev")
        B.act(scr, H.tile(t), AF.Square, tag="hn")
        B.red(nrm, scr.rearrange("p (h d) -> p h d", h=16), ALU.add,
              tag="hn")
        B.act(nrm, nrm, AF.Sqrt, tag="hn")
        B.recip(nrm, nrm)
        B.tt(hn.rearrange("p (h d) -> p h d", h=16),
             H.tile(t).rearrange("p (h d) -> p h d", h=16),
             nrm[:, :].unsqueeze(2).broadcast_to([128, 16, 64]), ALU.mult,
             tag="hn")
        bank = pool.bf16()
        for k in range(8):
            B.transpose(bank[:, 128 * k:128 * (k + 1)],
                        hn[:, 128 * k:128 * (k + 1)], B.ident, tag="hntr")
        for k in range(8):
            B.copy(hnT.chunk(k, t * 128, (t + 1) * 128),
                   bank[:, 128 * k:128 * (k + 1)], tag="hntrev")
    OT = FMT(B, None, D_MODEL, T_SEQ, BF16, ap=A.bf16(0, 16384))
    for h in range(16):
        hc, base = h // 2, 64 * (h % 2)

        def hrows(t0, t1):
            return hnT.sb[base:base + 64, hc * T_SEQ + t0:hc * T_SEQ + t1]
        for a in range(NT):
            for b0 in range(0, T_SEQ, 512):
                ps = pool.f32()
                B.mm_group(ps[:, 0:512],
                           [(hrows(a * 128, (a + 1) * 128),
                             hrows(b0, b0 + 512))], tag="g")
                B.act(Etile(a, b0, b0 + 512), ps[:, 0:512], AF.Exp,
                      accum_out=dtmp[:, (b0 // 512):(b0 // 512) + 1],
                      tag="ge")
            B.tt(dtmp[:, 0:1], dtmp[:, 0:1], dtmp[:, 1:2], ALU.add,
                 tag="dn")
            B.tt(dtmp[:, 2:3], dtmp[:, 2:3], dtmp[:, 3:4], ALU.add,
                 tag="dn")
            B.tt(dtmp[:, 0:1], dtmp[:, 0:1], dtmp[:, 2:3], ALU.add,
                 tag="dn")
            B.recip(rall[:, a:a + 1], dtmp[:, 0:1])
        rall_b = B.sb(f"rb16_{h}", [128, 1], BF16) if False else None
        for a in range(NT):
            B.copy(scr[:, 0:1].bitcast(BF16)[:, 0:1], rall[:, a:a + 1],
                   tag="rbc")
            bank = pool.bf16()
            B.transpose(bank[0:1, 0:128], scr[:, 0:1].bitcast(BF16)[:, 0:1],
                        B.ident, tag="rtr")
            B.copy(rrow[0:1, a * 128:(a + 1) * 128], bank[0:1, 0:128],
                   tag="rtr")
        for q0 in range(0, T_SEQ, 512):
            pb = pool.f32()
            B.mm_group(pb[:, 0:512],
                       [(B.onesb[0:1, 0:128], rrow[0:1, q0:q0 + 512])],
                       tag="rbb")
            B.copy(rb[:, q0:q0 + 512], pb[:, 0:512], tag="rbb")
        for q0 in range(0, T_SEQ, 512):
            ps = pool.f32()
            B.mm_group(ps[0:64, 0:512],
                       [(H.tile(k, 64 * h, 64 * h + 64),
                         Etile(k, q0, q0 + 512)) for k in range(NT)],
                       tag="pv")
            B.tt(OT.sb[base:base + 64,
                       hc * T_SEQ + q0:hc * T_SEQ + q0 + 512],
                 ps[0:64, 0:512], rb[base:base + 64, q0:q0 + 512],
                 ALU.mult, tag="pvev")
    Wp = FMT(B, None, 1024, 1024, BF16, ap=A.bf16(8192, 8192))
    for k in range(8):
        st = stage[:, 0:1024]
        B.dma(st, wpd[128 * k:128 * (k + 1), :], ch="w0")
        B.copy(Wp.chunk(k), st, tag="wconv")
    bprow = A.t[0:1, 12288:13312]
    B.dma(bprow, rowsd[0:1, 1024:2048], ch="bp", batch=True)
    res = B.sb("res", [128, 1024], F32)
    for t in range(NT):
        xt = stage[:, 0:1024]
        rt = res[:, 0:1024]
        B.dma(xt, xind[t], ch="x0")
        for n0 in (0, 512):
            ps = mmA(B, pool, OT, Wp, t, n0, 512,
                     bias_row=bprow[0:1, 0:1024], tag="wpmm")
            B.tt(rt[:, n0:n0 + 512], ps, xt[:, n0:n0 + 512], ALU.add,
                 tag="resid")
        B.dma(xoutd[t], rt, ch="xo", on="scalar")
    B.finalize()
    return B



def prep_l2_ffn(inp):
    f32 = np.float32
    w1 = np.asarray(inp["L2_ffn_w1"], f32)
    w2 = np.asarray(inp["L2_ffn_w2"], f32)
    w3 = np.asarray(inp["L2_ffn_w3"], f32)
    ws1 = np.asarray(inp["L2_ffn_ws1"], f32)
    ws2 = np.asarray(inp["L2_ffn_ws2"], f32)
    ws3 = np.asarray(inp["L2_ffn_ws3"], f32)
    w1r = np.zeros((17, 1024, 768), f32)
    w3r = np.zeros((17, 1024, 768), f32)
    w2r = np.zeros((17, 768, 1024), f32)
    for e in range(16):
        w1r[e] = w1[e].T
        w3r[e] = w3[e].T
        w2r[e] = w2[e].T
    w1r[16] = ws1.T
    w3r[16] = ws3.T
    w2r[16] = ws2.T
    return {
        "w1r": w1r, "w3r": w3r, "w2r": w2r,
        "gwT": np.ascontiguousarray(np.asarray(inp["L2_ffn_gate_w"],
                                               f32).T),
        "gbias": np.asarray(inp["L2_ffn_gate_bias"], f32).reshape(1, 16),
    }


def build_l2_ffn(nc):
    xin = nc.dram_tensor("xin", [T_SEQ, D_MODEL], F32,
                         kind="ExternalInput").ap()
    xout = nc.dram_tensor("xout", [T_SEQ, D_MODEL], F32,
                          kind="ExternalOutput").ap()
    w1d = nc.dram_tensor("w1r", [17, 1024, 768], F32,
                         kind="ExternalInput").ap()
    w3d = nc.dram_tensor("w3r", [17, 1024, 768], F32,
                         kind="ExternalInput").ap()
    w2d = nc.dram_tensor("w2r", [17, 768, 1024], F32,
                         kind="ExternalInput").ap()
    gwd = nc.dram_tensor("gwT", [1024, 16], F32, kind="ExternalInput").ap()
    gbd = nc.dram_tensor("gbias", [1, 16], F32, kind="ExternalInput").ap()

    B = Builder(nc)
    pool = PsumPool(B, n_f32=4)
    A = Arena(B, "A", 32768)
    xind = xin.rearrange("(t p) c -> t p c", p=128)
    xoutd = xout.rearrange("(t p) c -> t p c", p=128)
    stage = B.sb("stage", [128, 2048], F32)
    lncols = B.sb("lncols", [128, 4 * NT], F32)
    scr = B.sb("scr", [128, 1024], F32)
    rt16 = B.sb("rt16", [128, 96], F32)
    ccols = B.sb("ccols", [128, 16 * NT], F32)
    gbb = B.sb("gbb", [128, 16], F32)
    gwsb = FMT(B, "gwsb", 1024, 16, BF16)
    gwst = B.sb("gwst", [128, 128], F32)
    B.dma(gwst, gwd.rearrange("(k p) e -> p k e", p=128)
          .rearrange("p k e -> p (k e)") if False else gwst, ch="nul",
          batch=True) if False else None
    B.dma(gwst.rearrange("p (k e) -> p k e", k=8),
          gwd.rearrange("(k p) e -> p k e", p=128), ch="gw", batch=True)
    B.copy(gwsb.sb, gwst, tag="wconv")
    gbs = B.sb("gbs", [1, 16], F32)
    B.dma(gbs, gbd, ch="gb", batch=True)
    gb16 = B.sb("gb16", [1, 16], BF16)
    B.copy(gb16, gbs, tag="gb")
    pb = pool.f32()
    B.mm_group(pb[:, 0:16], [(B.onesb[0:1, 0:128], gb16[0:1, :])], tag="gbb")
    B.copy(gbb, pb[:, 0:16], tag="gbb")

    xn = TMT(B, "xn", T_SEQ, D_MODEL, BF16, ap=A.bf16(0, 16384))
    for t in range(NT):
        xt = stage[:, 0:1024]
        B.dma(xt, xind[t], ch="x0")
        pre_ln_tile(B, xt, xn.tile(t), lncols[:, 4 * t:4 * t + 4],
                    scr[:, 0:1024])
    xnT = FMT(B, "xnT", D_MODEL, T_SEQ, BF16, ap=A.bf16(8192, 16384))
    tm_to_fm(B, xn, xnT, pool)

    for t in range(NT):
        sc = rt16[:, 0:16]
        sb_ = rt16[:, 16:32]
        m1 = rt16[:, 32:36]
        mm_ = rt16[:, 36:52]
        gsum = rt16[:, 52:56]
        gcol = rt16[:, 56:57]
        gcol2 = rt16[:, 57:58]
        gmask = rt16[:, 58:62]
        ml = rt16[:, 62:78]
        sel = rt16[:, 78:94]
        fbi = rt16[:, 36:52]
        ps = mmA(B, pool, xnT, gwsb, t, 0, 16, tag="logit")
        B.act(sc, ps, AF.Sigmoid, tag="rt")
        B.tt(sb_, sc, gbb, ALU.add, tag="rt")
        gv = sb_.rearrange("p (g e) -> p g e", g=4)
        B.red(m1, gv, ALU.max, tag="rt")
        B.tt(mm_.rearrange("p (g e) -> p g e", g=4), gv,
             m1[:, :].unsqueeze(2).broadcast_to([128, 4, 4]), ALU.is_ge,
             tag="rt")
        B.stt(mm_, mm_, -2e9, sb_, ALU.mult, ALU.add, tag="rt")
        B.red(gsum, mm_.rearrange("p (g e) -> p g e", g=4), ALU.max,
              tag="rt")
        B.tt(gsum, gsum, m1, ALU.add, tag="rt")
        B.red(gcol, gsum, ALU.max, tag="rt")
        B.ts(gmask, gsum, gcol, None, ALU.is_ge, tag="rt")
        B.stt(rt16[:, 36:40], gmask, -2e9, gsum, ALU.mult, ALU.add, tag="rt")
        B.red(gcol2, rt16[:, 36:40], ALU.max, tag="rt")
        B.ts(rt16[:, 40:44], rt16[:, 36:40], gcol2, None, ALU.is_ge,
             tag="rt")
        B.tt(gmask, gmask, rt16[:, 40:44], ALU.add, tag="rt")
        emv = gmask.unsqueeze(2).broadcast_to([128, 4, 4])
        B.tt(ml.rearrange("p (g e) -> p g e", g=4), sb_.rearrange(
            "p (g e) -> p g e", g=4), emv, ALU.mult, tag="rt")
        B.ts(rt16[:, 36:40], gmask, -1.0, 1e30, ALU.add, ALU.mult, tag="rt")
        B.tt(ml.rearrange("p (g e) -> p g e", g=4),
             ml.rearrange("p (g e) -> p g e", g=4),
             rt16[:, 36:40].unsqueeze(2).broadcast_to([128, 4, 4]),
             ALU.add, tag="rt")
        B.memset(sel, 0.0, tag="rt")
        for it in range(4):
            B.red(gcol, ml, ALU.max, tag="rt")
            B.ts(fbi, ml, gcol, None, ALU.is_ge, tag="rt")
            B.tt(sel, sel, fbi, ALU.add, tag="rt")
            B.stt(ml, fbi, -1e31, ml, ALU.mult, ALU.add, tag="rt")
        B.tt(sel, sel, sc, ALU.mult, tag="rt")
        B.red(gcol, sel, ALU.add, tag="rt")
        B.ts(gcol, gcol, 1e-20, None, ALU.add, tag="rt")
        B.recip(gcol2, gcol)
        B.ts(ccols[:, 16 * t:16 * t + 16], sel, gcol2, 2.5, ALU.mult,
             ALU.mult, tag="rt")
    facc = TMT(B, None, T_SEQ, D_MODEL, F32, ap=A.f32(16384, 16384))
    hactT = FMT(B, None, 768, T_SEQ, BF16, ap=A.bf16(0, 12288))
    w2sb = FMT(B, "w2sb", 768, 1024, BF16)
    wcb = [FMT(B, f"wcb{i}", 1024, 128, BF16) for i in range(4)]
    gl = B.sb("gl", [128, 512], BF16)
    for e in range(17):
        for m in range(6):
            s1, s3 = 2 * (m % 2), 2 * (m % 2) + 1
            st1 = stage[:, 0:1024]
            st3 = stage[:, 1024:2048]
            B.dma(st1.rearrange("p (k n) -> p k n", k=8),
                  w1d[e, :, 128 * m:128 * (m + 1)].rearrange(
                      "(k p) n -> p k n", p=128), ch="w1")
            B.copy(wcb[s1].sb, st1, tag="wconv")
            B.dma(st3.rearrange("p (k n) -> p k n", k=8),
                  w3d[e, :, 128 * m:128 * (m + 1)].rearrange(
                      "(k p) n -> p k n", p=128), ch="w3")
            B.copy(wcb[s3].sb, st3, tag="wconv")
            for t0 in range(0, T_SEQ, 512):
                pg = pool.f32()
                B.mm_group(pg[:, 0:512],
                           [(wcb[s1].chunk(k), xnT.chunk(k, t0, t0 + 512))
                            for k in range(8)], tag="w1mm")
                pu = pool.f32()
                B.mm_group(pu[:, 0:512],
                           [(wcb[s3].chunk(k), xnT.chunk(k, t0, t0 + 512))
                            for k in range(8)], tag="w3mm")
                B.act(gl, pg[:, 0:512], AF.Silu, tag="silu")
                B.tt(hactT.chunk(m, t0, t0 + 512), pu[:, 0:512], gl,
                     ALU.mult, tag="hact")
        for k in range(6):
            st = stage[:, 0:1024]
            B.dma(st, w2d[e, 128 * k:128 * (k + 1), :], ch="w2")
            B.copy(w2sb.chunk(k), st, tag="wconv")
        for t in range(NT):
            cc = ccols[:, 16 * t + e:16 * t + e + 1] if e < 16 else None
            for n0 in (0, 512):
                ps = pool.f32()
                B.mm_group(ps[:, 0:512],
                           [(hactT.chunk(k, t * 128, (t + 1) * 128),
                             w2sb.chunk(k, n0, n0 + 512))
                            for k in range(6)], tag="w2mm")
                ft = facc.tile(t, n0, n0 + 512)
                if e == 0:
                    B.ts(ft, ps[:, 0:512], cc, None, ALU.mult, tag="facc")
                elif e < 16:
                    B.stt(ft, ps[:, 0:512], cc, ft, ALU.mult, ALU.add,
                          tag="facc")
                else:
                    B.tt(ft, ps[:, 0:512], ft, ALU.add, tag="facc")
    res = B.sb("res", [128, 1024], F32)
    for t in range(NT):
        xt = stage[:, 0:1024]
        B.dma(xt, xind[t], ch="x0")
        B.tt(res, facc.tile(t), xt, ALU.add, tag="resid")
        B.dma(xoutd[t], res, ch="xo", on="scalar")
    B.finalize()
    return B



_CACHE = {}


def _get_prog(name, builder):
    if name not in _CACHE:
        nc = bass.Bass("TRN2", target_bir_lowering=False)
        Bd = builder(nc)
        _CACHE[name] = (nc, Bd)
    return _CACHE[name]


_STAGES = [
    ("l0a", build_l0_attn, prep_l0_attn),
    ("l0f", build_l0_ffn, prep_l0_ffn),
    ("l1a", build_l1_attn, prep_l1_attn),
    ("l1f", lambda nc: build_local_stack(nc, 2, 8, 512),
     lambda i: prep_local_stack("L1", i, 2, 512)),
    ("l2a", build_l2_attn, prep_l2_attn),
    ("l2f", build_l2_ffn, prep_l2_ffn),
    ("l3a", build_l3_attn, prep_l3_attn),
    ("l3f", lambda nc: build_local_stack(nc, 6, 16, 256),
     lambda i: prep_local_stack("L3", i, 6, 256)),
]


def kernel(**inputs):
    x = np.ascontiguousarray(np.asarray(inputs["L0_attn_x"], np.float32))
    n = x.shape[0]
    for name, bld, prep in _STAGES:
        nc, Bd = _get_prog(name, bld)
        w = prep(inputs)
        consts = Bd.host_consts()
        in_maps = [dict(xin=np.ascontiguousarray(x[i]), **w, **consts)
                   for i in range(n)]
        res = run_bass_kernel_spmd(nc, in_maps, core_ids=list(range(n)))
        x = np.stack([np.asarray(r["xout"]) for r in res.results])
    return x


# version 03
# speedup vs baseline: 1.2076x; 1.2076x over previous
import math
import numpy as np
import ml_dtypes
import concourse.bass as bass
import concourse.mybir as mybir
from concourse.bass_utils import run_bass_kernel_spmd
from contextlib import ExitStack

F32 = mybir.dt.float32
I32 = mybir.dt.int32
BF16 = mybir.dt.bfloat16
AF = mybir.ActivationFunctionType
ALU = mybir.AluOpType
AX = mybir.AxisListType
BF16NP = ml_dtypes.bfloat16

ENG_NAMES = ("tensor", "vector", "scalar", "gpsimd", "sync")
REDUCE_WAITS = True



class _Op:
    __slots__ = ("engine", "emit", "reads", "writes", "dma_ch", "gid",
                 "inc_marked", "inc_index", "edges", "tag", "eng_pos")

    def __init__(self, engine, emit, reads, writes, dma_ch, gid, tag):
        self.engine = engine
        self.emit = emit
        self.reads = reads
        self.writes = writes
        self.dma_ch = dma_ch
        self.gid = gid
        self.inc_marked = False
        self.inc_index = None
        self.edges = []
        self.tag = tag
        self.eng_pos = None


class Region:
    __slots__ = ("name", "p0", "p1", "b0", "b1")

    def __init__(self, name, p0, p1, b0, b1):
        self.name, self.p0, self.p1, self.b0, self.b1 = name, p0, p1, b0, b1

    def overlaps(self, o):
        return (self.name == o.name and self.p0 < o.p1 and o.p0 < self.p1
                and self.b0 < o.b1 and o.b0 < self.b1)

    def covers(self, o):
        return (self.name == o.name and self.p0 <= o.p0 and o.p1 <= self.p1
                and self.b0 <= o.b0 and o.b1 <= self.b1)


PSUM_BANK_B = 2048


def _ap_region(ap):
    t = ap.tensor
    reg = _ap_region_raw(ap)
    if type(t).__name__ == "PSumTensorHandle":
        reg.b0 = (reg.b0 // PSUM_BANK_B) * PSUM_BANK_B
        reg.b1 = ((reg.b1 + PSUM_BANK_B - 1) // PSUM_BANK_B) * PSUM_BANK_B
        reg.p0, reg.p1 = 0, 256
    return reg


def _ap_region_raw(ap):
    t = ap.tensor
    name = t.name
    dts = mybir.dt.size(ap.dtype)
    shape = list(t.shape)
    is_sb = hasattr(t, "base_partition")
    dims = [list(d) for d in ap.ap]
    off_b = ap.offset * dts
    if not is_sb:
        lo = hi = off_b
        for step, count in dims:
            sb = step * dts
            if count <= 0:
                count = 1
            if sb >= 0:
                hi += sb * (count - 1)
            else:
                lo += sb * (count - 1)
        return Region(name, 0, 1, lo, hi + dts)
    tdts = mybir.dt.size(t.dtype)
    row_b = tdts
    for s in shape[1:]:
        row_b *= s
    base_p = getattr(t, "base_partition", 0) or 0
    if dims and dims[0][0] * dts == row_b and dims[0][0] > 0:
        p0 = off_b // row_b
        c0 = off_b % row_b
        lo = hi = c0
        for step, count in dims[1:]:
            sb = step * dts
            if count <= 0:
                count = 1
            if sb >= 0:
                hi += sb * (count - 1)
            else:
                lo += sb * (count - 1)
        if lo < 0 or hi + dts > row_b:
            return Region(name, 0, 256, 0, row_b)
        return Region(name, base_p + p0, base_p + p0 + dims[0][1],
                      lo, hi + dts)
    lo = hi = off_b % row_b
    p0 = off_b // row_b
    ok = True
    for step, count in dims:
        sb = step * dts
        if count <= 0:
            count = 1
        if sb >= 0:
            hi += sb * (count - 1)
        else:
            lo += sb * (count - 1)
    if lo >= 0 and hi + dts <= row_b:
        return Region(name, base_p + p0, base_p + p0 + 1, lo, hi + dts)
    return Region(name, 0, 256, 0, row_b)


def _aap(x):
    if x is None or isinstance(x, (int, float, bass.AP)):
        return x
    return x[:]


CMAX = 64


class Builder:
    def __init__(self, nc, paranoid=False, consts=True):
        self.nc = nc
        self.prefix = ""
        self.ga = None
        self.ops = []
        self.es = ExitStack()
        self.paranoid = paranoid
        self._uid = 0
        self._dma_counts = {}
        self._batch_chs = set()
        self._sb_names = set()
        self._const_vals = []
        self._const_idx = {}
        if consts:
            cd = nc.dram_tensor("c_scalars", [128, CMAX], F32,
                                kind="ExternalInput").ap()
            idd = nc.dram_tensor("c_ident", [128, 128], BF16,
                                 kind="ExternalInput").ap()
            onb = nc.dram_tensor("c_onesb", [128, 512], BF16,
                                 kind="ExternalInput").ap()
            onf = nc.dram_tensor("c_onesf", [128, 512], F32,
                                 kind="ExternalInput").ap()
            self.csb = self.sb("csb", [128, CMAX], F32)
            self.ident = self.sb("ident_sb", [128, 128], BF16)
            self.onesb = self.sb("onesb_sb", [128, 512], BF16)
            self.onesf = self.sb("onesf_sb", [128, 512], F32)
            self.dma(self.csb, cd, ch="consts", batch=True, tag="consts")
            self.dma(self.ident, idd, ch="consts", batch=True, tag="consts")
            self.dma(self.onesb, onb, ch="consts", batch=True, tag="consts")
            self.dma(self.onesf, onf, ch="consts", batch=True, tag="consts")

    def cs(self, value):
        v = float(value)
        if v not in self._const_idx:
            assert len(self._const_vals) < CMAX
            self._const_idx[v] = len(self._const_vals)
            self._const_vals.append(v)
        k = self._const_idx[v]
        return self.csb[:, k:k + 1]

    def host_consts(self):
        c = np.zeros((128, CMAX), np.float32)
        for i, v in enumerate(self._const_vals):
            c[:, i] = v
        return {
            "c_scalars": c,
            "c_ident": np.eye(128, dtype=BF16NP),
            "c_onesb": np.ones((128, 512), BF16NP),
            "c_onesf": np.ones((128, 512), np.float32),
        }

    def sb(self, name, shape, dtype):
        if self.ga is not None:
            return self.scratch(shape, dtype)
        name = self.prefix + name
        assert name not in self._sb_names, name
        self._sb_names.add(name)
        return self.es.enter_context(self.nc.sbuf_tensor(name, shape, dtype))

    def arena(self, n):
        assert self.ga is not None
        self._scr_floor = max(self._scr_floor, n)
        assert self._scr_floor <= self._scr_top
        return ArenaView(self.ga, 0)

    def init_ga(self, f32_cols):
        self.ga = Arena(self, "GA", f32_cols)
        self._scr_top = f32_cols

    def block_begin(self, pfx, floor=0):
        self.prefix = pfx + "_"
        self._scr_top = self.ga.cols
        self._scr_floor = floor

    def scratch(self, shape, dtype):
        p, n = shape
        fcols = (n * mybir.dt.size(dtype) + 3) // 4
        self._scr_top -= fcols
        assert self._scr_top >= self._scr_floor, \
            ("scratch underflow into arena", self._scr_top, self._scr_floor)
        c0 = self._scr_top
        if dtype == F32:
            ap = self.ga.f32(c0, n)
        elif dtype == BF16:
            ap = self.ga.bf16(c0, n)
        else:
            ap = self.ga.t[:, c0:c0 + fcols].bitcast(dtype)[:, 0:n]
        if p < 128:
            ap = ap[0:p, :]
        return ap

    def barrier(self):
        grp = len(self.ops)
        for e in ENG_NAMES:
            def emit(eng):
                return eng.nop()
            o = _Op(e, emit, [], [], None, len(self.ops), "BARR")
            o.inc_index = grp
            self.ops.append(o)

    def ps(self, name, shape, dtype=F32):
        return self.es.enter_context(self.nc.psum_tensor(name, shape, dtype))

    def uid(self):
        self._uid += 1
        return self._uid

    def op(self, engine, emit, reads, writes, dma_ch=None, tag=""):
        r = [x if isinstance(x, Region) else _ap_region(x) for x in reads]
        w = [x if isinstance(x, Region) else _ap_region(x) for x in writes]
        o = _Op(engine, emit, r, w, dma_ch, len(self.ops), tag)
        if dma_ch is not None:
            self._dma_counts[dma_ch] = self._dma_counts.get(dma_ch, 0) + 1
            o.inc_index = self._dma_counts[dma_ch]
        self.ops.append(o)
        return o

    def dma(self, out, in_, ch="ld", on="sync", batch=False, tag="dma"):
        out, in_ = _aap(out), _aap(in_)
        if batch:
            self._batch_chs.add(ch)

        def emit(eng):
            return eng.dma_start(out=out, in_=in_)
        return self.op(on, emit, [in_], [out], dma_ch=ch, tag=tag)

    def mm_group(self, psum_ap, pairs, start=True, stop=True, tag="mm"):
        psum_ap = _aap(psum_ap)
        pairs = [(_aap(l), _aap(r)) for (l, r) in pairs]
        n = len(pairs)

        def emit(eng):
            first = last = None
            for i, (l, r) in enumerate(pairs):
                last = eng.matmul(psum_ap, l, r,
                                  start=(start and i == 0),
                                  stop=(stop and i == n - 1))
                if first is None:
                    first = last
            return (first, last)
        reads = []
        for l, r in pairs:
            reads.append(l)
            reads.append(r)
        return self.op("tensor", emit, reads, [psum_ap], tag=tag)

    def transpose(self, psum_ap, in_ap, ident_ap, tag="tr"):
        psum_ap, in_ap, ident_ap = _aap(psum_ap), _aap(in_ap), _aap(ident_ap)
        def emit(eng):
            return eng.transpose(psum_ap, in_ap, ident_ap)
        return self.op("tensor", emit, [in_ap, ident_ap], [psum_ap], tag=tag)

    def act(self, out, in_, func, bias=0.0, scale=1.0, accum_out=None,
            tag="act"):
        out, in_, bias, scale, accum_out = (_aap(out), _aap(in_), _aap(bias),
                                            _aap(scale), _aap(accum_out))
        extra_r = [b for b in (bias, scale) if isinstance(b, bass.AP)]
        outs = [out] + ([accum_out] if accum_out is not None else [])

        def emit(eng):
            return eng.activation(out=out, in_=in_, func=func, bias=bias,
                                  scale=scale, accum_out=accum_out)
        return self.op("scalar", emit, [in_] + extra_r, outs, tag=tag)

    def tt(self, out, a, b, alu, engine="vector", tag="tt"):
        out, a, b = _aap(out), _aap(a), _aap(b)
        def emit(eng):
            return eng.tensor_tensor(out, a, b, alu)
        return self.op(engine, emit, [a, b], [out], tag=tag)

    def ts(self, out, in_, s1, s2, op0, op1=ALU.bypass, engine="vector",
           accum_out=None, tag="ts"):
        out, in_, s1, s2, accum_out = (_aap(out), _aap(in_), _aap(s1),
                                       _aap(s2), _aap(accum_out))
        extra = [s for s in (s1, s2) if isinstance(s, bass.AP)]
        outs = [out] + ([accum_out] if accum_out is not None else [])

        def emit(eng):
            return eng.tensor_scalar(out, in_, s1, s2, op0, op1,
                                     accum_out=accum_out)
        return self.op(engine, emit, [in_] + extra, outs, tag=tag)

    def stt(self, out, in0, scalar, in1, op0, op1, engine="vector", tag="stt"):
        out, in0, scalar, in1 = _aap(out), _aap(in0), _aap(scalar), _aap(in1)
        extra = [scalar] if isinstance(scalar, bass.AP) else []

        def emit(eng):
            return eng.scalar_tensor_tensor(out, in0, scalar, in1, op0, op1)
        return self.op(engine, emit, [in0, in1] + extra, [out], tag=tag)

    def red(self, out, in_, alu, negate=False, engine="vector", tag="red"):
        out, in_ = _aap(out), _aap(in_)
        def emit(eng):
            return eng.tensor_reduce(out, in_, AX.X, alu, negate=negate)
        return self.op(engine, emit, [in_], [out], tag=tag)

    def copy(self, out, in_, engine="vector", tag="cp"):
        out, in_ = _aap(out), _aap(in_)
        def emit(eng):
            return eng.tensor_copy(out, in_)
        return self.op(engine, emit, [in_], [out], tag=tag)

    def memset(self, out, val, engine="vector", tag="memset"):
        out = _aap(out)
        def emit(eng):
            return eng.memset(out, val)
        return self.op(engine, emit, [], [out], tag=tag)

    def recip(self, out, in_, tag="recip"):
        out, in_ = _aap(out), _aap(in_)
        def emit(eng):
            return eng.reciprocal(out, in_)
        return self.op("vector", emit, [in_], [out], tag=tag)

    def select(self, out, mask, on_true, on_false, tag="sel"):
        out, mask, on_true, on_false = (_aap(out), _aap(mask), _aap(on_true),
                                        _aap(on_false))
        def emit(eng):
            return eng.select(out, mask, on_true, on_false)
        return self.op("vector", emit, [mask, on_true, on_false], [out],
                       tag=tag)

    def raw(self, engine, emit, reads, writes, tag="raw"):
        reads = [_aap(r) for r in reads]
        writes = [_aap(w) for w in writes]
        return self.op(engine, emit, reads, writes, tag=tag)

    def _compute_edges(self):
        last_writes = {}
        reads_since = {}
        cur_barrier = []
        new_barrier = []
        barr_deps = None
        for o in self.ops:
            if o.tag == "BARR":
                if barr_deps is None:
                    barr_deps = set(cur_barrier)
                    for tbl in (last_writes, reads_since):
                        for lst in tbl.values():
                            for _, p in lst:
                                barr_deps.add(p)
                o.edges = [p for p in barr_deps if p is not o]
                new_barrier.append(o)
                continue
            if new_barrier:
                cur_barrier = new_barrier
                new_barrier = []
                barr_deps = None
                last_writes.clear()
                reads_since.clear()
            deps = set(cur_barrier)
            for r in o.reads:
                for reg, w in last_writes.get(r.name, ()):
                    if r.overlaps(reg):
                        deps.add(w)
            for wreg in o.writes:
                for reg, w in last_writes.get(wreg.name, ()):
                    if wreg.overlaps(reg):
                        deps.add(w)
                for reg, rd in reads_since.get(wreg.name, ()):
                    if wreg.overlaps(reg):
                        deps.add(rd)
            deps.discard(o)
            o.edges = [d for d in deps]
            del deps
            for wreg in o.writes:
                lw = last_writes.setdefault(wreg.name, [])
                lw[:] = [(reg, w) for (reg, w) in lw if not wreg.covers(reg)]
                lw.append((wreg, o))
                rs = reads_since.setdefault(wreg.name, [])
                rs[:] = [(reg, rr) for (reg, rr) in rs
                         if not wreg.covers(reg)]
            for r in o.reads:
                reads_since.setdefault(r.name, []).append((r, o))

    def _mark_incs(self):
        for o in self.ops:
            for p in o.edges:
                if p.dma_ch is None:
                    p.inc_marked = True
        counters = {e: 0 for e in ENG_NAMES}
        pos = {e: 0 for e in ENG_NAMES}
        for o in self.ops:
            if o.tag == "BARR":
                o.inc_index = None
            o.eng_pos = pos[o.engine]
            pos[o.engine] += 1
            if o.dma_ch is None and o.inc_marked:
                counters[o.engine] += 1
                o.inc_index = counters[o.engine]
        self._final_counts = counters

    def finalize(self):
        self._compute_edges()
        self._mark_incs()
        nc = self.nc
        sems = {}
        for e in ENG_NAMES:
            sems[e] = self.es.enter_context(nc.semaphore(f"sem_{e}"))
        for c in self._dma_counts:
            sems["dma_" + c] = self.es.enter_context(
                nc.semaphore(f"sem_dma_{c}"))

        import bisect
        ch_gids = {}
        for o in self.ops:
            if o.dma_ch is not None:
                ch_gids.setdefault(o.dma_ch, []).append(o.gid)

        waits = []
        for o in self.ops:
            wl = {}
            for p in o.edges:
                if p.dma_ch is not None:
                    key = "dma_" + p.dma_ch
                    if p.dma_ch in self._batch_chs:
                        val = 16 * bisect.bisect_left(ch_gids[p.dma_ch],
                                                      o.gid)
                    else:
                        val = 16 * p.inc_index
                else:
                    key = p.engine
                    val = p.inc_index
                if wl.get(key, -1) < val:
                    wl[key] = val
            waits.append(wl)
        marks = {e: {} for e in ENG_NAMES}
        for o, wl in zip(self.ops, waits):
            wm = marks[o.engine]
            red = {}
            for k, v in wl.items():
                if k == "tensor" and o.engine == "tensor":
                    continue
                if REDUCE_WAITS and wm.get(k, 0) >= v:
                    continue
                red[k] = v
                wm[k] = max(wm.get(k, 0), v)
            waits[o.gid] = red

        per_engine = {e: [] for e in ENG_NAMES}
        for o in self.ops:
            per_engine[o.engine].append(o)

        block = self.es.enter_context(nc.Block())

        def make_body(ename):
            ops_list = per_engine[ename]

            def body(eng):
                for o in ops_list:
                    wl = sorted(waits[o.gid].items())
                    for k, v in wl[1:]:
                        eng.wait_ge(sems[k], v)
                    ret = o.emit(eng)
                    first, inst = ret if isinstance(ret, tuple) else (ret,
                                                                      ret)
                    if wl:
                        k, v = wl[0]
                        first._wait_ge(sems[k], v)
                    if o.dma_ch is not None:
                        inst.then_inc(sems["dma_" + o.dma_ch], 16)
                    elif o.inc_marked:
                        inst.then_inc(sems[o.engine], 1)
                if ename == "sync":
                    for c, cnt in sorted(self._dma_counts.items()):
                        if cnt:
                            eng.wait_ge(sems["dma_" + c], 16 * cnt)
            return body

        block.tensor(make_body("tensor"))
        block.vector(make_body("vector"))
        block.scalar(make_body("scalar"))
        block.gpsimd(make_body("gpsimd"))
        block.sync(make_body("sync"))
        self.es.close()
        return nc



T_SEQ = 2048
NT = 16
D_MODEL = 1024


class Arena:

    def __init__(self, B, name, f32_cols):
        self.t = B.es.enter_context(
            B.nc.sbuf_tensor(name, [128, f32_cols], F32))
        self.cols = f32_cols

    def f32(self, c0, cols):
        assert c0 + cols <= self.cols, "arena overflow"
        return self.t[:, c0:c0 + cols]

    def bf16(self, c0_f32, bf_cols):
        assert 2 * c0_f32 + bf_cols <= 2 * self.cols, "arena overflow"
        return self.t[:, :].bitcast(BF16)[:, 2 * c0_f32:2 * c0_f32 + bf_cols]


class ArenaView:

    def __init__(self, arena, base):
        self.A = arena
        self.base = base
        self.cols = arena.cols - base

    def f32(self, c0, cols):
        return self.A.f32(self.base + c0, cols)

    def bf16(self, c0_f32, bf_cols):
        return self.A.bf16(self.base + c0_f32, bf_cols)

    def row(self, c0, cols):
        return self.A.t[0:1, self.base + c0:self.base + c0 + cols]


class FMT:

    def __init__(self, B, name, C, T, dtype, ap=None):
        self.C, self.T = C, T
        self.nk = (C + 127) // 128
        self.sb = ap if ap is not None else B.sb(name, [128, self.nk * T],
                                                 dtype)

    def rows(self, k):
        return min(128, self.C - 128 * k)

    def chunk(self, k, t0=0, t1=None):
        t1 = self.T if t1 is None else t1
        return self.sb[0:self.rows(k), k * self.T + t0:k * self.T + t1]


class TMT:

    def __init__(self, B, name, T, C, dtype, ap=None):
        self.T, self.C = T, C
        self.nt = (T + 127) // 128
        self.sb = ap if ap is not None else B.sb(name, [128, self.nt * C],
                                                 dtype)

    def tile(self, t, c0=0, c1=None):
        c1 = self.C if c1 is None else c1
        return self.sb[:, t * self.C + c0:t * self.C + c1]


class PsumPool:
    def __init__(self, B, n_f32=6, n_bf16=2, pref="pp"):
        self.f32s = [B.ps(f"{pref}_f{i}", [128, 512], F32)
                     for i in range(n_f32)]
        self.b16s = [B.ps(f"{pref}_b{i}", [128, 1024], BF16)
                     for i in range(n_bf16)]
        self.fi = 0
        self.bi = 0

    def f32(self):
        t = self.f32s[self.fi % len(self.f32s)]
        self.fi += 1
        return t

    def bf16(self):
        t = self.b16s[self.bi % len(self.b16s)]
        self.bi += 1
        return t


def load_x(B, x_dram, name="x"):
    x = TMT(B, name, T_SEQ, D_MODEL, F32)
    B.dma(x.sb.rearrange("p (t c) -> p t c", t=NT),
          x_dram.rearrange("(t p) c -> p t c", p=128), ch=name + "_in",
          batch=True)
    return x


def store_x(B, x, x_dram):
    B.dma(x_dram.rearrange("(t p) c -> p t c", p=128),
          x.sb.rearrange("p (t c) -> p t c", t=x.nt), ch="x_out",
          on="scalar", batch=True)


def pre_ln_tile(B, x_ap, out_ap, cols4, sq, eps=1e-5, pref="ln"):
    C = x_ap.shape[-1]
    negmu = cols4[:, 0:1]
    vs = cols4[:, 1:2]
    std = cols4[:, 2:3]
    rstd = cols4[:, 3:4]
    B.red(negmu, x_ap, ALU.add, negate=True, tag=pref)
    B.ts(negmu, negmu, 1.0 / C, None, ALU.mult, tag=pref)
    B.act(sq[:, 0:C], x_ap, AF.Square, bias=negmu, accum_out=vs, tag=pref)
    B.act(std, vs, AF.Sqrt, bias=B.cs(eps), scale=1.0 / C, tag=pref)
    B.recip(rstd, std)
    B.ts(out_ap, x_ap, negmu, rstd, ALU.add, ALU.mult, tag=pref)


def pre_ln(B, x, out, pref="ln", eps=1e-5, t_tiles=None, sq=None):
    C = x.C
    cols = B.scratch([128, 4 * x.nt], F32)
    if sq is None:
        sq = B.scratch([128, C], F32)
    for t in (t_tiles if t_tiles is not None else range(x.nt)):
        pre_ln_tile(B, x.tile(t), out.tile(t), cols[:, 4 * t:4 * t + 4], sq,
                    eps=eps, pref=pref)


def tm_to_fm(B, tm, fm, pool, t_tiles=None, k_chunks=None):
    ts_list = list(t_tiles if t_tiles is not None else range(tm.nt))
    ks = list(k_chunks if k_chunks is not None else range((tm.C + 127) // 128))
    for k in ks:
        rows = min(128, tm.C - 128 * k)
        for g0 in range(0, len(ts_list), 8):
            grp = ts_list[g0:g0 + 8]
            bank = pool.bf16()
            for j, t in enumerate(grp):
                B.transpose(bank[0:rows, j * 128:(j + 1) * 128],
                            tm.tile(t, 128 * k, 128 * k + rows),
                            B.ident, tag="tr")
            if grp == list(range(grp[0], grp[0] + len(grp))):
                B.copy(fm.sb[0:rows,
                             k * fm.T + grp[0] * 128:
                             k * fm.T + (grp[0] + len(grp)) * 128],
                       bank[0:rows, 0:len(grp) * 128], tag="trev")
            else:
                for j, t in enumerate(grp):
                    B.copy(fm.chunk(k, t * 128, (t + 1) * 128),
                           bank[0:rows, j * 128:(j + 1) * 128], tag="trev")


def load_weight(B, name, dram_ap, C, N, stage, out_dtype=BF16):
    nk = (C + 127) // 128
    w = FMT(B, "w_" + name, C, N, out_dtype)
    for k in range(nk):
        rows = min(128, C - 128 * k)
        half = (k % 2) * (stage.shape[1] // 2)
        st = stage[0:rows, half:half + N]
        B.dma(st, dram_ap[128 * k:128 * k + rows, :],
              ch=f"w_{name}_{k % 2}")
        B.copy(w.chunk(k), st, tag="wconv")
    return w


def load_vec_cols(B, name, dram_ap, n):
    v = B.scratch([128, n], F32)
    B.dma(v, dram_ap, ch="v_" + name, batch=True)
    return v


def load_rows(B, name, dram_ap, n):
    r = B.scratch([1, n], F32)
    B.dma(r, dram_ap, ch="r_" + name, batch=True)
    return r


def mmA(B, pool, xT, W, t, n0, nn, kcs=None, bias_row=None, extra=None,
        tag="mmA"):
    kcs = kcs if kcs is not None else range(xT.nk)
    bank = pool.f32()
    pairs = []
    for k in kcs:
        pairs.append((xT.chunk(k, t * 128, (t + 1) * 128),
                      W.chunk(k, n0, n0 + nn)))
    if bias_row is not None:
        pairs.append((B.onesf[0:1, 0:128], bias_row[0:1, n0:n0 + nn]))
    if extra:
        pairs.extend(extra)
    B.mm_group(bank[:, 0:nn], pairs, tag=tag)
    return bank[:, 0:nn]


def mmB(B, pool, W, xT, m, t0, nn, kcs=None, bias_row=None, tag="mmB"):
    kcs = kcs if kcs is not None else range(W.nk)
    mrows = min(128, W.T - m * 128)
    bank = pool.f32()
    pairs = []
    for k in kcs:
        pairs.append((W.chunk(k, m * 128, m * 128 + mrows),
                      xT.chunk(k, t0, t0 + nn)))
    if bias_row is not None:
        pairs.append((bias_row[0:1, m * 128:m * 128 + mrows],
                      B.onesf[0:1, 0:nn]))
    B.mm_group(bank[0:mrows, 0:nn], pairs, tag=tag)
    return bank[0:mrows, 0:nn]



PI = math.pi
TWO_PI = 2 * math.pi
SIN_OFF = PI + 1024 * TWO_PI


def prep_l0_attn(inp):
    f32 = np.float32
    rows = np.zeros((1, 4096), f32)
    rows[0, 0:1024] = inp["L0_attn_bq"]
    rows[0, 1024:2048] = inp["L0_attn_bk"]
    rows[0, 2048:3072] = inp["L0_attn_bv"]
    rows[0, 3072:4096] = inp["L0_attn_bo"]
    proj = np.asarray(inp["L0_attn_proj_mat"], f32)
    return {
        "wqT": np.ascontiguousarray(np.asarray(inp["L0_attn_Wq"], f32).T),
        "wkT": np.ascontiguousarray(np.asarray(inp["L0_attn_Wk"], f32).T),
        "wvT": np.ascontiguousarray(np.asarray(inp["L0_attn_Wv"], f32).T),
        "woT": np.ascontiguousarray(np.asarray(inp["L0_attn_Wo"], f32).T),
        "projdup": np.concatenate([proj, proj], 0),
        "rows": rows,
    }


def dram_l0a(nc, pfx=""):
    def dt_(n, shape, dt=F32):
        return nc.dram_tensor(pfx + n, shape, dt, kind="ExternalInput").ap()
    return dict(wqd=dt_("wqT", [1024, 1024]), wkd=dt_("wkT", [1024, 1024]),
                wvd=dt_("wvT", [1024, 1024]), wod=dt_("woT", [1024, 1024]),
                projd=dt_("projdup", [128, 128]),
                rowsd=dt_("rows", [1, 4096]))


def body_l0a(B, pool, d, xin, xout):
    wqd, wkd, wvd, wod, projd, rowsd = (d["wqd"], d["wkd"], d["wvd"],
                                        d["wod"], d["projd"], d["rowsd"])
    A = B.arena(42496)
    x = TMT(B, "x", T_SEQ, D_MODEL, F32, ap=A.f32(0, 16384))
    B.dma(x.sb.rearrange("p (t c) -> p t c", t=NT),
          xin.rearrange("(t p) c -> p t c", p=128), ch="x_in", batch=True)
    h = TMT(B, "h", T_SEQ, D_MODEL, BF16, ap=A.bf16(16384, 16384))
    rr0 = B.sb("rr0", [128, 4 * 512], F32)
    rr = [rr0, rr0]
    pre_ln(B, x, h, sq=rr0[:, 0:1024])
    hT = FMT(B, "hT", D_MODEL, T_SEQ, BF16, ap=A.bf16(24576, 16384))
    tm_to_fm(B, h, hT, pool)

    stage = A.f32(16384, 2048)
    Wq = FMT(B, None, 1024, 1024, BF16, ap=A.bf16(0, 8192))
    Wk = FMT(B, None, 1024, 1024, BF16, ap=A.bf16(4096, 8192))
    Wv = FMT(B, None, 1024, 1024, BF16, ap=A.bf16(8192, 8192))
    Wo = FMT(B, None, 1024, 1024, BF16, ap=A.bf16(12288, 8192))
    for nm, wd_, W in (("q", wqd, Wq), ("k", wkd, Wk), ("v", wvd, Wv),
                       ("o", wod, Wo)):
        for k in range(8):
            st = stage[:, (k % 2) * 1024:(k % 2) * 1024 + 1024]
            B.dma(st, wd_[128 * k:128 * (k + 1), :], ch=f"w{nm}{k % 2}")
            B.copy(W.chunk(k), st, tag="wconv")
    projsb = B.sb("projsb", [128, 128], BF16)
    projst = B.sb("projst", [128, 128], F32)
    B.dma(projst, projd, ch="proj", batch=True)
    B.ts(projsb, projst, 1.0 / (16.0 * PI), None, ALU.mult, tag="wconv")
    rows = load_rows(B, "l0", rowsd, 4096)

    outT = FMT(B, "outT", D_MODEL, T_SEQ, BF16, ap=A.bf16(32768, 16384))
    qTb = [A.bf16(18432 + i * 1024, 2048) for i in range(2)]
    kTb = [A.bf16(20480 + i * 1024, 2048) for i in range(2)]
    vb = [TMT(B, None, T_SEQ, 128, BF16, ap=A.bf16(22528 + i * 1024, 2048))
          for i in range(2)]
    kp = [A.bf16(40960 + i * 128, 256) for i in range(2)]
    qpT = [FMT(B, None, 256, 512, BF16, ap=A.bf16(41216 + i * 512, 1024))
           for i in range(2)]
    kvp = [B.ps(f"kvp{i}", [128, 512], F32) for i in range(2)]
    kvsb = [B.sb(f"kv{i}", [128, 128], BF16) for i in range(2)]
    otk = [B.sb(f"otk{i}", [128, 64], BF16) for i in range(2)]

    def fourier(s, psum, dst_cos, dst_sin):
        n = psum.shape[1]
        M = 12582912.0
        r1 = rr0[:, 0:n]
        t2 = rr0[:, 512:512 + n]
        r2 = rr0[:, 1024:1024 + n]
        f2 = rr0[:, 1536:1536 + n]
        B.ts(r1, psum, M, None, ALU.add, tag="four")
        B.stt(r1, r1, M, psum, ALU.subtract, ALU.subtract, tag="four")
        B.act(dst_sin, r1, AF.Sin, scale=TWO_PI, tag="four")
        B.ts(t2, psum, 0.25, None, ALU.add, tag="four")
        B.ts(r2, t2, M, None, ALU.add, tag="four")
        B.stt(f2, r2, M, t2, ALU.subtract, ALU.subtract, tag="four")
        B.act(dst_cos, f2, AF.Sin, scale=TWO_PI, tag="four")

    for hp in range(8):
        s = hp % 2
        for t0 in range(0, T_SEQ, 512):
            ps = mmB(B, pool, Wq, hT, hp, t0, 512,
                     bias_row=rows[0:1, 0:1024], tag="qkv")
            B.copy(qTb[s][:, t0:t0 + 512], ps, tag="qkv_ev")
            ps = mmB(B, pool, Wk, hT, hp, t0, 512,
                     bias_row=rows[0:1, 1024:2048], tag="qkv")
            B.copy(kTb[s][:, t0:t0 + 512], ps, tag="qkv_ev")
        for t in range(NT):
            ps = mmA(B, pool, hT, Wv, t, hp * 128, 128,
                     bias_row=rows[0:1, 2048:3072], tag="qkv")
            B.copy(vb[s].tile(t), ps, tag="qkv_ev")
        for h_ in (0, 1):
            head = 2 * hp + h_
            base = h_ * 64
            for t in range(NT):
                sk = t % 2
                pk = pool.f32()
                B.mm_group(pk[:, 0:128],
                           [(kTb[s][base:base + 64, t * 128:(t + 1) * 128],
                             projsb[base:base + 64, :])], tag="pk")
                fourier(sk, pk[:, 0:128], kp[sk][:, 0:128],
                        kp[sk][:, 128:256])
                for m in range(2):
                    B.mm_group(kvp[m][:, 0:64],
                               [(kp[sk][:, m * 128:(m + 1) * 128],
                                 vb[s].tile(t, base, base + 64))],
                               start=(t == 0), stop=(t == NT - 1), tag="kv")
            for m in range(2):
                B.copy(kvsb[s][:, m * 64:(m + 1) * 64], kvp[m][:, 0:64],
                       tag="kv_ev")
            for t0 in range(0, T_SEQ, 512):
                sq_ = (t0 // 512) % 2
                pq = pool.f32()
                B.mm_group(pq[:, 0:512],
                           [(projsb[base:base + 64, :],
                             qTb[s][base:base + 64, t0:t0 + 512])], tag="pq")
                fourier(sq_, pq[:, 0:512], qpT[sq_].chunk(0),
                        qpT[sq_].chunk(1))
                bank = pool.bf16()
                for j in range(4):
                    t = t0 // 128 + j
                    ops_ = pool.f32()
                    B.mm_group(ops_[:, 0:64],
                               [(qpT[sq_].chunk(0, j * 128, (j + 1) * 128),
                                 kvsb[s][:, 0:64]),
                                (qpT[sq_].chunk(1, j * 128, (j + 1) * 128),
                                 kvsb[s][:, 64:128])], tag="outh")
                    B.ts(otk[s], ops_[:, 0:64], 1.0 / 128.0, None, ALU.mult,
                         tag="outh_ev")
                    B.transpose(bank[base:base + 64, j * 128:(j + 1) * 128],
                                otk[s], B.ident, tag="outh_tr")
                B.copy(outT.sb[base:base + 64,
                               hp * T_SEQ + t0:hp * T_SEQ + t0 + 512],
                       bank[base:base + 64, 0:512], tag="outh_trev")

    xr = A.f32(16384, 4096)
    res = A.f32(20480, 4096)
    for t in range(NT):
        xt = xr[:, (t % 2) * 2048:(t % 2) * 2048 + 1024]
        rt = res[:, (t % 2) * 2048:(t % 2) * 2048 + 1024]
        B.dma(xt, xin.rearrange("(t p) c -> t p c", p=128)[t],
              ch=f"xr{t % 2}")
        for n0 in (0, 512):
            ps = mmA(B, pool, outT, Wo, t, n0, 512,
                     bias_row=rows[0:1, 3072:4096], tag="wo")
            B.tt(rt[:, n0:n0 + 512], ps, xt[:, n0:n0 + 512], ALU.add,
                 tag="resid")
        B.dma(xout.rearrange("(t p) c -> t p c", p=128)[t], rt,
              ch=f"xo{t % 2}", on="scalar")



def _causal_masks_l0f():
    p = np.arange(128)[:, None]
    c = np.arange(512)[None, :]
    m1 = np.stack([np.where(c <= 128 * j + p, 0.0, -1e30) for j in range(4)])
    mT = np.stack([np.where(c >= 128 * j + p, 0.0, -1e30) for j in range(4)])
    return (np.concatenate([m1.transpose(1, 0, 2).reshape(128, 2048),
                            mT.transpose(1, 0, 2).reshape(128, 2048)], 1)
            .astype(BF16NP))


def prep_l0_ffn(inp):
    f32 = np.float32
    qkvw = np.concatenate([np.asarray(inp["L0_ffn_q_proj"], f32),
                           np.asarray(inp["L0_ffn_k_proj"], f32),
                           np.asarray(inp["L0_ffn_v_proj"], f32)], axis=2)
    return {
        "qkvw": qkvw,
        "ow": np.asarray(inp["L0_ffn_o_proj"], f32),
        "simw": np.asarray(inp["L0_ffn_sim_matrix"], f32),
        "gates": np.asarray(inp["L0_ffn_gates"], f32).reshape(1, 8),
        "masksd": _causal_masks_l0f(),
    }


def dram_l0f(nc, pfx=""):
    def dt_(n, shape, dt=F32):
        return nc.dram_tensor(pfx + n, shape, dt, kind="ExternalInput").ap()
    return dict(qkvd=dt_("qkvw", [8, 1024, 384]),
                owd=dt_("ow", [8, 128, 1024]), simd=dt_("simw", [1024, 8]),
                gated=dt_("gates", [1, 8]),
                maskd=dt_("masksd", [128, 4096], BF16))


def body_l0f(B, pool, d, xin, xout):
    qkvd, owd, simd, gated, maskd = (d["qkvd"], d["owd"], d["simd"],
                                     d["gated"], d["maskd"])
    A = B.arena(25600)
    xs2 = A.f32(23552, 2048)
    xind = xin.rearrange("(t p) c -> t p c", p=128)
    xoutd = xout.rearrange("(t p) c -> t p c", p=128)
    h = TMT(B, "h", T_SEQ, D_MODEL, BF16, ap=A.bf16(0, 16384))
    lncols = B.sb("lncols", [128, 4 * NT], F32)
    sq = B.sb("sqscr", [128, 1024], F32)
    for t in range(NT):
        xt = xs2[:, (t % 2) * 1024:(t % 2) * 1024 + 1024]
        B.dma(xt, xind[t], ch=f"x{t % 2}")
        pre_ln_tile(B, xt, h.tile(t), lncols[:, 4 * t:4 * t + 4], sq)
    hT = FMT(B, "hT", D_MODEL, T_SEQ, BF16, ap=A.bf16(8192, 16384))
    tm_to_fm(B, h, hT, pool)
    rnorm = B.sb("rnorm", [128, NT], F32)
    rrec = B.sb("rrec", [128, NT], F32)
    hsq = B.sb("hsqscr", [128, 1024], F32)
    for t in range(NT):
        B.act(hsq, h.tile(t), AF.Square, accum_out=rnorm[:, t:t + 1],
              tag="rn")
        B.act(rnorm[:, t:t + 1], rnorm[:, t:t + 1], AF.Sqrt, tag="rn")
        B.ts(rnorm[:, t:t + 1], rnorm[:, t:t + 1], 1e-12, None, ALU.max,
             tag="rn")
        B.recip(rrec[:, t:t + 1], rnorm[:, t:t + 1])

    simsb = FMT(B, "simsb", 1024, 8, BF16)
    simst = B.sb("simst", [128, 64], F32)
    B.dma(simst.rearrange("p (k e) -> p k e", k=8),
          simd.rearrange("(k p) e -> p k e", p=128), ch="sim", batch=True)
    B.copy(simsb.sb, simst, tag="sim")
    simsq = B.sb("simsq", [128, 64], F32)
    B.act(simsq, simst, AF.Square, tag="sim")
    csps = pool.f32()
    B.mm_group(csps[0:1, 0:8],
               [(B.onesf[:, 0:1], simsq[:, 8 * k:8 * k + 8])
                for k in range(8)], tag="simn")
    csr = B.sb("csr", [1, 8], F32)
    B.act(csr, csps[0:1, 0:8], AF.Sqrt, tag="sim")
    B.ts(csr, csr, 1e-12, None, ALU.max, tag="sim")
    B.recip(csr, csr)
    csrb = B.sb("csrb", [128, 8], F32)
    csb16 = B.sb("csb16", [1, 8], BF16)
    B.copy(csb16, csr, tag="sim")
    csps2 = pool.f32()
    B.mm_group(csps2[:, 0:8], [(B.onesb[0:1, 0:128], csb16[0:1, :])],
               tag="simb")
    B.copy(csrb, csps2[:, 0:8], tag="sim")
    gsb = B.sb("gsb", [1, 8], F32)
    B.dma(gsb, gated, ch="gates", batch=True)
    sgate = B.sb("sgate", [1, 8], BF16)
    B.act(sgate, gsb, AF.Sigmoid, tag="gate")
    sgb = B.sb("sgb", [128, 8], F32)
    csps3 = pool.f32()
    B.mm_group(csps3[:, 0:8], [(B.onesb[0:1, 0:128], sgate[0:1, :])],
               tag="gateb")
    B.copy(sgb, csps3[:, 0:8], tag="gate")

    mcols = B.sb("mcols", [128, 8 * NT], F32)
    wcols = B.sb("wcols", [128, 8 * NT], F32)
    rt = B.sb("rtscr", [128, 64], F32)
    for t in range(NT):
        lg = rt[:, 0:8]
        gt_ = rt[:, 8:16]
        fb1 = rt[:, 16:24]
        m2s = rt[:, 24:32]
        ml = rt[:, 32:40]
        ex = rt[:, 40:48]
        col = rt[:, 48:49]
        col2 = rt[:, 49:50]
        mk = mcols[:, 8 * t:8 * t + 8]
        wk = wcols[:, 8 * t:8 * t + 8]
        ps = mmA(B, pool, hT, simsb, t, 0, 8, tag="logit")
        B.stt(lg, ps, rrec[:, t:t + 1], csrb, ALU.mult, ALU.mult, tag="rt")
        B.tt(lg, lg, sgb, ALU.subtract, tag="rt")
        B.ts(gt_, lg, 0.0, None, ALU.max, tag="rt")
        B.ts(mk, lg, 0.0, None, ALU.is_gt, tag="rt")
        B.red(col, mk, ALU.add, tag="rt")
        B.ts(col, col, 0.0, None, ALU.is_equal, tag="rt")
        B.red(col2, lg, ALU.max, tag="rt")
        B.ts(fb1, lg, col2, None, ALU.is_ge, tag="rt")
        B.stt(m2s, fb1, -2e9, lg, ALU.mult, ALU.add, tag="rt")
        B.red(col2, m2s, ALU.max, tag="rt")
        B.ts(m2s, m2s, col2, None, ALU.is_ge, tag="rt")
        B.tt(fb1, fb1, m2s, ALU.add, tag="rt")
        B.stt(mk, fb1, col, mk, ALU.mult, ALU.max, tag="rt")
        B.tt(ml, gt_, mk, ALU.mult, tag="rt")
        B.ts(ex, mk, -1.0, 1e9, ALU.add, ALU.mult, tag="rt")
        B.tt(ml, ml, ex, ALU.add, tag="rt")
        B.red(col2, ml, ALU.max, negate=True, tag="rt")
        B.act(ex, ml, AF.Exp, bias=col2, accum_out=col, tag="rt")
        B.recip(col, col)
        B.stt(wk, ex, col, mk, ALU.mult, ALU.mult, tag="rt")
    qkv = TMT(B, "qkv", T_SEQ, 384, F32, ap=A.f32(0, 6144))
    wstage = B.sb("wstage", [128, 2 * 3072], F32)
    qkvw_sb = [FMT(B, f"qkvw{i}", 1024, 384, BF16) for i in range(2)]
    for e in range(8):
        se = e % 2
        st = wstage[:, se * 3072:se * 3072 + 3072]
        B.dma(st.rearrange("p (k n) -> p k n", k=8),
              qkvd[e].rearrange("(k p) n -> p k n", p=128), ch=f"qkv{se}")
        B.copy(qkvw_sb[se].sb, st, tag="wconv")
        for t in range(NT):
            ps = mmA(B, pool, hT, qkvw_sb[se], t, 0, 384, tag="eqkv")
            mc = mcols[:, 8 * t + e:8 * t + e + 1]
            if e == 0:
                B.ts(qkv.tile(t), ps, mc, None, ALU.mult, tag="eacc")
            else:
                B.stt(qkv.tile(t), ps, mc, qkv.tile(t), ALU.mult, ALU.add,
                      tag="eacc")
    qT = FMT(B, "qT", 128, T_SEQ, BF16, ap=A.bf16(6144, 2048))
    kT = FMT(B, "kT", 128, T_SEQ, BF16, ap=A.bf16(7168, 2048))
    vb = TMT(B, "vb", T_SEQ, 128, BF16, ap=A.bf16(21504, 2048))
    qsc = B.sb("qsc", [128, 128], BF16)
    for t in range(NT):
        B.ts(qsc, qkv.tile(t, 0, 128), 1.0 / math.sqrt(128.0), None,
             ALU.mult, tag="qsc")
        bank = pool.bf16()
        B.transpose(bank[:, 0:128], qsc, B.ident, tag="qtr")
        B.copy(qT.chunk(0, t * 128, (t + 1) * 128), bank[:, 0:128],
               tag="qtr")
        B.copy(qsc, qkv.tile(t, 128, 256), tag="ksc")
        bank2 = pool.bf16()
        B.transpose(bank2[:, 128:256], qsc, B.ident, tag="ktr")
        B.copy(kT.chunk(0, t * 128, (t + 1) * 128), bank2[:, 128:256],
               tag="ktr")
        B.copy(vb.tile(t), qkv.tile(t, 256, 384), tag="vsc")
    masks = B.sb("masks", [128, 4096], BF16)
    B.dma(masks, maskd, ch="masks", batch=True)
    rm = B.sb("rm", [128, NT], F32)
    rscr = B.sb("rscr", [128, 2], F32)
    for t in range(NT):
        nchunks = t // 4 + 1
        for c5 in range(nchunks):
            ps = pool.f32()
            B.mm_group(ps[:, 0:512],
                       [(qT.chunk(0, t * 128, (t + 1) * 128),
                         kT.chunk(0, c5 * 512, (c5 + 1) * 512))], tag="s1")
            if c5 == t // 4:
                j = t % 4
                B.tt(ps[:, 0:512], ps[:, 0:512],
                     masks[:, 512 * j:512 * j + 512], ALU.add, tag="s1m")
            if c5 == 0:
                B.red(rm[:, t:t + 1], ps[:, 0:512], ALU.max, tag="s1r")
            else:
                B.red(rscr[:, 0:1], ps[:, 0:512], ALU.max, tag="s1r")
                B.tt(rm[:, t:t + 1], rm[:, t:t + 1], rscr[:, 0:1], ALU.max,
                     tag="s1r")
    rmb = B.sb("rmb", [128, NT], BF16)
    B.ts(rmb, rm, -1.0, None, ALU.mult, tag="rmneg")
    rmrow = B.sb("rmrow", [1, T_SEQ], BF16)
    for t in range(NT):
        bank = pool.bf16()
        B.transpose(bank[0:1, 0:128], rmb[:, t:t + 1], B.ident, tag="rmtr")
        B.copy(rmrow[0:1, 128 * t:128 * t + 128], bank[0:1, 0:128],
               tag="rmtr")
    ET = B.sb("ET", [128, 16 * 512], BF16)
    ao = TMT(B, "ao", T_SEQ, 128, BF16, ap=A.bf16(20480, 2048))
    dn = B.sb("dnrow", [1, 512], F32)
    for b in range(4):
        nblk = 4 * b + 4
        for tkb in range(nblk):
            ps = pool.f32()
            B.mm_group(ps[:, 0:512],
                       [(kT.chunk(0, tkb * 128, (tkb + 1) * 128),
                         qT.chunk(0, b * 512, (b + 1) * 512)),
                        (B.onesb[0:1, 0:128],
                         rmrow[0:1, b * 512:(b + 1) * 512])], tag="s2")
            if tkb >= 4 * b:
                j = tkb % 4
                B.tt(ps[:, 0:512], ps[:, 0:512],
                     masks[:, 2048 + 512 * j:2048 + 512 * j + 512],
                     ALU.add, tag="s2m")
            B.act(ET[:, 512 * tkb:512 * tkb + 512], ps[:, 0:512], AF.Exp,
                  tag="s2e")
        dps = pool.f32()
        B.mm_group(dps[0:1, 0:512],
                   [(B.onesb[:, 0:1], ET[:, 512 * k:512 * k + 512])
                    for k in range(nblk)], tag="dn")
        B.copy(dn, dps[0:1, 0:512], tag="dn")
        B.recip(dn, dn)
        for j in range(4):
            t = 4 * b + j
            ops_ = pool.f32()
            B.mm_group(ops_[:, 0:128],
                       [(ET[:, 512 * k + 128 * j:512 * k + 128 * j + 128],
                         vb.tile(k)) for k in range(nblk)], tag="pv")
            dcol = pool.f32()
            B.transpose(dcol[:, 0:1], dn[0:1, 128 * j:128 * j + 128],
                        B.cs(1.0)[0:1, 0:1], tag="dtr")
            B.ts(ao.tile(t), ops_[:, 0:128], dcol[:, 0:1], None, ALU.mult,
                 tag="aoev")
    aoT = FMT(B, "aoT", 128, T_SEQ, BF16, ap=A.bf16(22528, 2048))
    tm_to_fm(B, ao, aoT, pool)
    facc = TMT(B, "facc", T_SEQ, D_MODEL, F32, ap=A.f32(0, 16384))
    ostage = B.sb("ostage", [128, 2 * 1024], F32)
    ow_sb = [B.sb(f"ow{i}", [128, 1024], BF16) for i in range(2)]
    for e in range(8):
        se = e % 2
        st = ostage[:, se * 1024:se * 1024 + 1024]
        B.dma(st, owd[e], ch=f"ow{se}")
        B.copy(ow_sb[se], st, tag="wconv")
        for t in range(NT):
            wc = wcols[:, 8 * t + e:8 * t + e + 1]
            for n0 in (0, 512):
                ps = pool.f32()
                B.mm_group(ps[:, 0:512],
                           [(aoT.chunk(0, t * 128, (t + 1) * 128),
                             ow_sb[se][:, n0:n0 + 512])], tag="oproj")
                if e == 0:
                    B.ts(facc.tile(t, n0, n0 + 512), ps[:, 0:512], wc, None,
                         ALU.mult, tag="oacc")
                else:
                    B.stt(facc.tile(t, n0, n0 + 512), ps[:, 0:512], wc,
                          facc.tile(t, n0, n0 + 512), ALU.mult, ALU.add,
                          tag="oacc")
    for t in range(NT):
        xt = xs2[:, (t % 2) * 1024:(t % 2) * 1024 + 1024]
        B.dma(xt, xind[t], ch=f"x{t % 2}")
        B.tt(facc.tile(t), facc.tile(t), xt, ALU.add, tag="resid")
        B.dma(xoutd[t], facc.tile(t), ch=f"xo{t % 2}", on="scalar")



def _rope_tabs_l1a():
    d = 64
    inv = 1.0 / (10000.0 ** (np.arange(0, d, 2) / d))
    fr = np.arange(T_SEQ)[:, None] * inv[None, :]
    cos = np.repeat(np.cos(fr), 2, axis=1).T
    sin = np.repeat(np.sin(fr), 2, axis=1).T
    cos2 = np.concatenate([cos, cos], 0)
    sin2 = np.concatenate([sin, sin], 0)
    return np.concatenate([cos2, sin2], 1).astype(BF16NP)


def _rot_mat_128():
    R = np.zeros((128, 128), np.float32)
    for j in range(64):
        R[2 * j, 2 * j + 1] = -1.0
        R[2 * j + 1, 2 * j] = 1.0
    return R.T.astype(BF16NP)


def prep_l1_attn(inp):
    f32 = np.float32
    pmk = np.asarray(inp["L1_attn_pm_k"], f32)
    pmv = np.asarray(inp["L1_attn_pm_v"], f32)
    pmkT = np.zeros((128, 128), f32)
    pmv_h = np.zeros((16, 1024), f32)
    for h in range(16):
        pmkT[64 * (h % 2):64 * (h % 2) + 64, 16 * (h // 2):16 * (h // 2) + 16] \
            = pmk[h].T
        pmv_h[:, 64 * h:64 * h + 64] = pmv[h]
    vecs = np.zeros((128, 8), f32)
    vecs[:] = np.asarray(inp["L1_attn_norm_w"], f32).reshape(8, 128).T
    rows = np.asarray(inp["L1_attn_b_out"], f32).reshape(1, 1024)
    return {
        "wqkv": np.asarray(inp["L1_attn_w_qkv"], f32),
        "wout": np.asarray(inp["L1_attn_w_out"], f32),
        "pmkT": pmkT, "pmv": pmv_h,
        "vecs": vecs, "rows": rows,
        "tabs": _rope_tabs_l1a(), "rotT": _rot_mat_128(),
        "masksd": _causal_masks_l0f(),
    }


def dram_l1a(nc, pfx=""):
    def dt_(n, shape, dt=F32):
        return nc.dram_tensor(pfx + n, shape, dt, kind="ExternalInput").ap()
    return dict(wqkvd=dt_("wqkv", [1024, 3072]),
                woutd=dt_("wout", [1024, 1024]),
                pmkTd=dt_("pmkT", [128, 128]), pmvd=dt_("pmv", [16, 1024]),
                vecsd=dt_("vecs", [128, 8]), rowsd=dt_("rows", [1, 1024]),
                tabsd=dt_("tabs", [128, 2 * T_SEQ], BF16),
                rotd=dt_("rotT", [128, 128], BF16),
                maskd=dt_("masksd", [128, 4096], BF16))


def body_l1a(B, pool, d, xin, xout):
    (wqkvd, woutd, pmkTd, pmvd, vecsd, rowsd, tabsd, rotd, maskd) = (
        d["wqkvd"], d["woutd"], d["pmkTd"], d["pmvd"], d["vecsd"],
        d["rowsd"], d["tabsd"], d["rotd"], d["maskd"])
    A = B.arena(43008)
    xs2 = A.f32(40960, 2048)
    xind = xin.rearrange("(t p) c -> t p c", p=128)
    xoutd = xout.rearrange("(t p) c -> t p c", p=128)
    vecs = load_vec_cols(B, "l1a", vecsd, 8)
    tabs = B.sb("tabs_sb", [128, 2 * T_SEQ], BF16)
    B.dma(tabs, tabsd, ch="tabs", batch=True)
    rott = B.sb("rott", [128, 128], BF16)
    B.dma(rott, rotd, ch="rot", batch=True)
    masks = B.sb("masks", [128, 4096], BF16)
    B.dma(masks, maskd, ch="masks", batch=True)
    stage = B.sb("stage_l1a", [128, 1024], F32)
    pmk_sb = B.sb("pmk", [128, 128], BF16)
    B.dma(stage[:, 0:128], pmkTd, ch="pmk", batch=True)
    B.copy(pmk_sb, stage[:, 0:128], tag="wconv")
    pmv_sb = B.sb("pmv_sbuf", [16, 1024], BF16)
    B.dma(stage[0:16, 0:1024], pmvd, ch="pmv", batch=True)
    B.copy(pmv_sb, stage[0:16, 0:1024], tag="wconv")

    xn = TMT(B, "xn", T_SEQ, D_MODEL, BF16, ap=A.bf16(0, 16384))
    cols = B.sb("cols", [128, 8 * NT], F32)
    rsc = B.sb("rsc", [128, 2 * 512], F32)
    sq = rsc[:, 0:1024]
    for t in range(NT):
        xt = xs2[:, (t % 2) * 1024:(t % 2) * 1024 + 1024]
        B.dma(xt, xind[t], ch=f"x{t % 2}")
        c8 = cols[:, 8 * t:8 * t + 8]
        negmu, vs, std, rstd = (c8[:, 0:1], c8[:, 1:2], c8[:, 2:3], c8[:, 3:4])
        t1, rmsstd, rmsr, comb = (c8[:, 4:5], c8[:, 5:6], c8[:, 6:7],
                                  c8[:, 7:8])
        B.red(negmu, xt, ALU.add, negate=True, tag="ln")
        B.ts(negmu, negmu, 1.0 / 1024, None, ALU.mult, tag="ln")
        B.act(sq, xt, AF.Square, bias=negmu, accum_out=vs, tag="ln")
        B.act(std, vs, AF.Sqrt, bias=B.cs(1e-5), scale=1.0 / 1024, tag="ln")
        B.recip(rstd, std)
        B.tt(t1, rstd, rstd, ALU.mult, tag="rms")
        B.tt(t1, t1, vs, ALU.mult, tag="rms")
        B.act(rmsstd, t1, AF.Sqrt, bias=B.cs(1e-6), scale=1.0 / 1024,
              tag="rms")
        B.recip(rmsr, rmsstd)
        B.tt(comb, rstd, rmsr, ALU.mult, tag="rms")
        B.ts(xn.tile(t), xt, negmu, comb, ALU.add, ALU.mult, tag="ln")
    xnT = FMT(B, "xnT", D_MODEL, T_SEQ, BF16, ap=A.bf16(8192, 16384))
    tm_to_fm(B, xn, xnT, pool)

    v = TMT(B, "v", T_SEQ, D_MODEL, BF16, ap=A.bf16(32768, 16384))
    wvstage = stage
    Wv = FMT(B, "Wv", 1024, 1024, BF16, ap=A.bf16(0, 8192))
    for k in range(8):
        st = wvstage[:, 0:1024]
        B.dma(st, wqkvd[128 * k:128 * (k + 1), 2048:3072], ch=f"wv{k % 2}")
        B.ts(Wv.chunk(k), st, vecs[:, k:k + 1], None, ALU.mult, tag="wconv")
    for t in range(NT):
        for n0 in (0, 512):
            ps = mmA(B, pool, xnT, Wv, t, n0, 512, tag="vmm")
            B.copy(v.tile(t, n0, n0 + 512), ps, tag="vmm")

    qT = FMT(B, "qT", D_MODEL, T_SEQ, BF16, ap=A.bf16(16384, 16384))
    kT = FMT(B, "kT", D_MODEL, T_SEQ, BF16, ap=A.bf16(24576, 16384))
    qraw = [A.bf16(40960 + i * 1024, 2048) for i in range(2)]
    wqstage = wvstage
    wqb = [FMT(B, f"wqb{i}", 1024, 128, BF16) for i in range(2)]
    for m in range(16):
        sm = m % 2
        st = wqstage[:, 0:1024]
        B.dma(st.rearrange("p (k n) -> p k n", k=8),
              wqkvd[:, 128 * m:128 * (m + 1)]
              .rearrange("(k p) n -> p k n", p=128), ch=f"wq{sm}")
        for k in range(8):
            if m < 8:
                B.ts(wqb[sm].chunk(k), st[:, 128 * k:128 * k + 128],
                     vecs[:, k:k + 1], 0.125, ALU.mult, ALU.mult, tag="wconv")
            else:
                B.ts(wqb[sm].chunk(k), st[:, 128 * k:128 * k + 128],
                     vecs[:, k:k + 1], None, ALU.mult, tag="wconv")
        dst = qT if m < 8 else kT
        mc = m % 8
        for t0 in range(0, T_SEQ, 512):
            ps = mmB(B, pool, wqb[sm], xnT, 0, t0, 512, tag="qk")
            B.copy(qraw[sm][:, t0:t0 + 512], ps, tag="qk")
        for t0 in range(0, T_SEQ, 512):
            pr = pool.f32()
            B.mm_group(pr[:, 0:512], [(rott, qraw[sm][:, t0:t0 + 512])],
                       tag="rope")
            s1 = rsc[:, 0:512]
            s2 = rsc[:, 512:1024]
            B.tt(s1, qraw[sm][:, t0:t0 + 512], tabs[:, t0:t0 + 512],
                 ALU.mult, tag="rope")
            B.tt(s2, pr[:, 0:512], tabs[:, T_SEQ + t0:T_SEQ + t0 + 512],
                 ALU.mult, tag="rope")
            B.tt(dst.chunk(mc, t0, t0 + 512), s1, s2, ALU.add, tag="rope")

    outT = FMT(B, "outT", D_MODEL, T_SEQ, BF16, ap=A.bf16(0, 16384))
    ET = A.bf16(40960, 4 * 512)
    ETpm = A.bf16(41984, 512)[0:16, :]
    rm = B.sb("rm", [128, 2 * NT], F32)
    rmb = B.sb("rmb", [128, 1], BF16)
    rmrow = B.sb("rmrow", [1, 512], BF16)
    dn = B.sb("dnrow", [1, 512], F32)
    aotk = B.sb("aotk", [128, 64], BF16)
    for seg in range(4):
        for h in range(16):
            hc, base = h // 2, 64 * (h % 2)
            qrows = lambda t0, t1: qT.sb[base:base + 64,
                                         hc * T_SEQ + t0:hc * T_SEQ + t1]
            krows = lambda t0, t1: kT.sb[base:base + 64,
                                         hc * T_SEQ + t0:hc * T_SEQ + t1]
            s0 = seg * 512
            for j in range(4):
                t = 4 * seg + j
                pp = pool.f32()
                B.mm_group(pp[:, 0:16],
                           [(qrows(s0 + 128 * j, s0 + 128 * (j + 1)),
                             pmk_sb[base:base + 64, 16 * hc:16 * hc + 16])],
                           tag="s1pm")
                ps = pool.f32()
                B.mm_group(ps[:, 0:512],
                           [(qrows(s0 + 128 * j, s0 + 128 * (j + 1)),
                             krows(s0, s0 + 512))], tag="s1")
                B.tt(ps[:, 0:512], ps[:, 0:512],
                     masks[:, 512 * j:512 * j + 512], ALU.add, tag="s1m")
                rmc = rm[:, 2 * t:2 * t + 1]
                rs2 = rm[:, 2 * t + 1:2 * t + 2]
                B.red(rmc, pp[:, 0:16], ALU.max, tag="s1r")
                B.red(rs2, ps[:, 0:512], ALU.max, tag="s1r")
                B.tt(rmc, rmc, rs2, ALU.max, tag="s1r")
            for j in range(4):
                t = 4 * seg + j
                B.ts(rmb, rm[:, 2 * t:2 * t + 1], -1.0, None, ALU.mult,
                     tag="rmneg")
                bank = pool.bf16()
                B.transpose(bank[0:1, 0:128], rmb, B.ident, tag="rmtr")
                B.copy(rmrow[0:1, 128 * j:128 * j + 128], bank[0:1, 0:128],
                       tag="rmtr")
            pe = pool.f32()
            B.mm_group(pe[0:16, 0:512],
                       [(pmk_sb[base:base + 64, 16 * hc:16 * hc + 16],
                         qrows(s0, s0 + 512)),
                        (B.onesb[0:1, 0:16], rmrow[0:1, 0:512])], tag="s2pm")
            B.act(ETpm, pe[0:16, 0:512], AF.Exp, tag="s2e")
            for jt in range(4):
                ps = pool.f32()
                B.mm_group(ps[:, 0:512],
                           [(krows(s0 + 128 * jt, s0 + 128 * (jt + 1)),
                             qrows(s0, s0 + 512)),
                            (B.onesb[0:1, 0:128], rmrow[0:1, 0:512])],
                           tag="s2")
                B.tt(ps[:, 0:512], ps[:, 0:512],
                     masks[:, 2048 + 512 * jt:2048 + 512 * jt + 512],
                     ALU.add, tag="s2m")
                B.act(ET[:, 512 * jt:512 * jt + 512], ps[:, 0:512], AF.Exp,
                      tag="s2e")
            dps = pool.f32()
            B.mm_group(dps[0:1, 0:512],
                       [(B.onesb[0:16, 0:1], ETpm)] +
                       [(B.onesb[:, 0:1], ET[:, 512 * k:512 * k + 512])
                        for k in range(4)], tag="dn")
            B.copy(dn, dps[0:1, 0:512], tag="dn")
            B.recip(dn, dn)
            for j in range(4):
                t = 4 * seg + j
                ops_ = pool.f32()
                B.mm_group(ops_[:, 0:64],
                           [(ETpm[:, 128 * j:128 * j + 128],
                             pmv_sb[:, 64 * h:64 * h + 64])] +
                           [(ET[:, 512 * k + 128 * j:512 * k + 128 * j + 128],
                             v.tile(4 * seg + k, 64 * h, 64 * h + 64))
                            for k in range(4)], tag="pv")
                dcol = pool.f32()
                B.transpose(dcol[:, 0:1], dn[0:1, 128 * j:128 * j + 128],
                            B.cs(1.0)[0:1, 0:1], tag="dtr")
                B.ts(aotk, ops_[:, 0:64], dcol[:, 0:1], None, ALU.mult,
                     tag="aoev")
                bank = pool.bf16()
                B.transpose(bank[base:base + 64, 0:128], aotk, B.ident,
                            tag="aotr")
                B.copy(outT.sb[base:base + 64,
                               hc * T_SEQ + 128 * t:hc * T_SEQ + 128 * t + 128],
                       bank[base:base + 64, 0:128], tag="aotr")
    wostage = wvstage
    Wo = FMT(B, "Wo", 1024, 1024, BF16, ap=A.bf16(16384, 8192))
    rows = A.row(24576, 1024)
    B.dma(rows, rowsd, ch="rows_l1a", batch=True)
    for k in range(8):
        st = wostage[:, 0:1024]
        B.dma(st, woutd[128 * k:128 * (k + 1), :], ch=f"wo{k % 2}")
        B.copy(Wo.chunk(k), st, tag="wconv")
    res = A.f32(32768, 2048)
    for t in range(NT):
        xt = xs2[:, (t % 2) * 1024:(t % 2) * 1024 + 1024]
        rt = res[:, (t % 2) * 1024:(t % 2) * 1024 + 1024]
        B.dma(xt, xind[t], ch=f"x{t % 2}")
        for n0 in (0, 512):
            ps = mmA(B, pool, outT, Wo, t, n0, 512,
                     bias_row=rows[0:1, 0:1024], tag="womm")
            B.tt(rt[:, n0:n0 + 512], ps, xt[:, n0:n0 + 512], ALU.add,
                 tag="resid")
        B.dma(xoutd[t], rt, ch=f"xo{t % 2}", on="scalar")



def _local_masks(win, span):
    ntw, nts = win // 128, span // 128
    p = np.arange(128)
    out = []
    for jq in range(ntw):
        gi = (span - win) + 128 * jq + p[:, None]
        gj = np.arange(span)[None, :]
        out.append(np.where((gj <= gi) & (gi <= gj + win), 0.0, -1e30))
    for jk in range(nts):
        gj = 128 * jk + p[:, None]
        gi = (span - win) + np.arange(win)[None, :]
        out.append(np.where((gj <= gi) & (gi <= gj + win), 0.0, -1e30))
    return np.concatenate(out, 1).astype(BF16NP)


def _xpos_tabs(win):
    d = 64
    half = np.arange(0, d, 2).astype(np.float64)
    invf = 1.0 / 10000.0 ** (half / d)
    t = np.arange(T_SEQ, dtype=np.float64)
    fr = t[:, None] * invf[None, :]
    fr2 = np.concatenate([fr, fr], 1)
    base = (half + 0.4 * d) / (1.4 * d)
    power = (t - T_SEQ // 2) / (win // 2)
    sc = base[None, :] ** power[:, None]
    sc2 = np.concatenate([sc, sc], 1)
    sign = np.ones(64)
    sign[:32] = -1.0
    cq = np.cos(fr2) * sc2 * 8.0
    sq = np.sin(fr2) * sc2 * 8.0 * sign[None, :]
    ck = np.cos(fr2) / sc2
    sk = np.sin(fr2) / sc2 * sign[None, :]
    return np.concatenate([cq, sq, ck, sk], 1).astype(np.float32)


def _relay_w1(w1, b1):
    hid = w1.shape[1] // 2
    nkc = (hid + 127) // 128
    wout = np.zeros((w1.shape[0], 256 * nkc), np.float32)
    b1a = np.zeros((128, nkc), np.float32)
    b1g = np.zeros((128, nkc), np.float32)
    for j in range(nkc):
        r = min(128, hid - 128 * j)
        wout[:, 256 * j:256 * j + r] = w1[:, 128 * j:128 * j + r]
        wout[:, 256 * j + 128:256 * j + 128 + r] = \
            w1[:, hid + 128 * j:hid + 128 * j + r]
        b1a[:r, j] = b1[128 * j:128 * j + r]
        b1g[:r, j] = b1[hid + 128 * j:hid + 128 * j + r]
    return wout, b1a, b1g


def prep_local_stack(pref, inp, nlayers, win):
    f32 = np.float32
    names = {"L1": dict(wqkv="L1_ffn_wqkv", qs="L1_ffn_q_scale",
                        ks="L1_ffn_k_scale", wo="L1_ffn_wo", lg="L1_ffn_ln_g",
                        lb="L1_ffn_ln_b", w1="L1_ffn_w1", b1="L1_ffn_b1",
                        w2="L1_ffn_w2", b2="L1_ffn_b2"),
             "L3": dict(wqkv="L3_ffn_Wqkv", qs="L3_ffn_q_scale",
                        ks="L3_ffn_k_scale", wo="L3_ffn_Wo", lg="L3_ffn_ln_g",
                        lb="L3_ffn_ln_b", w1="L3_ffn_W1", b1="L3_ffn_b1",
                        w2="L3_ffn_W2", b2="L3_ffn_b2")}[pref]
    out = {"tabs": _xpos_tabs(win), "masksd": _local_masks(win, 2 * win)}
    hid = np.asarray(inp[names["w1"]], f32).shape[2] // 2
    nkc = (hid + 127) // 128
    w1r = np.zeros((nlayers, 1024, 256 * nkc), f32)
    b1a = np.zeros((nlayers, 128, nkc), f32)
    b1g = np.zeros((nlayers, 128, nkc), f32)
    for l in range(nlayers):
        w1r[l], b1a[l], b1g[l] = _relay_w1(
            np.asarray(inp[names["w1"]][l], f32),
            np.asarray(inp[names["b1"]][l], f32))
    qs = np.asarray(inp[names["qs"]], f32)
    ks = np.asarray(inp[names["ks"]], f32)
    qkrows = np.zeros((1, nlayers * 128), f32)
    for l in range(nlayers):
        qkrows[0, 128 * l:128 * l + 64] = qs[l]
        qkrows[0, 128 * l + 64:128 * l + 128] = ks[l]
    lglb = np.zeros((1, nlayers * 2048), f32)
    b2r = np.zeros((1, nlayers * 1024), f32)
    for l in range(nlayers):
        lglb[0, 2048 * l:2048 * l + 1024] = np.asarray(inp[names["lg"]][l])
        lglb[0, 2048 * l + 1024:2048 * l + 2048] = \
            np.asarray(inp[names["lb"]][l])
        b2r[0, 1024 * l:1024 * (l + 1)] = np.asarray(inp[names["b2"]][l])
    out.update({
        "wqkv": np.asarray(inp[names["wqkv"]], f32),
        "wo": np.asarray(inp[names["wo"]], f32),
        "w1r": w1r, "b1a": b1a, "b1g": b1g,
        "w2": np.asarray(inp[names["w2"]], f32),
        "qkrows": qkrows, "lglb": lglb, "b2r": b2r,
    })
    return out


class Cursor:
    def __init__(self, arena, start=0):
        self.A = arena
        self.c = start

    def f32(self, n):
        ap = self.A.f32(self.c, n)
        self.c += n
        return ap

    def bf16(self, n):
        f = (n + 1) // 2
        ap = self.A.bf16(self.c, n)
        self.c += f
        return ap


def dram_local(nc, nlayers, heads, win, pfx=""):
    H = heads
    DQ = 64 * H
    span = 2 * win
    ntw, nts = win // 128, span // 128
    NKC = 22
    HID = 2730

    def dt_(n, shape, dt=F32):
        return nc.dram_tensor(pfx + n, shape, dt, kind="ExternalInput").ap()
    return dict(wqkvd=dt_("wqkv", [nlayers, 1024, 3 * DQ]),
                wod=dt_("wo", [nlayers, DQ, 1024]),
                w1d=dt_("w1r", [nlayers, 1024, 256 * NKC]),
                b1ad=dt_("b1a", [nlayers, 128, NKC]),
                b1gd=dt_("b1g", [nlayers, 128, NKC]),
                w2d=dt_("w2", [nlayers, HID, 1024]),
                qkrd=dt_("qkrows", [1, nlayers * 128]),
                lglbd=dt_("lglb", [1, nlayers * 2048]),
                b2rd=dt_("b2r", [1, nlayers * 1024]),
                tabsd=dt_("tabs", [T_SEQ, 256]),
                maskd=dt_("masksd", [128, ntw * span + nts * win], BF16))


def body_local(B, pool, d, xin, xout, nlayers, heads, win):
    H = heads
    DQ = 64 * H
    NDQ = DQ // 128
    span = 2 * win
    ntw, nts = win // 128, span // 128
    nwin = T_SEQ // win
    HID = 2730
    NKC = 22
    (wqkvd, wod, w1d, b1ad, b1gd, w2d, qkrd, lglbd, b2rd, tabsd, maskd) = (
        d["wqkvd"], d["wod"], d["w1d"], d["b1ad"], d["b1gd"], d["w2d"],
        d["qkrd"], d["lglbd"], d["b2rd"], d["tabsd"], d["maskd"])
    xind = xin.rearrange("(t p) c -> t p c", p=128)
    xoutd = xout.rearrange("(t p) c -> t p c", p=128)

    xi = TMT(B, "xi", T_SEQ, D_MODEL, F32)
    attn_f = (24 * DQ + NDQ * 1024 + 2 * DQ + 2 * NDQ * win
              + NDQ * nts * 128 + nts * DQ + nts * win + 2) // 2
    ffn_f = ((16384 + NKC * 512 + 2 * NKC * 128 + 2 * 8 * 256
              + 1024 + 512 + 512 + 2) // 2
             + 2 * 1024 + 11 * NT + 64)
    A = B.arena(max(attn_f, ffn_f) + 16)
    ca = Cursor(A)
    wqkv = FMT(B, None, 1024, 3 * DQ, BF16, ap=ca.bf16(8 * 3 * DQ))
    wosb = FMT(B, None, DQ, 1024, BF16, ap=ca.bf16(NDQ * 1024))
    qkl = ca.bf16(2 * DQ)
    qTf = FMT(B, None, DQ, win, BF16, ap=ca.bf16(NDQ * win))
    kTf = FMT(B, None, DQ, nts * 128, BF16, ap=ca.bf16(NDQ * nts * 128))
    vbr = [ca.bf16(DQ) for _ in range(nts)]
    outTw = FMT(B, None, DQ, win, BF16, ap=ca.bf16(NDQ * win))
    ET = ca.bf16(nts * win)
    attn_top = ca.c
    cf = Cursor(A)
    h2nT = FMT(B, None, D_MODEL, T_SEQ, BF16, ap=cf.bf16(8 * T_SEQ))
    uT = FMT(B, None, HID, 512, BF16, ap=cf.bf16(NKC * 512))
    w2b = [FMT(B, None, HID, 128, BF16, ap=cf.bf16(NKC * 128))
           for _ in range(2)]
    lgb = cf.f32(1024)
    lbb = cf.f32(1024)
    yTs = cf.bf16(512)
    h2n = cf.bf16(1024)
    w1b = [FMT(B, None, 1024, 256, BF16, ap=cf.bf16(8 * 256))
           for _ in range(2)]
    glscr = cf.bf16(512)
    fcols = cf.f32(11 * NT)
    b1ac = cf.f32(NKC)
    b1gc = cf.f32(NKC)
    b2c = cf.f32(8)
    edgec = cf.f32(5)
    assert max(attn_top, cf.c) <= A.cols, (attn_top, cf.c, A.cols)

    tabs = TMT(B, "tabs_sb", T_SEQ, 256, F32)
    B.dma(tabs.sb.rearrange("p (t c) -> p t c", t=NT),
          tabsd.rearrange("(t p) c -> p t c", p=128), ch="tabs", batch=True)
    masks = B.sb("masks_sb", [128, ntw * span + nts * win], BF16)
    B.dma(masks, maskd, ch="masks", batch=True)

    def mask1(jq):
        return masks[:, jq * span:(jq + 1) * span]

    def mask2(jk):
        return masks[:, ntw * span + jk * win:ntw * span + (jk + 1) * win]

    qkrows = load_rows(B, "qkr", qkrd, nlayers * 128)
    cols = B.sb("cols", [128, 4 * NT], F32)
    stage = B.sb("stage", [128, 3072], F32)
    qn = stage[:, 0:2 * DQ]
    sq = stage[:, 2048:3072]
    rawT = B.sb("rawT", [128, 8 * 128], BF16)
    bnd = B.sb("bnd", [128, 4], BF16)
    htu = B.sb("htu", [128, 4 * 128], BF16)
    xb = B.sb("xbscr", [128, 1024], BF16)
    qp = xb
    rstdh = B.sb("rstdh", [128, H], F32)
    qsb = B.sb("qsb", [128, 128], F32)
    rm = B.sb("rm", [128, 2], F32)
    Etok = B.sb("Etok", [128, span], BF16)
    dprt = B.sb("dprt", [128, 2], F32)
    dcl = B.sb("dcl", [128, ntw], F32)
    aotk = B.sb("aotk", [128, 64], BF16)

    for t in range(NT):
        xt = stage[:, (t % 2) * 1024:(t % 2) * 1024 + 1024]
        B.dma(xt, xind[t], ch=f"x{t % 2}")
        pre_ln_tile(B, xt, xi.tile(t), cols[:, 4 * t:4 * t + 4], sq)

    def hview(ap_):
        return ap_.rearrange("p (h d) -> p h d", h=H)

    def l2norm_xpos(t, src, dst, tc, tsn, qk):
        B.act(qn[:, 0:DQ], src, AF.Square, tag="l2")
        B.red(rstdh, hview(qn[:, 0:DQ]), ALU.add, tag="l2")
        B.act(rstdh, rstdh, AF.Sqrt, tag="l2")
        B.ts(rstdh, rstdh, 1e-12, None, ALU.max, tag="l2")
        B.recip(rstdh, rstdh)
        rb = rstdh[:, :].unsqueeze(2).broadcast_to([128, H, 64])
        B.tt(hview(qn[:, 0:DQ]), hview(src), rb, ALU.mult, tag="l2")
        qsv = qsb[:, qk * 64:qk * 64 + 64].unsqueeze(1).broadcast_to(
            [128, H, 64])
        B.tt(hview(qn[:, DQ:2 * DQ]), hview(qn[:, 0:DQ]), qsv, ALU.mult,
             tag="l2")
        cosv = tabs.tile(t, tc, tc + 64).unsqueeze(1).broadcast_to(
            [128, H, 64])
        sinv = tabs.tile(t, tsn, tsn + 64).unsqueeze(1).broadcast_to(
            [128, H, 64])
        qn2 = qn[:, DQ:2 * DQ]
        swv = qn2.rearrange("p (h two j) -> p h two j", h=H,
                            two=2)[:, :, ::-1, :]
        B.tt(hview(qn[:, 0:DQ]), hview(qn2), cosv, ALU.mult, tag="xpos")
        B.tt(hview(dst), swv,
             sinv.rearrange("p h (two j) -> p h two j", two=2), ALU.mult,
             tag="xpos")
        B.tt(dst, qn[:, 0:DQ], dst, ALU.add, tag="xpos")

    def attn_layer(l):
        for k in range(8):
            st = stage[:, 0:3 * DQ]
            B.dma(st, wqkvd[l, 128 * k:128 * (k + 1), :], ch="wq")
            B.copy(wqkv.chunk(k), st, tag="wconv")
        for k in range(NDQ):
            st = stage[:, 0:1024]
            B.dma(st, wod[l, 128 * k:128 * (k + 1), :], ch="wod")
            B.copy(wosb.chunk(k), st, tag="wconv")
        pqs = pool.f32()
        B.mm_group(pqs[:, 0:128], [(B.onesf[0:1, 0:128],
                                    qkrows[0:1, 128 * l:128 * (l + 1)])],
                   tag="qsb")
        B.copy(qsb, pqs[:, 0:128], tag="qsb")
        for w in range(nwin):
            for jt in range(ntw):
                t = w * ntw + jt
                B.copy(xb, xi.tile(t), tag="xb")
                bank = pool.bf16()
                for k in range(8):
                    B.transpose(bank[:, 128 * k:128 * (k + 1)],
                                xb[:, 128 * k:128 * (k + 1)], B.ident,
                                tag="xtr")
                if t > 0:
                    for k in range(4):
                        B.copy(htu[:, 128 * k:128 * k + 1],
                               bnd[:, k:k + 1], tag="sh")
                else:
                    B.memset(htu[:, 0:1], 0.0, tag="sh")
                    B.memset(htu[:, 128:129], 0.0, tag="sh")
                    B.memset(htu[:, 256:257], 0.0, tag="sh")
                    B.memset(htu[:, 384:385], 0.0, tag="sh")
                B.copy(rawT, bank[:, 0:1024], tag="xtrev")
                for k in range(4):
                    B.copy(htu[:, 128 * k + 1:128 * k + 128],
                           rawT[:, 128 * (4 + k):128 * (4 + k) + 127],
                           tag="sh")
                    B.copy(bnd[:, k:k + 1],
                           rawT[:, 128 * (4 + k) + 127:128 * (4 + k) + 128],
                           tag="sh")
                for n0 in range(0, 3 * DQ, 512):
                    nn = min(512, 3 * DQ - n0)
                    bankp = pool.f32()
                    pairs = []
                    for k in range(8):
                        lhs = (rawT[:, 128 * k:128 * (k + 1)] if k < 4
                               else htu[:, 128 * (k - 4):128 * (k - 3)])
                        pairs.append((lhs, wqkv.chunk(k, n0, n0 + nn)))
                    B.mm_group(bankp[:, 0:nn], pairs, tag="qkvmm")
                    lo, hi = n0, n0 + nn
                    if lo < 2 * DQ:
                        c = min(hi, 2 * DQ) - lo
                        B.copy(qkl[:, lo:lo + c], bankp[:, 0:c], tag="qkvev")
                    if hi > 2 * DQ:
                        c0 = max(lo, 2 * DQ)
                        B.copy(vbr[t % nts][:, c0 - 2 * DQ:hi - 2 * DQ],
                               bankp[:, c0 - lo:nn], tag="qkvev")
                l2norm_xpos(t, qkl[:, 0:DQ], qp[:, 0:DQ], 0, 64, 0)
                for g0 in range(NDQ):
                    bank = pool.bf16()
                    B.transpose(bank[:, 0:128],
                                qp[:, 128 * g0:128 * (g0 + 1)], B.ident,
                                tag="qtr")
                    B.copy(qTf.chunk(g0, jt * 128, (jt + 1) * 128),
                           bank[:, 0:128], tag="qtrev")
                l2norm_xpos(t, qkl[:, DQ:2 * DQ], qp[:, 0:DQ], 128, 192, 1)
                for g0 in range(NDQ):
                    bank = pool.bf16()
                    B.transpose(bank[:, 0:128],
                                qp[:, 128 * g0:128 * (g0 + 1)], B.ident,
                                tag="ktr")
                    B.copy(kTf.chunk(g0, (t % nts) * 128,
                                     (t % nts) * 128 + 128),
                           bank[:, 0:128], tag="ktrev")
            stiles = list(range(max(0, (w - 1) * ntw), (w + 1) * ntw))
            jk0 = nts - len(stiles)
            nst = len(stiles)
            for h in range(H):
                hc, base = h // 2, 64 * (h % 2)

                def qr(c0, c1):
                    return qTf.sb[base:base + 64, hc * win + c0:hc * win + c1]

                def kr(ti):
                    sl = (ti % nts) * 128
                    return kTf.sb[base:base + 64,
                                  hc * (nts * 128) + sl:
                                  hc * (nts * 128) + sl + 128]
                for jq in range(ntw):
                    nchunks = (nst * 128 + 511) // 512
                    pss = []
                    for r0 in range(0, nst, 4):
                        grp = stiles[r0:r0 + 4]
                        ps = pool.f32()
                        pss.append((ps, r0, len(grp)))
                        for gi, ti in enumerate(grp):
                            B.mm_group(ps[:, gi * 128:(gi + 1) * 128],
                                       [(qr(jq * 128, (jq + 1) * 128),
                                         kr(ti))], tag="s1")
                        mv = mask1(jq)[:, (jk0 + r0) * 128:
                                       (jk0 + r0 + len(grp)) * 128]
                        B.tt(ps[:, 0:len(grp) * 128],
                             ps[:, 0:len(grp) * 128], mv, ALU.add, tag="s1m")
                        if r0 == 0:
                            B.red(rm[:, 0:1], ps[:, 0:len(grp) * 128],
                                  ALU.max, negate=True, tag="s1r")
                        else:
                            B.red(rm[:, 1:2], ps[:, 0:len(grp) * 128],
                                  ALU.max, negate=True, tag="s1r")
                            B.tt(rm[:, 0:1], rm[:, 0:1], rm[:, 1:2],
                                 ALU.min, tag="s1r")
                    for pi, (ps, r0, ng) in enumerate(pss):
                        B.act(Etok[:, r0 * 128:(r0 + ng) * 128],
                              ps[:, 0:ng * 128], AF.Exp, bias=rm[:, 0:1],
                              accum_out=dprt[:, pi:pi + 1], tag="s2e")
                    if len(pss) > 1:
                        B.tt(dprt[:, 0:1], dprt[:, 0:1], dprt[:, 1:2],
                             ALU.add, tag="dn")
                    B.recip(dcl[:, jq:jq + 1], dprt[:, 0:1])
                    for r in range(nst):
                        bank = pool.bf16()
                        B.transpose(bank[:, 0:128],
                                    Etok[:, r * 128:(r + 1) * 128], B.ident,
                                    tag="etr")
                        B.copy(ET[:, r * win + 128 * jq:
                                  r * win + 128 * jq + 128],
                               bank[:, 0:128], tag="etr")
                for jq in range(ntw):
                    ops_ = pool.f32()
                    B.mm_group(ops_[:, 0:64],
                               [(ET[:, r * win + 128 * jq:
                                    r * win + 128 * jq + 128],
                                 vbr[ti % nts][:, 64 * h:64 * h + 64])
                                for r, ti in enumerate(stiles)], tag="pv")
                    B.ts(aotk, ops_[:, 0:64], dcl[:, jq:jq + 1], None,
                         ALU.mult, tag="aoev")
                    bank = pool.bf16()
                    B.transpose(bank[base:base + 64, 0:128], aotk, B.ident,
                                tag="aotr")
                    B.copy(outTw.sb[base:base + 64,
                                    hc * win + 128 * jq:
                                    hc * win + 128 * jq + 128],
                           bank[base:base + 64, 0:128], tag="aotr")
            for jt in range(ntw):
                t = w * ntw + jt
                for n0 in (0, 512):
                    ps = pool.f32()
                    B.mm_group(ps[:, 0:512],
                               [(outTw.chunk(k, jt * 128, (jt + 1) * 128),
                                 wosb.chunk(k, n0, n0 + 512))
                                for k in range(NDQ)], tag="womm")
                    B.tt(xi.tile(t, n0, n0 + 512), ps[:, 0:512],
                         xi.tile(t, n0, n0 + 512), ALU.add, tag="wores")

    def fc(stat, sl=None):
        if sl is None:
            return fcols[:, stat * NT:(stat + 1) * NT]
        return fcols[:, stat * NT + sl:stat * NT + sl + 1]

    def ffn_layer(l):
        B.dma(stage[0:1, 0:2048], lglbd[0:1, 2048 * l:2048 * (l + 1)],
              ch="lglb")
        for dst, off in ((lgb, 0), (lbb, 1024)):
            for n0 in (0, 512):
                pb = pool.f32()
                B.mm_group(pb[:, 0:512],
                           [(B.onesf[0:1, 0:128],
                             stage[0:1, off + n0:off + n0 + 512])],
                           tag="lgbc")
                B.copy(dst[:, n0:n0 + 512], pb[:, 0:512], tag="lgbc")
        B.dma(stage[0:1, 2048:3072], b2rd[0:1, 1024 * l:1024 * (l + 1)],
              ch="b2r")
        for k in range(8):
            pb = pool.f32()
            B.transpose(pb[:, 0:1],
                        stage[0:1, 2048 + 128 * k:2048 + 128 * (k + 1)],
                        B.cs(1.0)[0:1, 0:1], tag="b2c")
            B.copy(b2c[:, k:k + 1], pb[:, 0:1], tag="b2c")
        B.dma(b1ac, b1ad[l], ch="b1a")
        B.dma(b1gc, b1gd[l], ch="b1g")
        for t in range(NT):
            B.red(fc(0, t), xi.tile(t, 0, 512), ALU.add, tag="fst")
            B.act(sq[:, 0:512], xi.tile(t, 0, 512), AF.Square,
                  accum_out=fc(1, t), tag="fst")
            B.red(fc(2, t), xi.tile(t, 512, 1024), ALU.add, tag="fst")
            B.act(sq[:, 0:512], xi.tile(t, 512, 1024), AF.Square,
                  accum_out=fc(3, t), tag="fst")
        for cc, dc in ((2, 4), (3, 5)):
            B.dma(fc(dc)[0:1, 0:1], B.cs(0.0)[0:1, 0:1], ch="fsh",
                  batch=True)
            B.dma(fc(dc)[1:128, :], fc(cc)[0:127, :], ch="fsh", batch=True)
            B.dma(fc(dc)[0:1, 1:NT], fc(cc)[127:128, 0:NT - 1], ch="fsh", batch=True)
        B.tt(fc(6), fc(0), fc(4), ALU.add, tag="fst")
        B.ts(fc(6), fc(6), -1.0 / 1024, None, ALU.mult, tag="fst")
        B.tt(fc(7), fc(1), fc(5), ALU.add, tag="fst")
        B.ts(fc(7), fc(7), 1.0 / 1024, None, ALU.mult, tag="fst")
        B.tt(fc(10), fc(6), fc(6), ALU.mult, tag="fst")
        B.tt(fc(7), fc(7), fc(10), ALU.subtract, tag="fst")
        B.act(fc(7), fc(7), AF.Sqrt, bias=B.cs(1e-5), tag="fst")
        B.recip(fc(7), fc(7))
        for cc, dc in ((6, 8), (7, 9)):
            B.dma(fc(dc)[0:127, :], fc(cc)[1:128, :], ch="fsh", batch=True)
            B.dma(fc(dc)[127:128, 0:NT - 1], fc(cc)[0:1, 1:NT], ch="fsh",
                  batch=True)
            B.dma(fc(dc)[127:128, NT - 1:NT], B.cs(0.0)[0:1, 0:1], ch="fsh",
                  batch=True)
        pb1 = pool.f32()
        B.mm_group(pb1[:, 0:1], [(B.onesf[0:1, 0:128],
                                  fcols[0:1, 6 * NT:6 * NT + 1])], tag="edge")
        B.copy(edgec[:, 0:1], pb1[:, 0:1], tag="edge")
        pb2 = pool.f32()
        B.mm_group(pb2[:, 0:1], [(B.onesf[0:1, 0:128],
                                  fcols[0:1, 7 * NT:7 * NT + 1])], tag="edge")
        B.copy(edgec[:, 1:2], pb2[:, 0:1], tag="edge")
        B.tt(edgec[:, 2:3], edgec[:, 0:1], edgec[:, 1:2], ALU.mult,
             tag="edge")
        for t in range(NT):
            B.ts(sq[:, 0:512], xi.tile(t, 0, 512), fc(6, t), fc(7, t),
                 ALU.add, ALU.mult, tag="f2")
            B.tt(sq[:, 0:512], sq[:, 0:512], lgb[:, 0:512], ALU.mult,
                 tag="f2")
            B.tt(h2n[:, 0:512], sq[:, 0:512], lbb[:, 0:512], ALU.add,
                 tag="f2")
            B.ts(sq[:, 512:1024], xi.tile(t, 512, 1024), fc(8, t), fc(9, t),
                 ALU.add, ALU.mult, tag="f2")
            B.tt(sq[:, 512:1024], sq[:, 512:1024], lgb[:, 512:1024],
                 ALU.mult, tag="f2")
            B.tt(h2n[:, 512:1024], sq[:, 512:1024], lbb[:, 512:1024],
                 ALU.add, tag="f2")
            bank = pool.bf16()
            for k in range(8):
                B.transpose(bank[:, 128 * k:128 * (k + 1)],
                            h2n[:, 128 * k:128 * (k + 1)], B.ident,
                            tag="f2tr")
            for k in range(4):
                B.copy(h2nT.chunk(k, t * 128, (t + 1) * 128),
                       bank[:, 128 * k:128 * (k + 1)], tag="f2trev")
            for k in range(4, 8):
                ncc = 127 if t == NT - 1 else 128
                B.copy(h2nT.chunk(k, t * 128 + 1, t * 128 + 1 + ncc),
                       bank[:, 128 * k:128 * k + ncc], tag="f2trev")
        for k in range(4, 8):
            lgcolp = pool.f32()
            B.transpose(lgcolp[:, 0:1], lgb[0:1, 128 * k:128 * (k + 1)],
                        B.cs(1.0)[0:1, 0:1], tag="edge")
            lbcolp = pool.f32()
            B.transpose(lbcolp[:, 0:1], lbb[0:1, 128 * k:128 * (k + 1)],
                        B.cs(1.0)[0:1, 0:1], tag="edge")
            B.copy(edgec[:, 3:4], lgcolp[:, 0:1], tag="edge")
            B.copy(edgec[:, 4:5], lbcolp[:, 0:1], tag="edge")
            B.stt(h2nT.chunk(k, 0, 1), edgec[:, 3:4], edgec[:, 2:3],
                  edgec[:, 4:5], ALU.mult, ALU.add, tag="edge")
        for b in range(4):
            t0 = 512 * b
            for j in range(NKC):
                rj = min(128, HID - 128 * j)
                sj = j % 2
                st = stage[:, 0:2048]
                B.dma(st.rearrange("p (k n) -> p k n", k=8),
                      w1d[l, :, 256 * j:256 * (j + 1)].rearrange(
                          "(k p) n -> p k n", p=128), ch="w1")
                B.copy(w1b[sj].sb, st, tag="wconv")
                psa = mmB(B, pool, w1b[sj], h2nT, 0, t0, 512, tag="w1a")
                psg = mmB(B, pool, w1b[sj], h2nT, 1, t0, 512, tag="w1g")
                B.act(glscr[0:rj, 0:512], psg[0:rj, 0:512], AF.Gelu,
                      bias=b1gc[0:rj, j:j + 1], tag="gelu")
                B.stt(uT.chunk(j), psa[0:rj, 0:512], b1ac[0:rj, j:j + 1],
                      glscr[0:rj, 0:512], ALU.add, ALU.mult, tag="umul")
            for m in range(8):
                sm = m % 2
                st = stage[:, 0:NKC * 128]
                B.dma(st.rearrange("p (k n) -> p k n", k=NKC)[:, 0:21, :],
                      w2d[l, 0:2688, 128 * m:128 * (m + 1)].rearrange(
                          "(k p) n -> p k n", p=128), ch="w2a")
                B.dma(st[0:42, 21 * 128:21 * 128 + 128],
                      w2d[l, 2688:2730, 128 * m:128 * (m + 1)], ch="w2b")
                B.copy(w2b[sm].sb, st, tag="wconv")
                psy = pool.f32()
                pairs = [(w2b[sm].chunk(k), uT.chunk(k)) for k in range(NKC)]
                B.mm_group(psy[:, 0:512], pairs, tag="w2mm")
                B.ts(yTs[:, 0:512], psy[:, 0:512], b2c[:, m:m + 1], None,
                     ALU.add, tag="w2ev")
                for jt in range(4):
                    t = 4 * b + jt
                    bank = pool.bf16()
                    B.transpose(bank[:, 0:128],
                                yTs[:, 128 * jt:128 * (jt + 1)], B.ident,
                                tag="w2tr")
                    B.tt(xi.tile(t, 128 * m, 128 * (m + 1)),
                         bank[:, 0:128],
                         xi.tile(t, 128 * m, 128 * (m + 1)), ALU.add,
                         tag="w2res")

    for l in range(nlayers):
        attn_layer(l)
        ffn_layer(l)

    for t in range(NT):
        xt = stage[:, (t % 2) * 1024:(t % 2) * 1024 + 1024]
        rt = stage[:, 2048:3072]
        B.dma(xt, xind[t], ch=f"x{t % 2}")
        B.tt(rt, xt, xi.tile(t), ALU.add, tag="fres")
        B.dma(xoutd[t], rt, ch="xo", on="scalar")



def prep_l3_attn(inp):
    f32 = np.float32
    tri = np.triu(np.ones((128, 128), np.float32))
    return {
        "wqT": np.ascontiguousarray(np.asarray(inp["L3_attn_Wq"], f32).T),
        "wkT": np.ascontiguousarray(np.asarray(inp["L3_attn_Wk"], f32).T),
        "wvT": np.ascontiguousarray(np.asarray(inp["L3_attn_Wv"], f32).T),
        "woT": np.ascontiguousarray(np.asarray(inp["L3_attn_Wo"], f32).T),
        "rows": np.asarray(inp["L3_attn_bo"], f32).reshape(1, 1024),
        "trimask": tri.astype(BF16NP),
    }


def dram_l3a(nc, pfx=""):
    def dt_(n, shape, dt=F32):
        return nc.dram_tensor(pfx + n, shape, dt, kind="ExternalInput").ap()
    return dict(wqd=dt_("wqT", [1024, 1024]), wkd=dt_("wkT", [1024, 1024]),
                wvd=dt_("wvT", [1024, 1024]), wod=dt_("woT", [1024, 1024]),
                rowsd=dt_("rows", [1, 1024]),
                trid=dt_("trimask", [128, 128], BF16))


def body_l3a(B, pool, d, xin, xout):
    wqd, wkd, wvd, wod, rowsd, trid = (d["wqd"], d["wkd"], d["wvd"],
                                       d["wod"], d["rowsd"], d["trid"])
    A = B.arena(40960)
    xind = xin.rearrange("(t p) c -> t p c", p=128)
    xoutd = xout.rearrange("(t p) c -> t p c", p=128)
    stage = B.sb("stage", [128, 1024], F32)
    tri = B.sb("tri", [128, 128], BF16)
    B.dma(tri, trid, ch="tri", batch=True)
    lncols = B.sb("lncols", [128, 4 * NT], F32)
    escr = B.sb("escr", [128, 2 * 512], F32)
    S_sb = B.sb("S_sb", [128, 8 * 65], F32)
    S_bf = B.sb("S_bf", [128, 8 * 65], BF16)
    At = B.sb("At", [128, 128], BF16)
    dcol = B.sb("dcolx", [128, 2], F32)

    xn = TMT(B, "xn", T_SEQ, D_MODEL, BF16, ap=A.bf16(0, 16384))
    for t in range(NT):
        xt = stage[:, 0:1024]
        B.dma(xt, xind[t], ch="x0")
        pre_ln_tile(B, xt, xn.tile(t), lncols[:, 4 * t:4 * t + 4],
                    escr[:, 0:1024])
    xnT = FMT(B, "xnT", D_MODEL, T_SEQ, BF16, ap=A.bf16(8192, 16384))
    tm_to_fm(B, xn, xnT, pool)

    phiqT = FMT(B, None, D_MODEL, T_SEQ, BF16, ap=A.bf16(16384, 16384))
    phikT = FMT(B, None, D_MODEL, T_SEQ, BF16, ap=A.bf16(24576, 16384))
    vT_tmp = None
    phik_roll = B.sb("phik_roll", [128, 1024], BF16)
    v = TMT(B, None, T_SEQ, D_MODEL, BF16, ap=A.bf16(32768, 16384))
    wcb = [B.sb(f"wcb{i}", [128, 8 * 128], BF16) for i in range(2)]

    def proj_convB(wd_, dst_fm, elu):
        for m in range(8):
            sm = m % 2
            st = stage[:, 0:1024]
            B.dma(st.rearrange("p (k n) -> p k n", k=8),
                  wd_[:, 128 * m:128 * (m + 1)].rearrange(
                      "(k p) n -> p k n", p=128), ch=f"w{sm}")
            B.copy(wcb[sm], st, tag="wconv")
            for t0 in range(0, T_SEQ, 512):
                ps = pool.f32()
                B.mm_group(ps[:, 0:512],
                           [(wcb[sm][:, 128 * k:128 * (k + 1)],
                             xnT.chunk(k, t0, t0 + 512)) for k in range(8)],
                           tag="proj")
                if elu:
                    B.ts(escr[:, 0:512], ps[:, 0:512], 0.0, None, ALU.min,
                         tag="elu")
                    B.act(escr[:, 512:1024], escr[:, 0:512], AF.Exp,
                          tag="elu")
                    B.stt(dst_fm.chunk(m, t0, t0 + 512), ps[:, 0:512], 0.0,
                          escr[:, 512:1024], ALU.max, ALU.add, tag="elu")
                else:
                    B.copy(dst_fm.chunk(m, t0, t0 + 512), ps[:, 0:512],
                           tag="projev")
    proj_convB(wqd, phiqT, True)
    proj_convB(wkd, phikT, True)
    proj_convB(wvd, FMT(B, None, D_MODEL, T_SEQ, BF16,
                        ap=A.bf16(0, 16384)), False)
    vT = FMT(B, None, D_MODEL, T_SEQ, BF16, ap=A.bf16(0, 16384))
    for t in range(NT):
        bank = pool.bf16()
        for k in range(8):
            B.transpose(bank[:, 128 * k:128 * (k + 1)],
                        vT.chunk(k, t * 128, (t + 1) * 128), B.ident,
                        tag="vtr")
        B.copy(v.tile(t), bank[:, 0:1024], tag="vtrev")

    B.memset(S_sb, 0.0, tag="sinit")
    B.memset(S_bf, 0.0, tag="sinit")
    ao = TMT(B, None, T_SEQ, D_MODEL, BF16, ap=A.bf16(0, 16384))
    EPS = 1e-5
    for c in range(NT):
        bank = pool.bf16()
        for k in range(8):
            B.transpose(bank[:, 128 * k:128 * (k + 1)],
                        phikT.chunk(k, c * 128, (c + 1) * 128), B.ident,
                        tag="pktr")
        B.copy(phik_roll, bank[:, 0:1024], tag="pktrev")
        for h in range(16):
            hc, base = h // 2, 64 * (h % 2)
            qrows = phiqT.sb[base:base + 64,
                             hc * T_SEQ + c * 128:hc * T_SEQ + (c + 1) * 128]
            krows = phikT.sb[base:base + 64,
                             hc * T_SEQ + c * 128:hc * T_SEQ + (c + 1) * 128]
            Ssl = S_sb[base:base + 64, 65 * hc:65 * hc + 65]
            Sbl = S_bf[base:base + 64, 65 * hc:65 * hc + 65]
            pa = pool.f32()
            B.mm_group(pa[:, 0:128], [(krows, qrows)], tag="at")
            B.tt(At, pa[:, 0:128], tri, ALU.mult, tag="atm")
            pn = pool.f32()
            B.mm_group(pn[:, 0:64],
                       [(At, v.tile(c, 64 * h, 64 * h + 64)),
                        (qrows, Sbl[:, 0:64])], tag="num")
            B.mm_group(pn[:, 64:65],
                       [(At, B.onesb[:, 0:1]),
                        (qrows, Sbl[:, 64:65])], tag="den")
            B.ts(dcol[:, 0:1], pn[:, 64:65], EPS, None, ALU.add, tag="div")
            B.recip(dcol[:, 1:2], dcol[:, 0:1])
            B.ts(ao.tile(c, 64 * h, 64 * h + 64), pn[:, 0:64],
                 dcol[:, 1:2], None, ALU.mult, tag="aoev")
            pd = pool.f32()
            B.mm_group(pd[base:base + 64, 0:64],
                       [(phik_roll[:, 64 * h:64 * h + 64],
                         v.tile(c, 64 * h, 64 * h + 64))], tag="supd")
            B.mm_group(pd[base:base + 64, 64:65],
                       [(phik_roll[:, 64 * h:64 * h + 64],
                         B.onesb[:, 0:1])], tag="supd")
            B.tt(Ssl, Ssl, pd[base:base + 64, 0:65], ALU.add, tag="sadd")
            B.copy(Sbl, Ssl, tag="sbf")
    Wo = FMT(B, None, 1024, 1024, BF16, ap=A.bf16(16384, 8192))
    for k in range(8):
        st = stage[:, 0:1024]
        B.dma(st, wod[128 * k:128 * (k + 1), :], ch=f"w{k % 2}")
        B.copy(Wo.chunk(k), st, tag="wconv")
    rows = A.row(16384 + 4096, 1024)
    B.dma(rows, rowsd, ch="rows", batch=True)
    aoT = FMT(B, None, D_MODEL, T_SEQ, BF16, ap=A.bf16(8192, 16384))
    tm_to_fm(B, ao, aoT, pool)
    for t in range(NT):
        xt = stage[:, 0:1024]
        rt = escr[:, 0:1024]
        B.dma(xt, xind[t], ch="x0")
        for n0 in (0, 512):
            ps = mmA(B, pool, aoT, Wo, t, n0, 512,
                     bias_row=rows[0:1, 0:1024], tag="womm")
            B.tt(rt[:, n0:n0 + 512], ps, xt[:, n0:n0 + 512], ALU.add,
                 tag="resid")
        B.dma(xoutd[t], rt, ch=f"xo{t % 2}", on="scalar")



def prep_l2_attn(inp):
    f32 = np.float32
    wx = np.asarray(inp["L2_attn_Wx"], f32)
    wxc = np.concatenate([wx[h] for h in range(16)], axis=1)
    rows = np.zeros((1, 2048), f32)
    rows[0, 0:1024] = np.asarray(inp["L2_attn_bx"], f32).reshape(-1)
    rows[0, 1024:2048] = np.asarray(inp["L2_attn_bp"], f32)
    return {
        "wxc": wxc,
        "wp": np.asarray(inp["L2_attn_Wp"], f32),
        "rows": rows,
    }


def dram_l2a(nc, pfx=""):
    def dt_(n, shape, dt=F32):
        return nc.dram_tensor(pfx + n, shape, dt, kind="ExternalInput").ap()
    return dict(wxd=dt_("wxc", [1024, 1024]), wpd=dt_("wp", [1024, 1024]),
                rowsd=dt_("rows", [1, 2048]))


def body_l2a(B, pool, d, xin, xout):
    wxd, wpd, rowsd = d["wxd"], d["wpd"], d["rowsd"]
    A = B.arena(32768)
    xind = xin.rearrange("(t p) c -> t p c", p=128)
    xoutd = xout.rearrange("(t p) c -> t p c", p=128)
    stage = B.sb("stage", [128, 1024], F32)
    lncols = B.sb("lncols", [128, 4 * NT], F32)
    scr = B.sb("scr", [128, 1024], F32)
    bxrow = B.sb("bxrow", [1, 1024], F32)
    B.dma(bxrow, rowsd[0:1, 0:1024], ch="bx", batch=True)
    rall = B.sb("rall", [128, NT], F32)
    rrow = B.sb("rrow", [1, T_SEQ], BF16)
    rb = B.sb("rb", [128, T_SEQ], BF16)
    nrm = B.sb("nrm", [128, 16], F32)
    dtmp = B.sb("dtmp", [128, 4], F32)
    E2 = B.sb("E_sb", [128, 8 * T_SEQ], BF16)
    E1 = A.bf16(8192, 8 * T_SEQ)

    def Etile(a, c0, c1):
        e = E1 if a < 8 else E2
        a2 = a % 8
        return e[:, a2 * T_SEQ + c0:a2 * T_SEQ + c1]

    xn = TMT(B, "xn", T_SEQ, D_MODEL, BF16, ap=A.bf16(0, 16384))
    for t in range(NT):
        xt = stage[:, 0:1024]
        B.dma(xt, xind[t], ch="x0")
        pre_ln_tile(B, xt, xn.tile(t), lncols[:, 4 * t:4 * t + 4],
                    scr[:, 0:1024])
    xnT = FMT(B, "xnT", D_MODEL, T_SEQ, BF16, ap=A.bf16(8192, 16384))
    tm_to_fm(B, xn, xnT, pool)
    Wx = FMT(B, None, 1024, 1024, BF16, ap=A.bf16(0, 8192))
    for k in range(8):
        st = stage[:, 0:1024]
        B.dma(st, wxd[128 * k:128 * (k + 1), :], ch="w0")
        B.copy(Wx.chunk(k), st, tag="wconv")
    H = TMT(B, None, T_SEQ, D_MODEL, BF16, ap=A.bf16(16384, 16384))
    hn = B.sb("hnscr", [128, 1024], BF16)
    hnT = FMT(B, None, D_MODEL, T_SEQ, BF16, ap=A.bf16(24576, 16384))
    for t in range(NT):
        for n0 in (0, 512):
            ps = mmA(B, pool, xnT, Wx, t, n0, 512,
                     bias_row=bxrow[0:1, 0:1024], tag="hmm")
            B.copy(H.tile(t, n0, n0 + 512), ps, tag="hev")
        B.act(scr, H.tile(t), AF.Square, tag="hn")
        B.red(nrm, scr.rearrange("p (h d) -> p h d", h=16), ALU.add,
              tag="hn")
        B.act(nrm, nrm, AF.Sqrt, tag="hn")
        B.recip(nrm, nrm)
        B.tt(hn.rearrange("p (h d) -> p h d", h=16),
             H.tile(t).rearrange("p (h d) -> p h d", h=16),
             nrm[:, :].unsqueeze(2).broadcast_to([128, 16, 64]), ALU.mult,
             tag="hn")
        bank = pool.bf16()
        for k in range(8):
            B.transpose(bank[:, 128 * k:128 * (k + 1)],
                        hn[:, 128 * k:128 * (k + 1)], B.ident, tag="hntr")
        for k in range(8):
            B.copy(hnT.chunk(k, t * 128, (t + 1) * 128),
                   bank[:, 128 * k:128 * (k + 1)], tag="hntrev")
    OT = FMT(B, None, D_MODEL, T_SEQ, BF16, ap=A.bf16(0, 16384))
    for h in range(16):
        hc, base = h // 2, 64 * (h % 2)

        def hrows(t0, t1):
            return hnT.sb[base:base + 64, hc * T_SEQ + t0:hc * T_SEQ + t1]
        for a in range(NT):
            for b0 in range(0, T_SEQ, 512):
                ps = pool.f32()
                B.mm_group(ps[:, 0:512],
                           [(hrows(a * 128, (a + 1) * 128),
                             hrows(b0, b0 + 512))], tag="g")
                B.act(Etile(a, b0, b0 + 512), ps[:, 0:512], AF.Exp,
                      accum_out=dtmp[:, (b0 // 512):(b0 // 512) + 1],
                      tag="ge")
            B.tt(dtmp[:, 0:1], dtmp[:, 0:1], dtmp[:, 1:2], ALU.add,
                 tag="dn")
            B.tt(dtmp[:, 2:3], dtmp[:, 2:3], dtmp[:, 3:4], ALU.add,
                 tag="dn")
            B.tt(dtmp[:, 0:1], dtmp[:, 0:1], dtmp[:, 2:3], ALU.add,
                 tag="dn")
            B.recip(rall[:, a:a + 1], dtmp[:, 0:1])
        for a in range(NT):
            B.copy(scr[:, 0:1].bitcast(BF16)[:, 0:1], rall[:, a:a + 1],
                   tag="rbc")
            bank = pool.bf16()
            B.transpose(bank[0:1, 0:128], scr[:, 0:1].bitcast(BF16)[:, 0:1],
                        B.ident, tag="rtr")
            B.copy(rrow[0:1, a * 128:(a + 1) * 128], bank[0:1, 0:128],
                   tag="rtr")
        for q0 in range(0, T_SEQ, 512):
            pb = pool.f32()
            B.mm_group(pb[:, 0:512],
                       [(B.onesb[0:1, 0:128], rrow[0:1, q0:q0 + 512])],
                       tag="rbb")
            B.copy(rb[:, q0:q0 + 512], pb[:, 0:512], tag="rbb")
        for q0 in range(0, T_SEQ, 512):
            ps = pool.f32()
            B.mm_group(ps[0:64, 0:512],
                       [(H.tile(k, 64 * h, 64 * h + 64),
                         Etile(k, q0, q0 + 512)) for k in range(NT)],
                       tag="pv")
            B.tt(OT.sb[base:base + 64,
                       hc * T_SEQ + q0:hc * T_SEQ + q0 + 512],
                 ps[0:64, 0:512], rb[base:base + 64, q0:q0 + 512],
                 ALU.mult, tag="pvev")
    Wp = FMT(B, None, 1024, 1024, BF16, ap=A.bf16(8192, 8192))
    for k in range(8):
        st = stage[:, 0:1024]
        B.dma(st, wpd[128 * k:128 * (k + 1), :], ch="w0")
        B.copy(Wp.chunk(k), st, tag="wconv")
    bprow = A.row(12288, 1024)
    B.dma(bprow, rowsd[0:1, 1024:2048], ch="bp", batch=True)
    res = B.sb("res", [128, 1024], F32)
    for t in range(NT):
        xt = stage[:, 0:1024]
        rt = res[:, 0:1024]
        B.dma(xt, xind[t], ch="x0")
        for n0 in (0, 512):
            ps = mmA(B, pool, OT, Wp, t, n0, 512,
                     bias_row=bprow[0:1, 0:1024], tag="wpmm")
            B.tt(rt[:, n0:n0 + 512], ps, xt[:, n0:n0 + 512], ALU.add,
                 tag="resid")
        B.dma(xoutd[t], rt, ch="xo", on="scalar")



def prep_l2_ffn(inp):
    f32 = np.float32
    w1 = np.asarray(inp["L2_ffn_w1"], f32)
    w2 = np.asarray(inp["L2_ffn_w2"], f32)
    w3 = np.asarray(inp["L2_ffn_w3"], f32)
    ws1 = np.asarray(inp["L2_ffn_ws1"], f32)
    ws2 = np.asarray(inp["L2_ffn_ws2"], f32)
    ws3 = np.asarray(inp["L2_ffn_ws3"], f32)
    w1r = np.zeros((17, 1024, 768), f32)
    w3r = np.zeros((17, 1024, 768), f32)
    w2r = np.zeros((17, 768, 1024), f32)
    for e in range(16):
        w1r[e] = w1[e].T
        w3r[e] = w3[e].T
        w2r[e] = w2[e].T
    w1r[16] = ws1.T
    w3r[16] = ws3.T
    w2r[16] = ws2.T
    return {
        "w1r": w1r, "w3r": w3r, "w2r": w2r,
        "gwT": np.ascontiguousarray(np.asarray(inp["L2_ffn_gate_w"],
                                               f32).T),
        "gbias": np.asarray(inp["L2_ffn_gate_bias"], f32).reshape(1, 16),
    }


def dram_l2f(nc, pfx=""):
    def dt_(n, shape, dt=F32):
        return nc.dram_tensor(pfx + n, shape, dt, kind="ExternalInput").ap()
    return dict(w1d=dt_("w1r", [17, 1024, 768]),
                w3d=dt_("w3r", [17, 1024, 768]),
                w2d=dt_("w2r", [17, 768, 1024]),
                gwd=dt_("gwT", [1024, 16]), gbd=dt_("gbias", [1, 16]))


def body_l2f(B, pool, d, xin, xout):
    w1d, w3d, w2d, gwd, gbd = (d["w1d"], d["w3d"], d["w2d"], d["gwd"],
                               d["gbd"])
    A = B.arena(32768)
    xind = xin.rearrange("(t p) c -> t p c", p=128)
    xoutd = xout.rearrange("(t p) c -> t p c", p=128)
    stage = B.sb("stage", [128, 2048], F32)
    lncols = B.sb("lncols", [128, 4 * NT], F32)
    scr = B.sb("scr", [128, 1024], F32)
    rt16 = B.sb("rt16", [128, 96], F32)
    ccols = B.sb("ccols", [128, 16 * NT], F32)
    gbb = B.sb("gbb", [128, 16], F32)
    gwsb = FMT(B, "gwsb", 1024, 16, BF16)
    gwst = B.sb("gwst", [128, 128], F32)
    B.dma(gwst, gwd.rearrange("(k p) e -> p k e", p=128)
          .rearrange("p k e -> p (k e)") if False else gwst, ch="nul",
          batch=True) if False else None
    B.dma(gwst.rearrange("p (k e) -> p k e", k=8),
          gwd.rearrange("(k p) e -> p k e", p=128), ch="gw", batch=True)
    B.copy(gwsb.sb, gwst, tag="wconv")
    gbs = B.sb("gbs", [1, 16], F32)
    B.dma(gbs, gbd, ch="gb", batch=True)
    gb16 = B.sb("gb16", [1, 16], BF16)
    B.copy(gb16, gbs, tag="gb")
    pb = pool.f32()
    B.mm_group(pb[:, 0:16], [(B.onesb[0:1, 0:128], gb16[0:1, :])], tag="gbb")
    B.copy(gbb, pb[:, 0:16], tag="gbb")

    xn = TMT(B, "xn", T_SEQ, D_MODEL, BF16, ap=A.bf16(0, 16384))
    for t in range(NT):
        xt = stage[:, 0:1024]
        B.dma(xt, xind[t], ch="x0")
        pre_ln_tile(B, xt, xn.tile(t), lncols[:, 4 * t:4 * t + 4],
                    scr[:, 0:1024])
    xnT = FMT(B, "xnT", D_MODEL, T_SEQ, BF16, ap=A.bf16(8192, 16384))
    tm_to_fm(B, xn, xnT, pool)

    for t in range(NT):
        sc = rt16[:, 0:16]
        sb_ = rt16[:, 16:32]
        m1 = rt16[:, 32:36]
        mm_ = rt16[:, 36:52]
        gsum = rt16[:, 52:56]
        gcol = rt16[:, 56:57]
        gcol2 = rt16[:, 57:58]
        gmask = rt16[:, 58:62]
        ml = rt16[:, 62:78]
        sel = rt16[:, 78:94]
        fbi = rt16[:, 36:52]
        ps = mmA(B, pool, xnT, gwsb, t, 0, 16, tag="logit")
        B.act(sc, ps, AF.Sigmoid, tag="rt")
        B.tt(sb_, sc, gbb, ALU.add, tag="rt")
        gv = sb_.rearrange("p (g e) -> p g e", g=4)
        B.red(m1, gv, ALU.max, tag="rt")
        B.tt(mm_.rearrange("p (g e) -> p g e", g=4), gv,
             m1[:, :].unsqueeze(2).broadcast_to([128, 4, 4]), ALU.is_ge,
             tag="rt")
        B.stt(mm_, mm_, -2e9, sb_, ALU.mult, ALU.add, tag="rt")
        B.red(gsum, mm_.rearrange("p (g e) -> p g e", g=4), ALU.max,
              tag="rt")
        B.tt(gsum, gsum, m1, ALU.add, tag="rt")
        B.red(gcol, gsum, ALU.max, tag="rt")
        B.ts(gmask, gsum, gcol, None, ALU.is_ge, tag="rt")
        B.stt(rt16[:, 36:40], gmask, -2e9, gsum, ALU.mult, ALU.add, tag="rt")
        B.red(gcol2, rt16[:, 36:40], ALU.max, tag="rt")
        B.ts(rt16[:, 40:44], rt16[:, 36:40], gcol2, None, ALU.is_ge,
             tag="rt")
        B.tt(gmask, gmask, rt16[:, 40:44], ALU.add, tag="rt")
        emv = gmask.unsqueeze(2).broadcast_to([128, 4, 4])
        B.tt(ml.rearrange("p (g e) -> p g e", g=4), sb_.rearrange(
            "p (g e) -> p g e", g=4), emv, ALU.mult, tag="rt")
        B.ts(rt16[:, 36:40], gmask, -1.0, 1e30, ALU.add, ALU.mult, tag="rt")
        B.tt(ml.rearrange("p (g e) -> p g e", g=4),
             ml.rearrange("p (g e) -> p g e", g=4),
             rt16[:, 36:40].unsqueeze(2).broadcast_to([128, 4, 4]),
             ALU.add, tag="rt")
        B.memset(sel, 0.0, tag="rt")
        for it in range(4):
            B.red(gcol, ml, ALU.max, tag="rt")
            B.ts(fbi, ml, gcol, None, ALU.is_ge, tag="rt")
            B.tt(sel, sel, fbi, ALU.add, tag="rt")
            B.stt(ml, fbi, -1e31, ml, ALU.mult, ALU.add, tag="rt")
        B.tt(sel, sel, sc, ALU.mult, tag="rt")
        B.red(gcol, sel, ALU.add, tag="rt")
        B.ts(gcol, gcol, 1e-20, None, ALU.add, tag="rt")
        B.recip(gcol2, gcol)
        B.ts(ccols[:, 16 * t:16 * t + 16], sel, gcol2, 2.5, ALU.mult,
             ALU.mult, tag="rt")
    facc = TMT(B, None, T_SEQ, D_MODEL, F32, ap=A.f32(16384, 16384))
    hactT = FMT(B, None, 768, T_SEQ, BF16, ap=A.bf16(0, 12288))
    w2sb = FMT(B, "w2sb", 768, 1024, BF16)
    wcb = [FMT(B, f"wcb{i}", 1024, 128, BF16) for i in range(4)]
    gl = B.sb("gl", [128, 512], BF16)
    for e in range(17):
        for m in range(6):
            s1, s3 = 2 * (m % 2), 2 * (m % 2) + 1
            st1 = stage[:, 0:1024]
            st3 = stage[:, 1024:2048]
            B.dma(st1.rearrange("p (k n) -> p k n", k=8),
                  w1d[e, :, 128 * m:128 * (m + 1)].rearrange(
                      "(k p) n -> p k n", p=128), ch="w1")
            B.copy(wcb[s1].sb, st1, tag="wconv")
            B.dma(st3.rearrange("p (k n) -> p k n", k=8),
                  w3d[e, :, 128 * m:128 * (m + 1)].rearrange(
                      "(k p) n -> p k n", p=128), ch="w3")
            B.copy(wcb[s3].sb, st3, tag="wconv")
            for t0 in range(0, T_SEQ, 512):
                pg = pool.f32()
                B.mm_group(pg[:, 0:512],
                           [(wcb[s1].chunk(k), xnT.chunk(k, t0, t0 + 512))
                            for k in range(8)], tag="w1mm")
                pu = pool.f32()
                B.mm_group(pu[:, 0:512],
                           [(wcb[s3].chunk(k), xnT.chunk(k, t0, t0 + 512))
                            for k in range(8)], tag="w3mm")
                B.act(gl, pg[:, 0:512], AF.Silu, tag="silu")
                B.tt(hactT.chunk(m, t0, t0 + 512), pu[:, 0:512], gl,
                     ALU.mult, tag="hact")
        for k in range(6):
            st = stage[:, 0:1024]
            B.dma(st, w2d[e, 128 * k:128 * (k + 1), :], ch="w2")
            B.copy(w2sb.chunk(k), st, tag="wconv")
        for t in range(NT):
            cc = ccols[:, 16 * t + e:16 * t + e + 1] if e < 16 else None
            for n0 in (0, 512):
                ps = pool.f32()
                B.mm_group(ps[:, 0:512],
                           [(hactT.chunk(k, t * 128, (t + 1) * 128),
                             w2sb.chunk(k, n0, n0 + 512))
                            for k in range(6)], tag="w2mm")
                ft = facc.tile(t, n0, n0 + 512)
                if e == 0:
                    B.ts(ft, ps[:, 0:512], cc, None, ALU.mult, tag="facc")
                elif e < 16:
                    B.stt(ft, ps[:, 0:512], cc, ft, ALU.mult, ALU.add,
                          tag="facc")
                else:
                    B.tt(ft, ps[:, 0:512], ft, ALU.add, tag="facc")
    res = B.sb("res", [128, 1024], F32)
    for t in range(NT):
        xt = stage[:, 0:1024]
        B.dma(xt, xind[t], ch="x0")
        B.tt(res, facc.tile(t), xt, ALU.add, tag="resid")
        B.dma(xoutd[t], res, ch="xo", on="scalar")



GA_COLS = 52300


def _standalone(nc, dram_fn, body_fn):
    d = dram_fn(nc)
    xin = nc.dram_tensor("xin", [T_SEQ, D_MODEL], F32,
                         kind="ExternalInput").ap()
    xout = nc.dram_tensor("xout", [T_SEQ, D_MODEL], F32,
                          kind="ExternalOutput").ap()
    B = Builder(nc)
    pool = PsumPool(B, n_f32=4)
    B.init_ga(GA_COLS)
    B.block_begin("blk")
    body_fn(B, pool, d, xin, xout)
    B.finalize()
    return B


def build_l0_attn(nc):
    return _standalone(nc, dram_l0a, body_l0a)


def build_l0_ffn(nc):
    return _standalone(nc, dram_l0f, body_l0f)


def build_l1_attn(nc):
    return _standalone(nc, dram_l1a, body_l1a)


def build_local_stack(nc, nlayers, heads, win):
    return _standalone(
        nc, lambda n_: dram_local(n_, nlayers, heads, win),
        lambda B, pool, d, xi_, xo_: body_local(B, pool, d, xi_, xo_,
                                                nlayers, heads, win))


def build_l3_attn(nc):
    return _standalone(nc, dram_l3a, body_l3a)


def build_l2_attn(nc):
    return _standalone(nc, dram_l2a, body_l2a)


def build_l2_ffn(nc):
    return _standalone(nc, dram_l2f, body_l2f)


_SPECS = [
    ("l0a", dram_l0a, body_l0a, prep_l0_attn),
    ("l0f", dram_l0f, body_l0f, prep_l0_ffn),
    ("l1a", dram_l1a, body_l1a, prep_l1_attn),
    ("l1f", lambda n_, pfx="": dram_local(n_, 2, 8, 512, pfx),
     lambda B, pool, d, xi_, xo_: body_local(B, pool, d, xi_, xo_, 2, 8,
                                             512),
     lambda i: prep_local_stack("L1", i, 2, 512)),
    ("l2a", dram_l2a, body_l2a, prep_l2_attn),
    ("l2f", dram_l2f, body_l2f, prep_l2_ffn),
    ("l3a", dram_l3a, body_l3a, prep_l3_attn),
    ("l3f", lambda n_, pfx="": dram_local(n_, 6, 16, 256, pfx),
     lambda B, pool, d, xi_, xo_: body_local(B, pool, d, xi_, xo_, 6, 16,
                                             256),
     lambda i: prep_local_stack("L3", i, 6, 256)),
]


def build_fused(nc):
    B = Builder(nc)
    pool = PsumPool(B, n_f32=4)
    B.init_ga(GA_COLS)
    xin = nc.dram_tensor("xin", [T_SEQ, D_MODEL], F32,
                         kind="ExternalInput").ap()
    xout = nc.dram_tensor("xout", [T_SEQ, D_MODEL], F32,
                          kind="ExternalOutput").ap()
    xm = [nc.dram_tensor(f"xmid{i}", [T_SEQ, D_MODEL], F32,
                         kind="Internal").ap() for i in range(2)]
    chain = [xin] + [xm[i % 2] for i in range(7)] + [xout]
    for i, spec in enumerate(_SPECS):
        pfx, dfn, bfn = spec[0], spec[1], spec[2]
        d = dfn(nc, pfx + "_")
        B.block_begin(pfx)
        bfn(B, pool, d, chain[i], chain[i + 1])
        if i < len(_SPECS) - 1:
            B.barrier()
    B.finalize()
    return B



_CACHE = {}


def kernel(**inputs):
    x = np.ascontiguousarray(np.asarray(inputs["L0_attn_x"], np.float32))
    n = x.shape[0]
    if "fused" not in _CACHE:
        nc = bass.Bass("TRN2", target_bir_lowering=False)
        Bd = build_fused(nc)
        _CACHE["fused"] = (nc, Bd)
    nc, Bd = _CACHE["fused"]
    consts = Bd.host_consts()
    wmap = dict(consts)
    for spec in _SPECS:
        w = spec[3](inputs)
        for k, v in w.items():
            wmap[spec[0] + "_" + k] = v
    in_maps = [dict(xin=np.ascontiguousarray(x[i]), **wmap)
               for i in range(n)]
    res = run_bass_kernel_spmd(nc, in_maps, core_ids=list(range(n)))
    return np.stack([np.asarray(r["xout"]) for r in res.results])


# version 04 (final)
# speedup vs baseline: 1.4547x; 1.2046x over previous
import math
import numpy as np
import ml_dtypes
import concourse.bass as bass
import concourse.mybir as mybir
from concourse.bass_utils import run_bass_kernel_spmd
from contextlib import ExitStack

F32 = mybir.dt.float32
I32 = mybir.dt.int32
BF16 = mybir.dt.bfloat16
AF = mybir.ActivationFunctionType
ALU = mybir.AluOpType
AX = mybir.AxisListType
BF16NP = ml_dtypes.bfloat16

ENG_NAMES = ("tensor", "vector", "scalar", "gpsimd", "sync")
REDUCE_WAITS = True
SCHEDULE = True



class _Op:
    __slots__ = ("engine", "emit", "reads", "writes", "dma_ch", "gid",
                 "inc_marked", "inc_index", "edges", "tag", "eng_pos", "est",
                 "rec")

    def __init__(self, engine, emit, reads, writes, dma_ch, gid, tag):
        self.engine = engine
        self.emit = emit
        self.reads = reads
        self.writes = writes
        self.dma_ch = dma_ch
        self.gid = gid
        self.inc_marked = False
        self.inc_index = None
        self.edges = []
        self.tag = tag
        self.eng_pos = None
        self.est = None
        self.rec = gid


class Region:
    __slots__ = ("name", "p0", "p1", "b0", "b1")

    def __init__(self, name, p0, p1, b0, b1):
        self.name, self.p0, self.p1, self.b0, self.b1 = name, p0, p1, b0, b1

    def overlaps(self, o):
        return (self.name == o.name and self.p0 < o.p1 and o.p0 < self.p1
                and self.b0 < o.b1 and o.b0 < self.b1)

    def covers(self, o):
        return (self.name == o.name and self.p0 <= o.p0 and o.p1 <= self.p1
                and self.b0 <= o.b0 and o.b1 <= self.b1)


PSUM_BANK_B = 2048


def _ap_region(ap):
    t = ap.tensor
    reg = _ap_region_raw(ap)
    if type(t).__name__ == "PSumTensorHandle":
        reg.b0 = (reg.b0 // PSUM_BANK_B) * PSUM_BANK_B
        reg.b1 = ((reg.b1 + PSUM_BANK_B - 1) // PSUM_BANK_B) * PSUM_BANK_B
        reg.p0, reg.p1 = 0, 256
    return reg


def _ap_region_raw(ap):
    t = ap.tensor
    name = t.name
    dts = mybir.dt.size(ap.dtype)
    shape = list(t.shape)
    is_sb = hasattr(t, "base_partition")
    dims = [list(d) for d in ap.ap]
    off_b = ap.offset * dts
    if not is_sb:
        lo = hi = off_b
        for step, count in dims:
            sb = step * dts
            if count <= 0:
                count = 1
            if sb >= 0:
                hi += sb * (count - 1)
            else:
                lo += sb * (count - 1)
        return Region(name, 0, 1, lo, hi + dts)
    tdts = mybir.dt.size(t.dtype)
    row_b = tdts
    for s in shape[1:]:
        row_b *= s
    base_p = getattr(t, "base_partition", 0) or 0
    if dims and dims[0][0] * dts == row_b and dims[0][0] > 0:
        p0 = off_b // row_b
        c0 = off_b % row_b
        lo = hi = c0
        for step, count in dims[1:]:
            sb = step * dts
            if count <= 0:
                count = 1
            if sb >= 0:
                hi += sb * (count - 1)
            else:
                lo += sb * (count - 1)
        if lo < 0 or hi + dts > row_b:
            return Region(name, 0, 256, 0, row_b)
        return Region(name, base_p + p0, base_p + p0 + dims[0][1],
                      lo, hi + dts)
    lo = hi = off_b % row_b
    p0 = off_b // row_b
    ok = True
    for step, count in dims:
        sb = step * dts
        if count <= 0:
            count = 1
        if sb >= 0:
            hi += sb * (count - 1)
        else:
            lo += sb * (count - 1)
    if lo >= 0 and hi + dts <= row_b:
        return Region(name, base_p + p0, base_p + p0 + 1, lo, hi + dts)
    return Region(name, 0, 256, 0, row_b)


def _aap(x):
    if x is None or isinstance(x, (int, float, bass.AP)):
        return x
    return x[:]


CMAX = 64


class Builder:
    def __init__(self, nc, paranoid=False, consts=True):
        self.nc = nc
        self.prefix = ""
        self.ga = None
        self.ops = []
        self.es = ExitStack()
        self.paranoid = paranoid
        self._uid = 0
        self._dma_counts = {}
        self._batch_chs = set()
        self._sb_names = set()
        self._const_vals = []
        self._const_idx = {}
        if consts:
            cd = nc.dram_tensor("c_scalars", [128, CMAX], F32,
                                kind="ExternalInput").ap()
            idd = nc.dram_tensor("c_ident", [128, 128], BF16,
                                 kind="ExternalInput").ap()
            onb = nc.dram_tensor("c_onesb", [128, 512], BF16,
                                 kind="ExternalInput").ap()
            onf = nc.dram_tensor("c_onesf", [128, 512], F32,
                                 kind="ExternalInput").ap()
            self.csb = self.sb("csb", [128, CMAX], F32)
            self.ident = self.sb("ident_sb", [128, 128], BF16)
            self.onesb = self.sb("onesb_sb", [128, 512], BF16)
            self.onesf = self.sb("onesf_sb", [128, 512], F32)
            self.dma(self.csb, cd, ch="consts", batch=True, tag="consts")
            self.dma(self.ident, idd, ch="consts", batch=True, tag="consts")
            self.dma(self.onesb, onb, ch="consts", batch=True, tag="consts")
            self.dma(self.onesf, onf, ch="consts", batch=True, tag="consts")

    def cs(self, value):
        v = float(value)
        if v not in self._const_idx:
            assert len(self._const_vals) < CMAX
            self._const_idx[v] = len(self._const_vals)
            self._const_vals.append(v)
        k = self._const_idx[v]
        return self.csb[:, k:k + 1]

    def host_consts(self):
        c = np.zeros((128, CMAX), np.float32)
        for i, v in enumerate(self._const_vals):
            c[:, i] = v
        return {
            "c_scalars": c,
            "c_ident": np.eye(128, dtype=BF16NP),
            "c_onesb": np.ones((128, 512), BF16NP),
            "c_onesf": np.ones((128, 512), np.float32),
        }

    def sb(self, name, shape, dtype):
        if self.ga is not None:
            return self.scratch(shape, dtype)
        name = self.prefix + name
        assert name not in self._sb_names, name
        self._sb_names.add(name)
        return self.es.enter_context(self.nc.sbuf_tensor(name, shape, dtype))

    def arena(self, n):
        assert self.ga is not None
        self._scr_floor = max(self._scr_floor, n)
        assert self._scr_floor <= self._scr_top
        return ArenaView(self.ga, 0)

    def init_ga(self, f32_cols):
        self.ga = Arena(self, "GA", f32_cols)
        self._scr_top = f32_cols

    def block_begin(self, pfx, floor=0):
        self.prefix = pfx + "_"
        self._scr_top = self.ga.cols
        self._scr_floor = floor

    def scratch(self, shape, dtype):
        p, n = shape
        fcols = (n * mybir.dt.size(dtype) + 3) // 4
        self._scr_top -= fcols
        assert self._scr_top >= self._scr_floor, \
            ("scratch underflow into arena", self._scr_top, self._scr_floor)
        c0 = self._scr_top
        if dtype == F32:
            ap = self.ga.f32(c0, n)
        elif dtype == BF16:
            ap = self.ga.bf16(c0, n)
        else:
            ap = self.ga.t[:, c0:c0 + fcols].bitcast(dtype)[:, 0:n]
        if p < 128:
            ap = ap[0:p, :]
        return ap

    def barrier(self):
        grp = len(self.ops)
        for e in ENG_NAMES:
            def emit(eng):
                return eng.nop()
            o = _Op(e, emit, [], [], None, len(self.ops), "BARR")
            o.inc_index = grp
            self.ops.append(o)

    def ps(self, name, shape, dtype=F32):
        return self.es.enter_context(self.nc.psum_tensor(name, shape, dtype))

    def uid(self):
        self._uid += 1
        return self._uid

    def op(self, engine, emit, reads, writes, dma_ch=None, tag="", est=None):
        r = [x if isinstance(x, Region) else _ap_region(x) for x in reads]
        w = [x if isinstance(x, Region) else _ap_region(x) for x in writes]
        o = _Op(engine, emit, r, w, dma_ch, len(self.ops), tag)
        o.est = est
        if dma_ch is not None:
            self._dma_counts[dma_ch] = self._dma_counts.get(dma_ch, 0) + 1
            o.inc_index = self._dma_counts[dma_ch]
        self.ops.append(o)
        return o

    def dma(self, out, in_, ch="ld", on="sync", batch=False, tag="dma"):
        out, in_ = _aap(out), _aap(in_)
        if batch:
            self._batch_chs.add(ch)

        def emit(eng):
            return eng.dma_start(out=out, in_=in_)
        return self.op(on, emit, [in_], [out], dma_ch=ch, tag=tag)

    def mm_group(self, psum_ap, pairs, start=True, stop=True, tag="mm"):
        psum_ap = _aap(psum_ap)
        pairs = [(_aap(l), _aap(r)) for (l, r) in pairs]
        n = len(pairs)

        def emit(eng):
            first = last = None
            for i, (l, r) in enumerate(pairs):
                last = eng.matmul(psum_ap, l, r,
                                  start=(start and i == 0),
                                  stop=(stop and i == n - 1))
                if first is None:
                    first = last
            return (first, last)
        reads = []
        est = 0.0
        for l, r in pairs:
            reads.append(l)
            reads.append(r)
            est += 60.0 + 0.42 * r.shape[-1]
        return self.op("tensor", emit, reads, [psum_ap], tag=tag, est=est)

    def transpose(self, psum_ap, in_ap, ident_ap, tag="tr"):
        psum_ap, in_ap, ident_ap = _aap(psum_ap), _aap(in_ap), _aap(ident_ap)
        def emit(eng):
            return eng.transpose(psum_ap, in_ap, ident_ap)
        return self.op("tensor", emit, [in_ap, ident_ap], [psum_ap], tag=tag,
                       est=115.0)

    def act(self, out, in_, func, bias=0.0, scale=1.0, accum_out=None,
            tag="act"):
        out, in_, bias, scale, accum_out = (_aap(out), _aap(in_), _aap(bias),
                                            _aap(scale), _aap(accum_out))
        extra_r = [b for b in (bias, scale) if isinstance(b, bass.AP)]
        outs = [out] + ([accum_out] if accum_out is not None else [])

        def emit(eng):
            return eng.activation(out=out, in_=in_, func=func, bias=bias,
                                  scale=scale, accum_out=accum_out)
        return self.op("scalar", emit, [in_] + extra_r, outs, tag=tag)

    def tt(self, out, a, b, alu, engine="vector", tag="tt"):
        out, a, b = _aap(out), _aap(a), _aap(b)
        def emit(eng):
            return eng.tensor_tensor(out, a, b, alu)
        return self.op(engine, emit, [a, b], [out], tag=tag)

    def ts(self, out, in_, s1, s2, op0, op1=ALU.bypass, engine="vector",
           accum_out=None, tag="ts"):
        out, in_, s1, s2, accum_out = (_aap(out), _aap(in_), _aap(s1),
                                       _aap(s2), _aap(accum_out))
        extra = [s for s in (s1, s2) if isinstance(s, bass.AP)]
        outs = [out] + ([accum_out] if accum_out is not None else [])

        def emit(eng):
            return eng.tensor_scalar(out, in_, s1, s2, op0, op1,
                                     accum_out=accum_out)
        return self.op(engine, emit, [in_] + extra, outs, tag=tag)

    def stt(self, out, in0, scalar, in1, op0, op1, engine="vector", tag="stt"):
        out, in0, scalar, in1 = _aap(out), _aap(in0), _aap(scalar), _aap(in1)
        extra = [scalar] if isinstance(scalar, bass.AP) else []

        def emit(eng):
            return eng.scalar_tensor_tensor(out, in0, scalar, in1, op0, op1)
        return self.op(engine, emit, [in0, in1] + extra, [out], tag=tag)

    def red(self, out, in_, alu, negate=False, engine="vector", tag="red"):
        out, in_ = _aap(out), _aap(in_)
        def emit(eng):
            return eng.tensor_reduce(out, in_, AX.X, alu, negate=negate)
        return self.op(engine, emit, [in_], [out], tag=tag)

    def copy(self, out, in_, engine="vector", tag="cp"):
        out, in_ = _aap(out), _aap(in_)
        def emit(eng):
            return eng.tensor_copy(out, in_)
        return self.op(engine, emit, [in_], [out], tag=tag)

    def memset(self, out, val, engine="vector", tag="memset"):
        out = _aap(out)
        def emit(eng):
            return eng.memset(out, val)
        return self.op(engine, emit, [], [out], tag=tag)

    def recip(self, out, in_, tag="recip"):
        out, in_ = _aap(out), _aap(in_)
        def emit(eng):
            return eng.reciprocal(out, in_)
        return self.op("vector", emit, [in_], [out], tag=tag)

    def select(self, out, mask, on_true, on_false, tag="sel"):
        out, mask, on_true, on_false = (_aap(out), _aap(mask), _aap(on_true),
                                        _aap(on_false))
        def emit(eng):
            return eng.select(out, mask, on_true, on_false)
        return self.op("vector", emit, [mask, on_true, on_false], [out],
                       tag=tag)

    def raw(self, engine, emit, reads, writes, tag="raw"):
        reads = [_aap(r) for r in reads]
        writes = [_aap(w) for w in writes]
        return self.op(engine, emit, reads, writes, tag=tag)

    def _compute_edges(self):
        last_writes = {}
        reads_since = {}
        cur_barrier = []
        new_barrier = []
        barr_deps = None
        for o in self.ops:
            if o.tag == "BARR":
                if barr_deps is None:
                    barr_deps = set(cur_barrier)
                    for tbl in (last_writes, reads_since):
                        for lst in tbl.values():
                            for _, p in lst:
                                barr_deps.add(p)
                o.edges = [p for p in barr_deps if p is not o]
                new_barrier.append(o)
                continue
            if new_barrier:
                cur_barrier = new_barrier
                new_barrier = []
                barr_deps = None
                last_writes.clear()
                reads_since.clear()
            deps = set(cur_barrier)
            for r in o.reads:
                for reg, w in last_writes.get(r.name, ()):
                    if r.overlaps(reg):
                        deps.add(w)
            for wreg in o.writes:
                for reg, w in last_writes.get(wreg.name, ()):
                    if wreg.overlaps(reg):
                        deps.add(w)
                for reg, rd in reads_since.get(wreg.name, ()):
                    if wreg.overlaps(reg):
                        deps.add(rd)
            deps.discard(o)
            o.edges = [d for d in deps]
            del deps
            for wreg in o.writes:
                lw = last_writes.setdefault(wreg.name, [])
                lw[:] = [(reg, w) for (reg, w) in lw if not wreg.covers(reg)]
                lw.append((wreg, o))
                rs = reads_since.setdefault(wreg.name, [])
                rs[:] = [(reg, rr) for (reg, rr) in rs
                         if not wreg.covers(reg)]
            for r in o.reads:
                reads_since.setdefault(r.name, []).append((r, o))

    def _dma_discipline(self):
        import bisect
        ch_dmas = {}
        for o in self.ops:
            if o.dma_ch is not None:
                ch_dmas.setdefault(o.dma_ch, []).append(o)
        self._ch_dmas = ch_dmas
        self._ch_recs = {c: [d.rec for d in lst] for c, lst in
                         ch_dmas.items()}
        for c, lst in ch_dmas.items():
            engs = {d.engine for d in lst}
            assert len(engs) == 1, ("dma channel on multiple engines", c,
                                    engs)
        for o in self.ops:
            chs = set()
            for p in o.edges:
                if p.dma_ch is not None and p.dma_ch != o.dma_ch:
                    chs.add(p.dma_ch)
            for c in chs:
                recs = self._ch_recs[c]
                j = bisect.bisect_left(recs, o.rec)
                if j < len(recs):
                    nxt = ch_dmas[c][j]
                    if o not in nxt.edges:
                        nxt.edges.append(o)

    def _schedule(self):
        import heapq
        ops = self.ops
        n = len(ops)
        if n == 0:
            return
        succ = [[] for _ in range(n)]
        indeg = [0] * n
        have = set()
        for o in ops:
            for p in o.edges:
                if (p.gid, o.gid) not in have:
                    have.add((p.gid, o.gid))
                    succ[p.gid].append(o.gid)
                    indeg[o.gid] += 1
        def add_edge(a, b):
            if a != b and (a, b) not in have:
                have.add((a, b))
                succ[a].append(b)
                indeg[b] += 1

        import bisect
        for ch, lst in self._ch_dmas.items():
            for a, b in zip(lst, lst[1:]):
                add_edge(a.gid, b.gid)
        for o in ops:
            chs = set()
            for p in o.edges:
                if p.dma_ch is not None and p.dma_ch != o.dma_ch:
                    chs.add(p.dma_ch)
            for c in chs:
                recs = self._ch_recs[c]
                j = bisect.bisect_left(recs, o.rec)
                if j > 0:
                    add_edge(self._ch_dmas[c][j - 1].gid, o.gid)
        est = [0.0] * n
        for o in ops:
            if o.est is not None:
                est[o.gid] = o.est
            elif o.dma_ch is not None:
                byt = sum((w.p1 - w.p0) * (w.b1 - w.b0) for w in o.writes)
                est[o.gid] = 600.0 + byt * 0.02
            elif o.tag == "BARR":
                est[o.gid] = 100.0
            else:
                bpp = 0
                for r in o.writes:
                    bpp = max(bpp, r.b1 - r.b0)
                for r in o.reads:
                    bpp = max(bpp, r.b1 - r.b0)
                f = 0.21 if o.engine in ("gpsimd", "scalar") else 0.13
                ov = 222.0 if o.engine == "scalar" else 64.0
                est[o.gid] = ov + f * bpp
        lp = [0.0] * n
        for i in range(n - 1, -1, -1):
            m = 0.0
            for s in succ[i]:
                if lp[s] > m:
                    m = lp[s]
            lp[i] = est[i] + m
        K = 8
        WIN = 400.0
        eng_time = {e: 0.0 for e in ENG_NAMES}
        ch_time = {}
        comp = [0.0] * n
        rtime = [0.0] * n
        heaps = {e: [] for e in ENG_NAMES}
        removed = set()
        cand_cache = {e: None for e in ENG_NAMES}

        for o in ops:
            if indeg[o.gid] == 0:
                heapq.heappush(heaps[o.engine], (-lp[o.gid], o.gid))

        def peek(e):
            h = heaps[e]
            popped = []
            while h and len(popped) < K:
                nl, g = heapq.heappop(h)
                if g in removed:
                    continue
                popped.append((nl, g))
            best = None
            et = eng_time[e]
            for nl, g in popped:
                st = rtime[g] if rtime[g] > et else et
                if best is None or st < best[0] - WIN or \
                        (st < best[0] + WIN and nl < best[1]):
                    best = (st, nl, g)
            for it in popped:
                heapq.heappush(h, it)
            return best

        order = []
        while len(order) < n:
            pick = None
            pick_e = None
            for e in ENG_NAMES:
                if cand_cache[e] is None:
                    cand_cache[e] = peek(e)
                c = cand_cache[e]
                if c is None:
                    continue
                if pick is None or (c[0], c[1]) < (pick[0], pick[1]):
                    pick = c
                    pick_e = e
            assert pick is not None, "scheduler: no ready ops (cycle?)"
            st, _, g = pick
            o = ops[g]
            removed.add(g)
            cand_cache[pick_e] = None
            if o.dma_ch is not None:
                eng_time[pick_e] = st + 50.0
                c0 = ch_time.get(o.dma_ch, 0.0)
                comp[g] = (c0 if c0 > st else st) + est[g]
                ch_time[o.dma_ch] = comp[g]
            else:
                comp[g] = st + est[g]
                eng_time[pick_e] = comp[g]
            order.append(o)
            for s in succ[g]:
                if comp[g] > rtime[s]:
                    rtime[s] = comp[g]
                indeg[s] -= 1
                if indeg[s] == 0:
                    eo = ops[s]
                    heapq.heappush(heaps[eo.engine], (-lp[s], s))
                    if cand_cache[eo.engine] is not None:
                        cand_cache[eo.engine] = None
        self.ops = order
        for i, o in enumerate(order):
            o.gid = i
        for o in order:
            for p in o.edges:
                assert p.gid < o.gid, "scheduler broke topological order"
        seen = {}
        for o in order:
            if o.dma_ch is not None:
                assert o.inc_index == seen.get(o.dma_ch, 0) + 1, \
                    "scheduler broke per-channel DMA order"
                seen[o.dma_ch] = o.inc_index
        self._sched_span = max(comp) if n else 0.0

    def _mark_incs(self):
        for o in self.ops:
            for p in o.edges:
                if p.dma_ch is None:
                    p.inc_marked = True
        counters = {e: 0 for e in ENG_NAMES}
        pos = {e: 0 for e in ENG_NAMES}
        for o in self.ops:
            if o.tag == "BARR":
                o.inc_index = None
            o.eng_pos = pos[o.engine]
            pos[o.engine] += 1
            if o.dma_ch is None and o.inc_marked:
                counters[o.engine] += 1
                o.inc_index = counters[o.engine]
        self._final_counts = counters

    def finalize(self):
        self._compute_edges()
        self._dma_discipline()
        if SCHEDULE:
            self._schedule()
        self._mark_incs()
        nc = self.nc
        sems = {}
        for e in ENG_NAMES:
            sems[e] = self.es.enter_context(nc.semaphore(f"sem_{e}"))
        for c in self._dma_counts:
            sems["dma_" + c] = self.es.enter_context(
                nc.semaphore(f"sem_dma_{c}"))

        import bisect

        waits = []
        for o in self.ops:
            wl = {}
            for p in o.edges:
                if p.dma_ch is not None:
                    key = "dma_" + p.dma_ch
                    val = 16 * bisect.bisect_left(self._ch_recs[p.dma_ch],
                                                  o.rec)
                else:
                    key = p.engine
                    val = p.inc_index
                if wl.get(key, -1) < val:
                    wl[key] = val
            waits.append(wl)
        marks = {e: {} for e in ENG_NAMES}
        for o, wl in zip(self.ops, waits):
            wm = marks[o.engine]
            red = {}
            for k, v in wl.items():
                if k == "tensor" and o.engine == "tensor":
                    continue
                if REDUCE_WAITS and wm.get(k, 0) >= v:
                    continue
                red[k] = v
                wm[k] = max(wm.get(k, 0), v)
            waits[o.gid] = red

        per_engine = {e: [] for e in ENG_NAMES}
        for o in self.ops:
            per_engine[o.engine].append(o)

        block = self.es.enter_context(nc.Block())

        def make_body(ename):
            ops_list = per_engine[ename]

            def body(eng):
                for o in ops_list:
                    wl = sorted(waits[o.gid].items())
                    for k, v in wl[1:]:
                        eng.wait_ge(sems[k], v)
                    ret = o.emit(eng)
                    first, inst = ret if isinstance(ret, tuple) else (ret,
                                                                      ret)
                    if wl:
                        k, v = wl[0]
                        first._wait_ge(sems[k], v)
                    if o.dma_ch is not None:
                        inst.then_inc(sems["dma_" + o.dma_ch], 16)
                    elif o.inc_marked:
                        inst.then_inc(sems[o.engine], 1)
                if ename == "sync":
                    for c, cnt in sorted(self._dma_counts.items()):
                        if cnt:
                            eng.wait_ge(sems["dma_" + c], 16 * cnt)
            return body

        block.tensor(make_body("tensor"))
        block.vector(make_body("vector"))
        block.scalar(make_body("scalar"))
        block.gpsimd(make_body("gpsimd"))
        block.sync(make_body("sync"))
        self.es.close()
        return nc



T_SEQ = 2048
NT = 16
D_MODEL = 1024


class Arena:

    def __init__(self, B, name, f32_cols):
        self.t = B.es.enter_context(
            B.nc.sbuf_tensor(name, [128, f32_cols], F32))
        self.cols = f32_cols

    def f32(self, c0, cols):
        assert c0 + cols <= self.cols, "arena overflow"
        return self.t[:, c0:c0 + cols]

    def bf16(self, c0_f32, bf_cols):
        assert 2 * c0_f32 + bf_cols <= 2 * self.cols, "arena overflow"
        return self.t[:, :].bitcast(BF16)[:, 2 * c0_f32:2 * c0_f32 + bf_cols]


class ArenaView:

    def __init__(self, arena, base):
        self.A = arena
        self.base = base
        self.cols = arena.cols - base

    def f32(self, c0, cols):
        return self.A.f32(self.base + c0, cols)

    def bf16(self, c0_f32, bf_cols):
        return self.A.bf16(self.base + c0_f32, bf_cols)

    def row(self, c0, cols):
        return self.A.t[0:1, self.base + c0:self.base + c0 + cols]


class FMT:

    def __init__(self, B, name, C, T, dtype, ap=None):
        self.C, self.T = C, T
        self.nk = (C + 127) // 128
        self.sb = ap if ap is not None else B.sb(name, [128, self.nk * T],
                                                 dtype)

    def rows(self, k):
        return min(128, self.C - 128 * k)

    def chunk(self, k, t0=0, t1=None):
        t1 = self.T if t1 is None else t1
        return self.sb[0:self.rows(k), k * self.T + t0:k * self.T + t1]


class TMT:

    def __init__(self, B, name, T, C, dtype, ap=None):
        self.T, self.C = T, C
        self.nt = (T + 127) // 128
        self.sb = ap if ap is not None else B.sb(name, [128, self.nt * C],
                                                 dtype)

    def tile(self, t, c0=0, c1=None):
        c1 = self.C if c1 is None else c1
        return self.sb[:, t * self.C + c0:t * self.C + c1]


class PsumPool:
    def __init__(self, B, n_f32=6, n_bf16=2, pref="pp"):
        self.f32s = [B.ps(f"{pref}_f{i}", [128, 512], F32)
                     for i in range(n_f32)]
        self.b16s = [B.ps(f"{pref}_b{i}", [128, 1024], BF16)
                     for i in range(n_bf16)]
        self.fi = 0
        self.bi = 0

    def f32(self):
        t = self.f32s[self.fi % len(self.f32s)]
        self.fi += 1
        return t

    def bf16(self):
        t = self.b16s[self.bi % len(self.b16s)]
        self.bi += 1
        return t


def load_x(B, x_dram, name="x"):
    x = TMT(B, name, T_SEQ, D_MODEL, F32)
    B.dma(x.sb.rearrange("p (t c) -> p t c", t=NT),
          x_dram.rearrange("(t p) c -> p t c", p=128), ch=name + "_in",
          batch=True)
    return x


def store_x(B, x, x_dram):
    B.dma(x_dram.rearrange("(t p) c -> p t c", p=128),
          x.sb.rearrange("p (t c) -> p t c", t=x.nt), ch="x_out",
          on="scalar", batch=True)


def pre_ln_tile(B, x_ap, out_ap, cols4, sq, eps=1e-5, pref="ln"):
    C = x_ap.shape[-1]
    negmu = cols4[:, 0:1]
    vs = cols4[:, 1:2]
    std = cols4[:, 2:3]
    rstd = cols4[:, 3:4]
    B.red(negmu, x_ap, ALU.add, negate=True, tag=pref)
    B.ts(negmu, negmu, 1.0 / C, None, ALU.mult, tag=pref)
    B.act(sq[:, 0:C], x_ap, AF.Square, bias=negmu, accum_out=vs, tag=pref)
    B.act(std, vs, AF.Sqrt, bias=B.cs(eps), scale=1.0 / C, tag=pref)
    B.recip(rstd, std)
    B.ts(out_ap, x_ap, negmu, rstd, ALU.add, ALU.mult, tag=pref)


def pre_ln(B, x, out, pref="ln", eps=1e-5, t_tiles=None, sq=None):
    C = x.C
    cols = B.scratch([128, 4 * x.nt], F32)
    if sq is None:
        sq = B.scratch([128, C], F32)
    for t in (t_tiles if t_tiles is not None else range(x.nt)):
        pre_ln_tile(B, x.tile(t), out.tile(t), cols[:, 4 * t:4 * t + 4], sq,
                    eps=eps, pref=pref)


def tm_to_fm(B, tm, fm, pool, t_tiles=None, k_chunks=None):
    ts_list = list(t_tiles if t_tiles is not None else range(tm.nt))
    ks = list(k_chunks if k_chunks is not None else range((tm.C + 127) // 128))
    for k in ks:
        rows = min(128, tm.C - 128 * k)
        for g0 in range(0, len(ts_list), 8):
            grp = ts_list[g0:g0 + 8]
            bank = pool.bf16()
            for j, t in enumerate(grp):
                B.transpose(bank[0:rows, j * 128:(j + 1) * 128],
                            tm.tile(t, 128 * k, 128 * k + rows),
                            B.ident, tag="tr")
            if grp == list(range(grp[0], grp[0] + len(grp))):
                B.copy(fm.sb[0:rows,
                             k * fm.T + grp[0] * 128:
                             k * fm.T + (grp[0] + len(grp)) * 128],
                       bank[0:rows, 0:len(grp) * 128], tag="trev")
            else:
                for j, t in enumerate(grp):
                    B.copy(fm.chunk(k, t * 128, (t + 1) * 128),
                           bank[0:rows, j * 128:(j + 1) * 128], tag="trev")


def load_weight(B, name, dram_ap, C, N, stage, out_dtype=BF16):
    nk = (C + 127) // 128
    w = FMT(B, "w_" + name, C, N, out_dtype)
    for k in range(nk):
        rows = min(128, C - 128 * k)
        half = (k % 2) * (stage.shape[1] // 2)
        st = stage[0:rows, half:half + N]
        B.dma(st, dram_ap[128 * k:128 * k + rows, :],
              ch=f"w_{name}_{k % 2}")
        B.copy(w.chunk(k), st, tag="wconv")
    return w


def load_vec_cols(B, name, dram_ap, n):
    v = B.scratch([128, n], F32)
    B.dma(v, dram_ap, ch="v_" + name, batch=True)
    return v


def load_rows(B, name, dram_ap, n):
    r = B.scratch([1, n], F32)
    B.dma(r, dram_ap, ch="r_" + name, batch=True)
    return r


def mmA(B, pool, xT, W, t, n0, nn, kcs=None, bias_row=None, extra=None,
        tag="mmA"):
    kcs = kcs if kcs is not None else range(xT.nk)
    bank = pool.f32()
    pairs = []
    for k in kcs:
        pairs.append((xT.chunk(k, t * 128, (t + 1) * 128),
                      W.chunk(k, n0, n0 + nn)))
    if bias_row is not None:
        pairs.append((B.onesf[0:1, 0:128], bias_row[0:1, n0:n0 + nn]))
    if extra:
        pairs.extend(extra)
    B.mm_group(bank[:, 0:nn], pairs, tag=tag)
    return bank[:, 0:nn]


def mmB(B, pool, W, xT, m, t0, nn, kcs=None, bias_row=None, tag="mmB"):
    kcs = kcs if kcs is not None else range(W.nk)
    mrows = min(128, W.T - m * 128)
    bank = pool.f32()
    pairs = []
    for k in kcs:
        pairs.append((W.chunk(k, m * 128, m * 128 + mrows),
                      xT.chunk(k, t0, t0 + nn)))
    if bias_row is not None:
        pairs.append((bias_row[0:1, m * 128:m * 128 + mrows],
                      B.onesf[0:1, 0:nn]))
    B.mm_group(bank[0:mrows, 0:nn], pairs, tag=tag)
    return bank[0:mrows, 0:nn]



PI = math.pi
TWO_PI = 2 * math.pi
SIN_OFF = PI + 1024 * TWO_PI


def prep_l0_attn(inp):
    f32 = np.float32
    rows = np.zeros((1, 4096), f32)
    rows[0, 0:1024] = inp["L0_attn_bq"]
    rows[0, 1024:2048] = inp["L0_attn_bk"]
    rows[0, 2048:3072] = inp["L0_attn_bv"]
    rows[0, 3072:4096] = inp["L0_attn_bo"]
    proj = np.asarray(inp["L0_attn_proj_mat"], f32)
    return {
        "wqT": np.ascontiguousarray(np.asarray(inp["L0_attn_Wq"], f32).T),
        "wkT": np.ascontiguousarray(np.asarray(inp["L0_attn_Wk"], f32).T),
        "wvT": np.ascontiguousarray(np.asarray(inp["L0_attn_Wv"], f32).T),
        "woT": np.ascontiguousarray(np.asarray(inp["L0_attn_Wo"], f32).T),
        "projdup": np.concatenate([proj, proj], 0),
        "rows": rows,
    }


def dram_l0a(nc, pfx=""):
    def dt_(n, shape, dt=F32):
        return nc.dram_tensor(pfx + n, shape, dt, kind="ExternalInput").ap()
    return dict(wqd=dt_("wqT", [1024, 1024]), wkd=dt_("wkT", [1024, 1024]),
                wvd=dt_("wvT", [1024, 1024]), wod=dt_("woT", [1024, 1024]),
                projd=dt_("projdup", [128, 128]),
                rowsd=dt_("rows", [1, 4096]))


def body_l0a(B, pool, d, xin, xout):
    wqd, wkd, wvd, wod, projd, rowsd = (d["wqd"], d["wkd"], d["wvd"],
                                        d["wod"], d["projd"], d["rowsd"])
    A = B.arena(42496)
    x = TMT(B, "x", T_SEQ, D_MODEL, F32, ap=A.f32(0, 16384))
    B.dma(x.sb.rearrange("p (t c) -> p t c", t=NT),
          xin.rearrange("(t p) c -> p t c", p=128), ch="x_in", batch=True)
    h = TMT(B, "h", T_SEQ, D_MODEL, BF16, ap=A.bf16(16384, 16384))
    rr0 = B.sb("rr0", [128, 4 * 512], F32)
    rr = [rr0, rr0]
    pre_ln(B, x, h, sq=rr0[:, 0:1024])
    hT = FMT(B, "hT", D_MODEL, T_SEQ, BF16, ap=A.bf16(24576, 16384))
    tm_to_fm(B, h, hT, pool)

    stage = A.f32(16384, 2048)
    Wq = FMT(B, None, 1024, 1024, BF16, ap=A.bf16(0, 8192))
    Wk = FMT(B, None, 1024, 1024, BF16, ap=A.bf16(4096, 8192))
    Wv = FMT(B, None, 1024, 1024, BF16, ap=A.bf16(8192, 8192))
    Wo = FMT(B, None, 1024, 1024, BF16, ap=A.bf16(12288, 8192))
    for nm, wd_, W in (("q", wqd, Wq), ("k", wkd, Wk), ("v", wvd, Wv),
                       ("o", wod, Wo)):
        for k in range(8):
            st = stage[:, (k % 2) * 1024:(k % 2) * 1024 + 1024]
            B.dma(st, wd_[128 * k:128 * (k + 1), :], ch=f"w{nm}{k % 2}")
            B.copy(W.chunk(k), st, tag="wconv")
    projsb = B.sb("projsb", [128, 128], BF16)
    projst = B.sb("projst", [128, 128], F32)
    B.dma(projst, projd, ch="proj", batch=True)
    B.ts(projsb, projst, 1.0 / (16.0 * PI), None, ALU.mult, tag="wconv")
    rows = load_rows(B, "l0", rowsd, 4096)

    outT = FMT(B, "outT", D_MODEL, T_SEQ, BF16, ap=A.bf16(32768, 16384))
    qTb = [A.bf16(18432 + i * 1024, 2048) for i in range(2)]
    kTb = [A.bf16(20480 + i * 1024, 2048) for i in range(2)]
    vb = [TMT(B, None, T_SEQ, 128, BF16, ap=A.bf16(22528 + i * 1024, 2048))
          for i in range(2)]
    kp = [A.bf16(40960 + i * 128, 256) for i in range(2)]
    qpT = [FMT(B, None, 256, 512, BF16, ap=A.bf16(41216 + i * 512, 1024))
           for i in range(2)]
    kvp = [B.ps(f"kvp{i}", [128, 512], F32) for i in range(2)]
    kvsb = [B.sb(f"kv{i}", [128, 128], BF16) for i in range(2)]
    otk = [B.sb(f"otk{i}", [128, 64], BF16) for i in range(2)]

    def fourier(s, psum, dst_cos, dst_sin):
        n = psum.shape[1]
        M = 12582912.0
        r1 = rr0[:, 0:n]
        t2 = rr0[:, 512:512 + n]
        r2 = rr0[:, 1024:1024 + n]
        f2 = rr0[:, 1536:1536 + n]
        B.ts(r1, psum, M, None, ALU.add, tag="four")
        B.stt(r1, r1, M, psum, ALU.subtract, ALU.subtract, tag="four")
        B.act(dst_sin, r1, AF.Sin, scale=TWO_PI, tag="four")
        B.ts(t2, psum, 0.25, None, ALU.add, tag="four")
        B.ts(r2, t2, M, None, ALU.add, tag="four")
        B.stt(f2, r2, M, t2, ALU.subtract, ALU.subtract, tag="four")
        B.act(dst_cos, f2, AF.Sin, scale=TWO_PI, tag="four")

    for hp in range(8):
        s = hp % 2
        for t0 in range(0, T_SEQ, 512):
            ps = mmB(B, pool, Wq, hT, hp, t0, 512,
                     bias_row=rows[0:1, 0:1024], tag="qkv")
            B.copy(qTb[s][:, t0:t0 + 512], ps, tag="qkv_ev")
            ps = mmB(B, pool, Wk, hT, hp, t0, 512,
                     bias_row=rows[0:1, 1024:2048], tag="qkv")
            B.copy(kTb[s][:, t0:t0 + 512], ps, tag="qkv_ev")
        for t in range(NT):
            ps = mmA(B, pool, hT, Wv, t, hp * 128, 128,
                     bias_row=rows[0:1, 2048:3072], tag="qkv")
            B.copy(vb[s].tile(t), ps, tag="qkv_ev")
        for h_ in (0, 1):
            head = 2 * hp + h_
            base = h_ * 64
            for t in range(NT):
                sk = t % 2
                pk = pool.f32()
                B.mm_group(pk[:, 0:128],
                           [(kTb[s][base:base + 64, t * 128:(t + 1) * 128],
                             projsb[base:base + 64, :])], tag="pk")
                fourier(sk, pk[:, 0:128], kp[sk][:, 0:128],
                        kp[sk][:, 128:256])
                for m in range(2):
                    B.mm_group(kvp[m][:, 0:64],
                               [(kp[sk][:, m * 128:(m + 1) * 128],
                                 vb[s].tile(t, base, base + 64))],
                               start=(t == 0), stop=(t == NT - 1), tag="kv")
            for m in range(2):
                B.copy(kvsb[s][:, m * 64:(m + 1) * 64], kvp[m][:, 0:64],
                       tag="kv_ev")
            for t0 in range(0, T_SEQ, 512):
                sq_ = (t0 // 512) % 2
                pq = pool.f32()
                B.mm_group(pq[:, 0:512],
                           [(projsb[base:base + 64, :],
                             qTb[s][base:base + 64, t0:t0 + 512])], tag="pq")
                fourier(sq_, pq[:, 0:512], qpT[sq_].chunk(0),
                        qpT[sq_].chunk(1))
                bank = pool.bf16()
                for j in range(4):
                    t = t0 // 128 + j
                    ops_ = pool.f32()
                    B.mm_group(ops_[:, 0:64],
                               [(qpT[sq_].chunk(0, j * 128, (j + 1) * 128),
                                 kvsb[s][:, 0:64]),
                                (qpT[sq_].chunk(1, j * 128, (j + 1) * 128),
                                 kvsb[s][:, 64:128])], tag="outh")
                    B.ts(otk[s], ops_[:, 0:64], 1.0 / 128.0, None, ALU.mult,
                         tag="outh_ev")
                    B.transpose(bank[base:base + 64, j * 128:(j + 1) * 128],
                                otk[s], B.ident, tag="outh_tr")
                B.copy(outT.sb[base:base + 64,
                               hp * T_SEQ + t0:hp * T_SEQ + t0 + 512],
                       bank[base:base + 64, 0:512], tag="outh_trev")

    xr = A.f32(16384, 4096)
    res = A.f32(20480, 4096)
    for t in range(NT):
        xt = xr[:, (t % 2) * 2048:(t % 2) * 2048 + 1024]
        rt = res[:, (t % 2) * 2048:(t % 2) * 2048 + 1024]
        B.dma(xt, xin.rearrange("(t p) c -> t p c", p=128)[t],
              ch=f"xr{t % 2}")
        for n0 in (0, 512):
            ps = mmA(B, pool, outT, Wo, t, n0, 512,
                     bias_row=rows[0:1, 3072:4096], tag="wo")
            B.tt(rt[:, n0:n0 + 512], ps, xt[:, n0:n0 + 512], ALU.add,
                 tag="resid")
        B.dma(xout.rearrange("(t p) c -> t p c", p=128)[t], rt,
              ch=f"xo{t % 2}", on="scalar")



def _causal_masks_l0f():
    p = np.arange(128)[:, None]
    c = np.arange(512)[None, :]
    m1 = np.stack([np.where(c <= 128 * j + p, 0.0, -1e30) for j in range(4)])
    mT = np.stack([np.where(c >= 128 * j + p, 0.0, -1e30) for j in range(4)])
    return (np.concatenate([m1.transpose(1, 0, 2).reshape(128, 2048),
                            mT.transpose(1, 0, 2).reshape(128, 2048)], 1)
            .astype(BF16NP))


def prep_l0_ffn(inp):
    f32 = np.float32
    qkvw = np.concatenate([np.asarray(inp["L0_ffn_q_proj"], f32),
                           np.asarray(inp["L0_ffn_k_proj"], f32),
                           np.asarray(inp["L0_ffn_v_proj"], f32)], axis=2)
    return {
        "qkvw": qkvw,
        "ow": np.asarray(inp["L0_ffn_o_proj"], f32),
        "simw": np.asarray(inp["L0_ffn_sim_matrix"], f32),
        "gates": np.asarray(inp["L0_ffn_gates"], f32).reshape(1, 8),
        "masksd": _causal_masks_l0f(),
    }


def dram_l0f(nc, pfx=""):
    def dt_(n, shape, dt=F32):
        return nc.dram_tensor(pfx + n, shape, dt, kind="ExternalInput").ap()
    return dict(qkvd=dt_("qkvw", [8, 1024, 384]),
                owd=dt_("ow", [8, 128, 1024]), simd=dt_("simw", [1024, 8]),
                gated=dt_("gates", [1, 8]),
                maskd=dt_("masksd", [128, 4096], BF16))


def body_l0f(B, pool, d, xin, xout):
    qkvd, owd, simd, gated, maskd = (d["qkvd"], d["owd"], d["simd"],
                                     d["gated"], d["maskd"])
    A = B.arena(25600)
    xs2 = A.f32(23552, 2048)
    xind = xin.rearrange("(t p) c -> t p c", p=128)
    xoutd = xout.rearrange("(t p) c -> t p c", p=128)
    h = TMT(B, "h", T_SEQ, D_MODEL, BF16, ap=A.bf16(0, 16384))
    lncols = B.sb("lncols", [128, 4 * NT], F32)
    sq = B.sb("sqscr", [128, 1024], F32)
    for t in range(NT):
        xt = xs2[:, (t % 2) * 1024:(t % 2) * 1024 + 1024]
        B.dma(xt, xind[t], ch=f"x{t % 2}")
        pre_ln_tile(B, xt, h.tile(t), lncols[:, 4 * t:4 * t + 4], sq)
    hT = FMT(B, "hT", D_MODEL, T_SEQ, BF16, ap=A.bf16(8192, 16384))
    tm_to_fm(B, h, hT, pool)
    rnorm = B.sb("rnorm", [128, NT], F32)
    rrec = B.sb("rrec", [128, NT], F32)
    hsq = B.sb("hsqscr", [128, 1024], F32)
    for t in range(NT):
        B.act(hsq, h.tile(t), AF.Square, accum_out=rnorm[:, t:t + 1],
              tag="rn")
        B.act(rnorm[:, t:t + 1], rnorm[:, t:t + 1], AF.Sqrt, tag="rn")
        B.ts(rnorm[:, t:t + 1], rnorm[:, t:t + 1], 1e-12, None, ALU.max,
             tag="rn")
        B.recip(rrec[:, t:t + 1], rnorm[:, t:t + 1])

    simsb = FMT(B, "simsb", 1024, 8, BF16)
    simst = B.sb("simst", [128, 64], F32)
    B.dma(simst.rearrange("p (k e) -> p k e", k=8),
          simd.rearrange("(k p) e -> p k e", p=128), ch="sim", batch=True)
    B.copy(simsb.sb, simst, tag="sim")
    simsq = B.sb("simsq", [128, 64], F32)
    B.act(simsq, simst, AF.Square, tag="sim")
    csps = pool.f32()
    B.mm_group(csps[0:1, 0:8],
               [(B.onesf[:, 0:1], simsq[:, 8 * k:8 * k + 8])
                for k in range(8)], tag="simn")
    csr = B.sb("csr", [1, 8], F32)
    B.act(csr, csps[0:1, 0:8], AF.Sqrt, tag="sim")
    B.ts(csr, csr, 1e-12, None, ALU.max, tag="sim")
    B.recip(csr, csr)
    csrb = B.sb("csrb", [128, 8], F32)
    csb16 = B.sb("csb16", [1, 8], BF16)
    B.copy(csb16, csr, tag="sim")
    csps2 = pool.f32()
    B.mm_group(csps2[:, 0:8], [(B.onesb[0:1, 0:128], csb16[0:1, :])],
               tag="simb")
    B.copy(csrb, csps2[:, 0:8], tag="sim")
    gsb = B.sb("gsb", [1, 8], F32)
    B.dma(gsb, gated, ch="gates", batch=True)
    sgate = B.sb("sgate", [1, 8], BF16)
    B.act(sgate, gsb, AF.Sigmoid, tag="gate")
    sgb = B.sb("sgb", [128, 8], F32)
    csps3 = pool.f32()
    B.mm_group(csps3[:, 0:8], [(B.onesb[0:1, 0:128], sgate[0:1, :])],
               tag="gateb")
    B.copy(sgb, csps3[:, 0:8], tag="gate")

    mcols = B.sb("mcols", [128, 8 * NT], F32)
    wcols = B.sb("wcols", [128, 8 * NT], F32)
    rt = B.sb("rtscr", [128, 64], F32)
    for t in range(NT):
        lg = rt[:, 0:8]
        gt_ = rt[:, 8:16]
        fb1 = rt[:, 16:24]
        m2s = rt[:, 24:32]
        ml = rt[:, 32:40]
        ex = rt[:, 40:48]
        col = rt[:, 48:49]
        col2 = rt[:, 49:50]
        mk = mcols[:, 8 * t:8 * t + 8]
        wk = wcols[:, 8 * t:8 * t + 8]
        ps = mmA(B, pool, hT, simsb, t, 0, 8, tag="logit")
        B.stt(lg, ps, rrec[:, t:t + 1], csrb, ALU.mult, ALU.mult, tag="rt")
        B.tt(lg, lg, sgb, ALU.subtract, tag="rt")
        B.ts(gt_, lg, 0.0, None, ALU.max, tag="rt")
        B.ts(mk, lg, 0.0, None, ALU.is_gt, tag="rt")
        B.red(col, mk, ALU.add, tag="rt")
        B.ts(col, col, 0.0, None, ALU.is_equal, tag="rt")
        B.red(col2, lg, ALU.max, tag="rt")
        B.ts(fb1, lg, col2, None, ALU.is_ge, tag="rt")
        B.stt(m2s, fb1, -2e9, lg, ALU.mult, ALU.add, tag="rt")
        B.red(col2, m2s, ALU.max, tag="rt")
        B.ts(m2s, m2s, col2, None, ALU.is_ge, tag="rt")
        B.tt(fb1, fb1, m2s, ALU.add, tag="rt")
        B.stt(mk, fb1, col, mk, ALU.mult, ALU.max, tag="rt")
        B.tt(ml, gt_, mk, ALU.mult, tag="rt")
        B.ts(ex, mk, -1.0, 1e9, ALU.add, ALU.mult, tag="rt")
        B.tt(ml, ml, ex, ALU.add, tag="rt")
        B.red(col2, ml, ALU.max, negate=True, tag="rt")
        B.act(ex, ml, AF.Exp, bias=col2, accum_out=col, tag="rt")
        B.recip(col, col)
        B.stt(wk, ex, col, mk, ALU.mult, ALU.mult, tag="rt")
    qkv = TMT(B, "qkv", T_SEQ, 384, F32, ap=A.f32(0, 6144))
    wstage = B.sb("wstage", [128, 2 * 3072], F32)
    qkvw_sb = [FMT(B, f"qkvw{i}", 1024, 384, BF16) for i in range(2)]
    for e in range(8):
        se = e % 2
        st = wstage[:, se * 3072:se * 3072 + 3072]
        B.dma(st.rearrange("p (k n) -> p k n", k=8),
              qkvd[e].rearrange("(k p) n -> p k n", p=128), ch=f"qkv{se}")
        B.copy(qkvw_sb[se].sb, st, tag="wconv")
        for t in range(NT):
            ps = mmA(B, pool, hT, qkvw_sb[se], t, 0, 384, tag="eqkv")
            mc = mcols[:, 8 * t + e:8 * t + e + 1]
            if e == 0:
                B.ts(qkv.tile(t), ps, mc, None, ALU.mult, tag="eacc")
            else:
                B.stt(qkv.tile(t), ps, mc, qkv.tile(t), ALU.mult, ALU.add,
                      tag="eacc")
    qT = FMT(B, "qT", 128, T_SEQ, BF16, ap=A.bf16(6144, 2048))
    kT = FMT(B, "kT", 128, T_SEQ, BF16, ap=A.bf16(7168, 2048))
    vb = TMT(B, "vb", T_SEQ, 128, BF16, ap=A.bf16(21504, 2048))
    qsc = B.sb("qsc", [128, 128], BF16)
    for t in range(NT):
        B.ts(qsc, qkv.tile(t, 0, 128), 1.0 / math.sqrt(128.0), None,
             ALU.mult, tag="qsc")
        bank = pool.bf16()
        B.transpose(bank[:, 0:128], qsc, B.ident, tag="qtr")
        B.copy(qT.chunk(0, t * 128, (t + 1) * 128), bank[:, 0:128],
               tag="qtr")
        B.copy(qsc, qkv.tile(t, 128, 256), tag="ksc")
        bank2 = pool.bf16()
        B.transpose(bank2[:, 128:256], qsc, B.ident, tag="ktr")
        B.copy(kT.chunk(0, t * 128, (t + 1) * 128), bank2[:, 128:256],
               tag="ktr")
        B.copy(vb.tile(t), qkv.tile(t, 256, 384), tag="vsc")
    masks = B.sb("masks", [128, 4096], BF16)
    B.dma(masks, maskd, ch="masks", batch=True)
    rm = B.sb("rm", [128, NT], F32)
    rscr = B.sb("rscr", [128, 2], F32)
    for t in range(NT):
        nchunks = t // 4 + 1
        for c5 in range(nchunks):
            ps = pool.f32()
            B.mm_group(ps[:, 0:512],
                       [(qT.chunk(0, t * 128, (t + 1) * 128),
                         kT.chunk(0, c5 * 512, (c5 + 1) * 512))], tag="s1")
            if c5 == t // 4:
                j = t % 4
                B.tt(ps[:, 0:512], ps[:, 0:512],
                     masks[:, 512 * j:512 * j + 512], ALU.add, tag="s1m")
            if c5 == 0:
                B.red(rm[:, t:t + 1], ps[:, 0:512], ALU.max, tag="s1r")
            else:
                B.red(rscr[:, 0:1], ps[:, 0:512], ALU.max, tag="s1r")
                B.tt(rm[:, t:t + 1], rm[:, t:t + 1], rscr[:, 0:1], ALU.max,
                     tag="s1r")
    rmb = B.sb("rmb", [128, NT], BF16)
    B.ts(rmb, rm, -1.0, None, ALU.mult, tag="rmneg")
    rmrow = B.sb("rmrow", [1, T_SEQ], BF16)
    for t in range(NT):
        bank = pool.bf16()
        B.transpose(bank[0:1, 0:128], rmb[:, t:t + 1], B.ident, tag="rmtr")
        B.copy(rmrow[0:1, 128 * t:128 * t + 128], bank[0:1, 0:128],
               tag="rmtr")
    ET = B.sb("ET", [128, 16 * 512], BF16)
    ao = TMT(B, "ao", T_SEQ, 128, BF16, ap=A.bf16(20480, 2048))
    dn = B.sb("dnrow", [1, 512], F32)
    for b in range(4):
        nblk = 4 * b + 4
        for tkb in range(nblk):
            ps = pool.f32()
            B.mm_group(ps[:, 0:512],
                       [(kT.chunk(0, tkb * 128, (tkb + 1) * 128),
                         qT.chunk(0, b * 512, (b + 1) * 512)),
                        (B.onesb[0:1, 0:128],
                         rmrow[0:1, b * 512:(b + 1) * 512])], tag="s2")
            if tkb >= 4 * b:
                j = tkb % 4
                B.tt(ps[:, 0:512], ps[:, 0:512],
                     masks[:, 2048 + 512 * j:2048 + 512 * j + 512],
                     ALU.add, tag="s2m")
            B.act(ET[:, 512 * tkb:512 * tkb + 512], ps[:, 0:512], AF.Exp,
                  tag="s2e")
        dps = pool.f32()
        B.mm_group(dps[0:1, 0:512],
                   [(B.onesb[:, 0:1], ET[:, 512 * k:512 * k + 512])
                    for k in range(nblk)], tag="dn")
        B.copy(dn, dps[0:1, 0:512], tag="dn")
        B.recip(dn, dn)
        for j in range(4):
            t = 4 * b + j
            ops_ = pool.f32()
            B.mm_group(ops_[:, 0:128],
                       [(ET[:, 512 * k + 128 * j:512 * k + 128 * j + 128],
                         vb.tile(k)) for k in range(nblk)], tag="pv")
            dcol = pool.f32()
            B.transpose(dcol[:, 0:1], dn[0:1, 128 * j:128 * j + 128],
                        B.cs(1.0)[0:1, 0:1], tag="dtr")
            B.ts(ao.tile(t), ops_[:, 0:128], dcol[:, 0:1], None, ALU.mult,
                 tag="aoev")
    aoT = FMT(B, "aoT", 128, T_SEQ, BF16, ap=A.bf16(22528, 2048))
    tm_to_fm(B, ao, aoT, pool)
    facc = TMT(B, "facc", T_SEQ, D_MODEL, F32, ap=A.f32(0, 16384))
    ostage = B.sb("ostage", [128, 2 * 1024], F32)
    ow_sb = [B.sb(f"ow{i}", [128, 1024], BF16) for i in range(2)]
    for e in range(8):
        se = e % 2
        st = ostage[:, se * 1024:se * 1024 + 1024]
        B.dma(st, owd[e], ch=f"ow{se}")
        B.copy(ow_sb[se], st, tag="wconv")
        for t in range(NT):
            wc = wcols[:, 8 * t + e:8 * t + e + 1]
            for n0 in (0, 512):
                ps = pool.f32()
                B.mm_group(ps[:, 0:512],
                           [(aoT.chunk(0, t * 128, (t + 1) * 128),
                             ow_sb[se][:, n0:n0 + 512])], tag="oproj")
                if e == 0:
                    B.ts(facc.tile(t, n0, n0 + 512), ps[:, 0:512], wc, None,
                         ALU.mult, tag="oacc")
                else:
                    B.stt(facc.tile(t, n0, n0 + 512), ps[:, 0:512], wc,
                          facc.tile(t, n0, n0 + 512), ALU.mult, ALU.add,
                          tag="oacc")
    for t in range(NT):
        xt = xs2[:, (t % 2) * 1024:(t % 2) * 1024 + 1024]
        B.dma(xt, xind[t], ch=f"x{t % 2}")
        B.tt(facc.tile(t), facc.tile(t), xt, ALU.add, tag="resid")
        B.dma(xoutd[t], facc.tile(t), ch=f"xo{t % 2}", on="scalar")



def _rope_tabs_l1a():
    d = 64
    inv = 1.0 / (10000.0 ** (np.arange(0, d, 2) / d))
    fr = np.arange(T_SEQ)[:, None] * inv[None, :]
    cos = np.repeat(np.cos(fr), 2, axis=1).T
    sin = np.repeat(np.sin(fr), 2, axis=1).T
    cos2 = np.concatenate([cos, cos], 0)
    sin2 = np.concatenate([sin, sin], 0)
    return np.concatenate([cos2, sin2], 1).astype(BF16NP)


def _rot_mat_128():
    R = np.zeros((128, 128), np.float32)
    for j in range(64):
        R[2 * j, 2 * j + 1] = -1.0
        R[2 * j + 1, 2 * j] = 1.0
    return R.T.astype(BF16NP)


def prep_l1_attn(inp):
    f32 = np.float32
    pmk = np.asarray(inp["L1_attn_pm_k"], f32)
    pmv = np.asarray(inp["L1_attn_pm_v"], f32)
    pmkT = np.zeros((128, 128), f32)
    pmv_h = np.zeros((16, 1024), f32)
    for h in range(16):
        pmkT[64 * (h % 2):64 * (h % 2) + 64, 16 * (h // 2):16 * (h // 2) + 16] \
            = pmk[h].T
        pmv_h[:, 64 * h:64 * h + 64] = pmv[h]
    vecs = np.zeros((128, 8), f32)
    vecs[:] = np.asarray(inp["L1_attn_norm_w"], f32).reshape(8, 128).T
    rows = np.asarray(inp["L1_attn_b_out"], f32).reshape(1, 1024)
    return {
        "wqkv": np.asarray(inp["L1_attn_w_qkv"], f32),
        "wout": np.asarray(inp["L1_attn_w_out"], f32),
        "pmkT": pmkT, "pmv": pmv_h,
        "vecs": vecs, "rows": rows,
        "tabs": _rope_tabs_l1a(), "rotT": _rot_mat_128(),
        "masksd": _causal_masks_l0f(),
    }


def dram_l1a(nc, pfx=""):
    def dt_(n, shape, dt=F32):
        return nc.dram_tensor(pfx + n, shape, dt, kind="ExternalInput").ap()
    return dict(wqkvd=dt_("wqkv", [1024, 3072]),
                woutd=dt_("wout", [1024, 1024]),
                pmkTd=dt_("pmkT", [128, 128]), pmvd=dt_("pmv", [16, 1024]),
                vecsd=dt_("vecs", [128, 8]), rowsd=dt_("rows", [1, 1024]),
                tabsd=dt_("tabs", [128, 2 * T_SEQ], BF16),
                rotd=dt_("rotT", [128, 128], BF16),
                maskd=dt_("masksd", [128, 4096], BF16))


def body_l1a(B, pool, d, xin, xout):
    (wqkvd, woutd, pmkTd, pmvd, vecsd, rowsd, tabsd, rotd, maskd) = (
        d["wqkvd"], d["woutd"], d["pmkTd"], d["pmvd"], d["vecsd"],
        d["rowsd"], d["tabsd"], d["rotd"], d["maskd"])
    A = B.arena(43008)
    xs2 = A.f32(40960, 2048)
    xind = xin.rearrange("(t p) c -> t p c", p=128)
    xoutd = xout.rearrange("(t p) c -> t p c", p=128)
    vecs = load_vec_cols(B, "l1a", vecsd, 8)
    tabs = B.sb("tabs_sb", [128, 2 * T_SEQ], BF16)
    B.dma(tabs, tabsd, ch="tabs", batch=True)
    rott = B.sb("rott", [128, 128], BF16)
    B.dma(rott, rotd, ch="rot", batch=True)
    masks = B.sb("masks", [128, 4096], BF16)
    B.dma(masks, maskd, ch="masks", batch=True)
    stage = B.sb("stage_l1a", [128, 1024], F32)
    pmk_sb = B.sb("pmk", [128, 128], BF16)
    B.dma(stage[:, 0:128], pmkTd, ch="pmk", batch=True)
    B.copy(pmk_sb, stage[:, 0:128], tag="wconv")
    pmv_sb = B.sb("pmv_sbuf", [16, 1024], BF16)
    B.dma(stage[0:16, 0:1024], pmvd, ch="pmv", batch=True)
    B.copy(pmv_sb, stage[0:16, 0:1024], tag="wconv")

    xn = TMT(B, "xn", T_SEQ, D_MODEL, BF16, ap=A.bf16(0, 16384))
    cols = B.sb("cols", [128, 8 * NT], F32)
    rsc = B.sb("rsc", [128, 2 * 512], F32)
    sq = rsc[:, 0:1024]
    for t in range(NT):
        xt = xs2[:, (t % 2) * 1024:(t % 2) * 1024 + 1024]
        B.dma(xt, xind[t], ch=f"x{t % 2}")
        c8 = cols[:, 8 * t:8 * t + 8]
        negmu, vs, std, rstd = (c8[:, 0:1], c8[:, 1:2], c8[:, 2:3], c8[:, 3:4])
        t1, rmsstd, rmsr, comb = (c8[:, 4:5], c8[:, 5:6], c8[:, 6:7],
                                  c8[:, 7:8])
        B.red(negmu, xt, ALU.add, negate=True, tag="ln")
        B.ts(negmu, negmu, 1.0 / 1024, None, ALU.mult, tag="ln")
        B.act(sq, xt, AF.Square, bias=negmu, accum_out=vs, tag="ln")
        B.act(std, vs, AF.Sqrt, bias=B.cs(1e-5), scale=1.0 / 1024, tag="ln")
        B.recip(rstd, std)
        B.tt(t1, rstd, rstd, ALU.mult, tag="rms")
        B.tt(t1, t1, vs, ALU.mult, tag="rms")
        B.act(rmsstd, t1, AF.Sqrt, bias=B.cs(1e-6), scale=1.0 / 1024,
              tag="rms")
        B.recip(rmsr, rmsstd)
        B.tt(comb, rstd, rmsr, ALU.mult, tag="rms")
        B.ts(xn.tile(t), xt, negmu, comb, ALU.add, ALU.mult, tag="ln")
    xnT = FMT(B, "xnT", D_MODEL, T_SEQ, BF16, ap=A.bf16(8192, 16384))
    tm_to_fm(B, xn, xnT, pool)

    v = TMT(B, "v", T_SEQ, D_MODEL, BF16, ap=A.bf16(32768, 16384))
    wvstage = stage
    Wv = FMT(B, "Wv", 1024, 1024, BF16, ap=A.bf16(0, 8192))
    for k in range(8):
        st = wvstage[:, 0:1024]
        B.dma(st, wqkvd[128 * k:128 * (k + 1), 2048:3072], ch=f"wv{k % 2}")
        B.ts(Wv.chunk(k), st, vecs[:, k:k + 1], None, ALU.mult, tag="wconv")
    for t in range(NT):
        for n0 in (0, 512):
            ps = mmA(B, pool, xnT, Wv, t, n0, 512, tag="vmm")
            B.copy(v.tile(t, n0, n0 + 512), ps, tag="vmm")

    qT = FMT(B, "qT", D_MODEL, T_SEQ, BF16, ap=A.bf16(16384, 16384))
    kT = FMT(B, "kT", D_MODEL, T_SEQ, BF16, ap=A.bf16(24576, 16384))
    qraw = [A.bf16(40960 + i * 1024, 2048) for i in range(2)]
    wqstage = wvstage
    wqb = [FMT(B, f"wqb{i}", 1024, 128, BF16) for i in range(2)]
    for m in range(16):
        sm = m % 2
        st = wqstage[:, 0:1024]
        B.dma(st.rearrange("p (k n) -> p k n", k=8),
              wqkvd[:, 128 * m:128 * (m + 1)]
              .rearrange("(k p) n -> p k n", p=128), ch=f"wq{sm}")
        for k in range(8):
            if m < 8:
                B.ts(wqb[sm].chunk(k), st[:, 128 * k:128 * k + 128],
                     vecs[:, k:k + 1], 0.125, ALU.mult, ALU.mult, tag="wconv")
            else:
                B.ts(wqb[sm].chunk(k), st[:, 128 * k:128 * k + 128],
                     vecs[:, k:k + 1], None, ALU.mult, tag="wconv")
        dst = qT if m < 8 else kT
        mc = m % 8
        for t0 in range(0, T_SEQ, 512):
            ps = mmB(B, pool, wqb[sm], xnT, 0, t0, 512, tag="qk")
            B.copy(qraw[sm][:, t0:t0 + 512], ps, tag="qk")
        for t0 in range(0, T_SEQ, 512):
            pr = pool.f32()
            B.mm_group(pr[:, 0:512], [(rott, qraw[sm][:, t0:t0 + 512])],
                       tag="rope")
            s1 = rsc[:, 0:512]
            s2 = rsc[:, 512:1024]
            B.tt(s1, qraw[sm][:, t0:t0 + 512], tabs[:, t0:t0 + 512],
                 ALU.mult, tag="rope")
            B.tt(s2, pr[:, 0:512], tabs[:, T_SEQ + t0:T_SEQ + t0 + 512],
                 ALU.mult, tag="rope")
            B.tt(dst.chunk(mc, t0, t0 + 512), s1, s2, ALU.add, tag="rope")

    outT = FMT(B, "outT", D_MODEL, T_SEQ, BF16, ap=A.bf16(0, 16384))
    ET = A.bf16(40960, 4 * 512)
    ETpm = A.bf16(41984, 512)[0:16, :]
    rm = B.sb("rm", [128, 2 * NT], F32)
    rmb = B.sb("rmb", [128, 1], BF16)
    rmrow = B.sb("rmrow", [1, 512], BF16)
    dn = B.sb("dnrow", [1, 512], F32)
    aotk = B.sb("aotk", [128, 64], BF16)
    for seg in range(4):
        for h in range(16):
            hc, base = h // 2, 64 * (h % 2)
            qrows = lambda t0, t1: qT.sb[base:base + 64,
                                         hc * T_SEQ + t0:hc * T_SEQ + t1]
            krows = lambda t0, t1: kT.sb[base:base + 64,
                                         hc * T_SEQ + t0:hc * T_SEQ + t1]
            s0 = seg * 512
            for j in range(4):
                t = 4 * seg + j
                pp = pool.f32()
                B.mm_group(pp[:, 0:16],
                           [(qrows(s0 + 128 * j, s0 + 128 * (j + 1)),
                             pmk_sb[base:base + 64, 16 * hc:16 * hc + 16])],
                           tag="s1pm")
                ps = pool.f32()
                B.mm_group(ps[:, 0:512],
                           [(qrows(s0 + 128 * j, s0 + 128 * (j + 1)),
                             krows(s0, s0 + 512))], tag="s1")
                B.tt(ps[:, 0:512], ps[:, 0:512],
                     masks[:, 512 * j:512 * j + 512], ALU.add, tag="s1m")
                rmc = rm[:, 2 * t:2 * t + 1]
                rs2 = rm[:, 2 * t + 1:2 * t + 2]
                B.red(rmc, pp[:, 0:16], ALU.max, tag="s1r")
                B.red(rs2, ps[:, 0:512], ALU.max, tag="s1r")
                B.tt(rmc, rmc, rs2, ALU.max, tag="s1r")
            for j in range(4):
                t = 4 * seg + j
                B.ts(rmb, rm[:, 2 * t:2 * t + 1], -1.0, None, ALU.mult,
                     tag="rmneg")
                bank = pool.bf16()
                B.transpose(bank[0:1, 0:128], rmb, B.ident, tag="rmtr")
                B.copy(rmrow[0:1, 128 * j:128 * j + 128], bank[0:1, 0:128],
                       tag="rmtr")
            pe = pool.f32()
            B.mm_group(pe[0:16, 0:512],
                       [(pmk_sb[base:base + 64, 16 * hc:16 * hc + 16],
                         qrows(s0, s0 + 512)),
                        (B.onesb[0:1, 0:16], rmrow[0:1, 0:512])], tag="s2pm")
            B.act(ETpm, pe[0:16, 0:512], AF.Exp, tag="s2e")
            for jt in range(4):
                ps = pool.f32()
                B.mm_group(ps[:, 0:512],
                           [(krows(s0 + 128 * jt, s0 + 128 * (jt + 1)),
                             qrows(s0, s0 + 512)),
                            (B.onesb[0:1, 0:128], rmrow[0:1, 0:512])],
                           tag="s2")
                B.tt(ps[:, 0:512], ps[:, 0:512],
                     masks[:, 2048 + 512 * jt:2048 + 512 * jt + 512],
                     ALU.add, tag="s2m")
                B.act(ET[:, 512 * jt:512 * jt + 512], ps[:, 0:512], AF.Exp,
                      tag="s2e")
            dps = pool.f32()
            B.mm_group(dps[0:1, 0:512],
                       [(B.onesb[0:16, 0:1], ETpm)] +
                       [(B.onesb[:, 0:1], ET[:, 512 * k:512 * k + 512])
                        for k in range(4)], tag="dn")
            B.copy(dn, dps[0:1, 0:512], tag="dn")
            B.recip(dn, dn)
            for j in range(4):
                t = 4 * seg + j
                ops_ = pool.f32()
                B.mm_group(ops_[:, 0:64],
                           [(ETpm[:, 128 * j:128 * j + 128],
                             pmv_sb[:, 64 * h:64 * h + 64])] +
                           [(ET[:, 512 * k + 128 * j:512 * k + 128 * j + 128],
                             v.tile(4 * seg + k, 64 * h, 64 * h + 64))
                            for k in range(4)], tag="pv")
                dcol = pool.f32()
                B.transpose(dcol[:, 0:1], dn[0:1, 128 * j:128 * j + 128],
                            B.cs(1.0)[0:1, 0:1], tag="dtr")
                B.ts(aotk, ops_[:, 0:64], dcol[:, 0:1], None, ALU.mult,
                     tag="aoev")
                bank = pool.bf16()
                B.transpose(bank[base:base + 64, 0:128], aotk, B.ident,
                            tag="aotr")
                B.copy(outT.sb[base:base + 64,
                               hc * T_SEQ + 128 * t:hc * T_SEQ + 128 * t + 128],
                       bank[base:base + 64, 0:128], tag="aotr")
    wostage = wvstage
    Wo = FMT(B, "Wo", 1024, 1024, BF16, ap=A.bf16(16384, 8192))
    rows = A.row(24576, 1024)
    B.dma(rows, rowsd, ch="rows_l1a", batch=True)
    for k in range(8):
        st = wostage[:, 0:1024]
        B.dma(st, woutd[128 * k:128 * (k + 1), :], ch=f"wo{k % 2}")
        B.copy(Wo.chunk(k), st, tag="wconv")
    res = A.f32(32768, 2048)
    for t in range(NT):
        xt = xs2[:, (t % 2) * 1024:(t % 2) * 1024 + 1024]
        rt = res[:, (t % 2) * 1024:(t % 2) * 1024 + 1024]
        B.dma(xt, xind[t], ch=f"x{t % 2}")
        for n0 in (0, 512):
            ps = mmA(B, pool, outT, Wo, t, n0, 512,
                     bias_row=rows[0:1, 0:1024], tag="womm")
            B.tt(rt[:, n0:n0 + 512], ps, xt[:, n0:n0 + 512], ALU.add,
                 tag="resid")
        B.dma(xoutd[t], rt, ch=f"xo{t % 2}", on="scalar")



def _local_masks(win, span):
    ntw, nts = win // 128, span // 128
    p = np.arange(128)
    out = []
    for jq in range(ntw):
        gi = (span - win) + 128 * jq + p[:, None]
        gj = np.arange(span)[None, :]
        out.append(np.where((gj <= gi) & (gi <= gj + win), 0.0, -1e30))
    for jk in range(nts):
        gj = 128 * jk + p[:, None]
        gi = (span - win) + np.arange(win)[None, :]
        out.append(np.where((gj <= gi) & (gi <= gj + win), 0.0, -1e30))
    return np.concatenate(out, 1).astype(BF16NP)


def _xpos_tabs(win):
    d = 64
    half = np.arange(0, d, 2).astype(np.float64)
    invf = 1.0 / 10000.0 ** (half / d)
    t = np.arange(T_SEQ, dtype=np.float64)
    fr = t[:, None] * invf[None, :]
    fr2 = np.concatenate([fr, fr], 1)
    base = (half + 0.4 * d) / (1.4 * d)
    power = (t - T_SEQ // 2) / (win // 2)
    sc = base[None, :] ** power[:, None]
    sc2 = np.concatenate([sc, sc], 1)
    sign = np.ones(64)
    sign[:32] = -1.0
    cq = np.cos(fr2) * sc2 * 8.0
    sq = np.sin(fr2) * sc2 * 8.0 * sign[None, :]
    ck = np.cos(fr2) / sc2
    sk = np.sin(fr2) / sc2 * sign[None, :]
    return np.concatenate([cq, sq, ck, sk], 1).astype(np.float32)


def _relay_w1(w1, b1):
    hid = w1.shape[1] // 2
    nkc = (hid + 127) // 128
    wout = np.zeros((w1.shape[0], 256 * nkc), np.float32)
    b1a = np.zeros((128, nkc), np.float32)
    b1g = np.zeros((128, nkc), np.float32)
    for j in range(nkc):
        r = min(128, hid - 128 * j)
        wout[:, 256 * j:256 * j + r] = w1[:, 128 * j:128 * j + r]
        wout[:, 256 * j + 128:256 * j + 128 + r] = \
            w1[:, hid + 128 * j:hid + 128 * j + r]
        b1a[:r, j] = b1[128 * j:128 * j + r]
        b1g[:r, j] = b1[hid + 128 * j:hid + 128 * j + r]
    return wout, b1a, b1g


def prep_local_stack(pref, inp, nlayers, win):
    f32 = np.float32
    names = {"L1": dict(wqkv="L1_ffn_wqkv", qs="L1_ffn_q_scale",
                        ks="L1_ffn_k_scale", wo="L1_ffn_wo", lg="L1_ffn_ln_g",
                        lb="L1_ffn_ln_b", w1="L1_ffn_w1", b1="L1_ffn_b1",
                        w2="L1_ffn_w2", b2="L1_ffn_b2"),
             "L3": dict(wqkv="L3_ffn_Wqkv", qs="L3_ffn_q_scale",
                        ks="L3_ffn_k_scale", wo="L3_ffn_Wo", lg="L3_ffn_ln_g",
                        lb="L3_ffn_ln_b", w1="L3_ffn_W1", b1="L3_ffn_b1",
                        w2="L3_ffn_W2", b2="L3_ffn_b2")}[pref]
    out = {"tabs": _xpos_tabs(win), "masksd": _local_masks(win, 2 * win)}
    hid = np.asarray(inp[names["w1"]], f32).shape[2] // 2
    nkc = (hid + 127) // 128
    w1r = np.zeros((nlayers, 1024, 256 * nkc), f32)
    b1a = np.zeros((nlayers, 128, nkc), f32)
    b1g = np.zeros((nlayers, 128, nkc), f32)
    for l in range(nlayers):
        w1r[l], b1a[l], b1g[l] = _relay_w1(
            np.asarray(inp[names["w1"]][l], f32),
            np.asarray(inp[names["b1"]][l], f32))
    qs = np.asarray(inp[names["qs"]], f32)
    ks = np.asarray(inp[names["ks"]], f32)
    qkrows = np.zeros((1, nlayers * 128), f32)
    for l in range(nlayers):
        qkrows[0, 128 * l:128 * l + 64] = qs[l]
        qkrows[0, 128 * l + 64:128 * l + 128] = ks[l]
    lglb = np.zeros((1, nlayers * 2048), f32)
    b2r = np.zeros((1, nlayers * 1024), f32)
    for l in range(nlayers):
        lglb[0, 2048 * l:2048 * l + 1024] = np.asarray(inp[names["lg"]][l])
        lglb[0, 2048 * l + 1024:2048 * l + 2048] = \
            np.asarray(inp[names["lb"]][l])
        b2r[0, 1024 * l:1024 * (l + 1)] = np.asarray(inp[names["b2"]][l])
    out.update({
        "wqkv": np.asarray(inp[names["wqkv"]], f32),
        "wo": np.asarray(inp[names["wo"]], f32),
        "w1r": w1r, "b1a": b1a, "b1g": b1g,
        "w2": np.asarray(inp[names["w2"]], f32),
        "qkrows": qkrows, "lglb": lglb, "b2r": b2r,
    })
    return out


class Cursor:
    def __init__(self, arena, start=0):
        self.A = arena
        self.c = start

    def f32(self, n):
        ap = self.A.f32(self.c, n)
        self.c += n
        return ap

    def bf16(self, n):
        f = (n + 1) // 2
        ap = self.A.bf16(self.c, n)
        self.c += f
        return ap


def dram_local(nc, nlayers, heads, win, pfx=""):
    H = heads
    DQ = 64 * H
    span = 2 * win
    ntw, nts = win // 128, span // 128
    NKC = 22
    HID = 2730

    def dt_(n, shape, dt=F32):
        return nc.dram_tensor(pfx + n, shape, dt, kind="ExternalInput").ap()
    return dict(wqkvd=dt_("wqkv", [nlayers, 1024, 3 * DQ]),
                wod=dt_("wo", [nlayers, DQ, 1024]),
                w1d=dt_("w1r", [nlayers, 1024, 256 * NKC]),
                b1ad=dt_("b1a", [nlayers, 128, NKC]),
                b1gd=dt_("b1g", [nlayers, 128, NKC]),
                w2d=dt_("w2", [nlayers, HID, 1024]),
                qkrd=dt_("qkrows", [1, nlayers * 128]),
                lglbd=dt_("lglb", [1, nlayers * 2048]),
                b2rd=dt_("b2r", [1, nlayers * 1024]),
                tabsd=dt_("tabs", [T_SEQ, 256]),
                maskd=dt_("masksd", [128, ntw * span + nts * win], BF16))


def body_local(B, pool, d, xin, xout, nlayers, heads, win):
    H = heads
    DQ = 64 * H
    NDQ = DQ // 128
    span = 2 * win
    ntw, nts = win // 128, span // 128
    nwin = T_SEQ // win
    HID = 2730
    NKC = 22
    (wqkvd, wod, w1d, b1ad, b1gd, w2d, qkrd, lglbd, b2rd, tabsd, maskd) = (
        d["wqkvd"], d["wod"], d["w1d"], d["b1ad"], d["b1gd"], d["w2d"],
        d["qkrd"], d["lglbd"], d["b2rd"], d["tabsd"], d["maskd"])
    xind = xin.rearrange("(t p) c -> t p c", p=128)
    xoutd = xout.rearrange("(t p) c -> t p c", p=128)

    xi = TMT(B, "xi", T_SEQ, D_MODEL, F32)
    attn_f = (24 * DQ + NDQ * 1024 + 2 * DQ + 2 * NDQ * win
              + NDQ * nts * 128 + nts * DQ + nts * win + 2) // 2
    ffn_f = ((16384 + NKC * 512 + 2 * NKC * 128 + 2 * 8 * 256
              + 1024 + 512 + 512 + 2) // 2
             + 2 * 1024 + 11 * NT + 64)
    A = B.arena(max(attn_f, ffn_f) + 16)
    ca = Cursor(A)
    wqkv = FMT(B, None, 1024, 3 * DQ, BF16, ap=ca.bf16(8 * 3 * DQ))
    wosb = FMT(B, None, DQ, 1024, BF16, ap=ca.bf16(NDQ * 1024))
    qkl = ca.bf16(2 * DQ)
    qTf = FMT(B, None, DQ, win, BF16, ap=ca.bf16(NDQ * win))
    kTf = FMT(B, None, DQ, nts * 128, BF16, ap=ca.bf16(NDQ * nts * 128))
    vbr = [ca.bf16(DQ) for _ in range(nts)]
    outTw = FMT(B, None, DQ, win, BF16, ap=ca.bf16(NDQ * win))
    ET = ca.bf16(nts * win)
    attn_top = ca.c
    cf = Cursor(A)
    h2nT = FMT(B, None, D_MODEL, T_SEQ, BF16, ap=cf.bf16(8 * T_SEQ))
    uT = FMT(B, None, HID, 512, BF16, ap=cf.bf16(NKC * 512))
    w2b = [FMT(B, None, HID, 128, BF16, ap=cf.bf16(NKC * 128))
           for _ in range(2)]
    lgb = cf.f32(1024)
    lbb = cf.f32(1024)
    yTs = cf.bf16(512)
    h2n = cf.bf16(1024)
    w1b = [FMT(B, None, 1024, 256, BF16, ap=cf.bf16(8 * 256))
           for _ in range(2)]
    glscr = cf.bf16(512)
    fcols = cf.f32(11 * NT)
    b1ac = cf.f32(NKC)
    b1gc = cf.f32(NKC)
    b2c = cf.f32(8)
    edgec = cf.f32(5)
    assert max(attn_top, cf.c) <= A.cols, (attn_top, cf.c, A.cols)

    tabs = TMT(B, "tabs_sb", T_SEQ, 256, F32)
    B.dma(tabs.sb.rearrange("p (t c) -> p t c", t=NT),
          tabsd.rearrange("(t p) c -> p t c", p=128), ch="tabs", batch=True)
    masks = B.sb("masks_sb", [128, ntw * span + nts * win], BF16)
    B.dma(masks, maskd, ch="masks", batch=True)

    def mask1(jq):
        return masks[:, jq * span:(jq + 1) * span]

    def mask2(jk):
        return masks[:, ntw * span + jk * win:ntw * span + (jk + 1) * win]

    qkrows = load_rows(B, "qkr", qkrd, nlayers * 128)
    cols = B.sb("cols", [128, 4 * NT], F32)
    stage = B.sb("stage", [128, 3072], F32)
    qn = stage[:, 0:2 * DQ]
    sq = stage[:, 2048:3072]
    rawT = B.sb("rawT", [128, 8 * 128], BF16)
    bnd = B.sb("bnd", [128, 4], BF16)
    htu = B.sb("htu", [128, 4 * 128], BF16)
    xb = B.sb("xbscr", [128, 1024], BF16)
    qp = xb
    rstdh = B.sb("rstdh", [128, H], F32)
    qsb = B.sb("qsb", [128, 128], F32)
    rm = B.sb("rm", [128, 2], F32)
    Etok = B.sb("Etok", [128, span], BF16)
    dprt = B.sb("dprt", [128, 2], F32)
    dcl = B.sb("dcl", [128, ntw], F32)
    aotk = B.sb("aotk", [128, 64], BF16)

    for t in range(NT):
        xt = stage[:, (t % 2) * 1024:(t % 2) * 1024 + 1024]
        B.dma(xt, xind[t], ch=f"x{t % 2}")
        pre_ln_tile(B, xt, xi.tile(t), cols[:, 4 * t:4 * t + 4], sq)

    def hview(ap_):
        return ap_.rearrange("p (h d) -> p h d", h=H)

    def l2norm_xpos(t, src, dst, tc, tsn, qk):
        B.act(qn[:, 0:DQ], src, AF.Square, tag="l2")
        B.red(rstdh, hview(qn[:, 0:DQ]), ALU.add, tag="l2")
        B.act(rstdh, rstdh, AF.Sqrt, tag="l2")
        B.ts(rstdh, rstdh, 1e-12, None, ALU.max, tag="l2")
        B.recip(rstdh, rstdh)
        rb = rstdh[:, :].unsqueeze(2).broadcast_to([128, H, 64])
        B.tt(hview(qn[:, 0:DQ]), hview(src), rb, ALU.mult, tag="l2")
        qsv = qsb[:, qk * 64:qk * 64 + 64].unsqueeze(1).broadcast_to(
            [128, H, 64])
        B.tt(hview(qn[:, DQ:2 * DQ]), hview(qn[:, 0:DQ]), qsv, ALU.mult,
             tag="l2")
        cosv = tabs.tile(t, tc, tc + 64).unsqueeze(1).broadcast_to(
            [128, H, 64])
        sinv = tabs.tile(t, tsn, tsn + 64).unsqueeze(1).broadcast_to(
            [128, H, 64])
        qn2 = qn[:, DQ:2 * DQ]
        swv = qn2.rearrange("p (h two j) -> p h two j", h=H,
                            two=2)[:, :, ::-1, :]
        B.tt(hview(qn[:, 0:DQ]), hview(qn2), cosv, ALU.mult, tag="xpos")
        B.tt(hview(dst), swv,
             sinv.rearrange("p h (two j) -> p h two j", two=2), ALU.mult,
             tag="xpos")
        B.tt(dst, qn[:, 0:DQ], dst, ALU.add, tag="xpos")

    def attn_layer(l):
        for k in range(8):
            st = stage[:, 0:3 * DQ]
            B.dma(st, wqkvd[l, 128 * k:128 * (k + 1), :], ch="wq")
            B.copy(wqkv.chunk(k), st, tag="wconv")
        for k in range(NDQ):
            st = stage[:, 0:1024]
            B.dma(st, wod[l, 128 * k:128 * (k + 1), :], ch="wod")
            B.copy(wosb.chunk(k), st, tag="wconv")
        pqs = pool.f32()
        B.mm_group(pqs[:, 0:128], [(B.onesf[0:1, 0:128],
                                    qkrows[0:1, 128 * l:128 * (l + 1)])],
                   tag="qsb")
        B.copy(qsb, pqs[:, 0:128], tag="qsb")
        for w in range(nwin):
            for jt in range(ntw):
                t = w * ntw + jt
                B.copy(xb, xi.tile(t), tag="xb")
                bank = pool.bf16()
                for k in range(8):
                    B.transpose(bank[:, 128 * k:128 * (k + 1)],
                                xb[:, 128 * k:128 * (k + 1)], B.ident,
                                tag="xtr")
                if t > 0:
                    for k in range(4):
                        B.copy(htu[:, 128 * k:128 * k + 1],
                               bnd[:, k:k + 1], tag="sh")
                else:
                    B.memset(htu[:, 0:1], 0.0, tag="sh")
                    B.memset(htu[:, 128:129], 0.0, tag="sh")
                    B.memset(htu[:, 256:257], 0.0, tag="sh")
                    B.memset(htu[:, 384:385], 0.0, tag="sh")
                B.copy(rawT, bank[:, 0:1024], tag="xtrev")
                for k in range(4):
                    B.copy(htu[:, 128 * k + 1:128 * k + 128],
                           rawT[:, 128 * (4 + k):128 * (4 + k) + 127],
                           tag="sh")
                    B.copy(bnd[:, k:k + 1],
                           rawT[:, 128 * (4 + k) + 127:128 * (4 + k) + 128],
                           tag="sh")
                for n0 in range(0, 3 * DQ, 512):
                    nn = min(512, 3 * DQ - n0)
                    bankp = pool.f32()
                    pairs = []
                    for k in range(8):
                        lhs = (rawT[:, 128 * k:128 * (k + 1)] if k < 4
                               else htu[:, 128 * (k - 4):128 * (k - 3)])
                        pairs.append((lhs, wqkv.chunk(k, n0, n0 + nn)))
                    B.mm_group(bankp[:, 0:nn], pairs, tag="qkvmm")
                    lo, hi = n0, n0 + nn
                    if lo < 2 * DQ:
                        c = min(hi, 2 * DQ) - lo
                        B.copy(qkl[:, lo:lo + c], bankp[:, 0:c], tag="qkvev")
                    if hi > 2 * DQ:
                        c0 = max(lo, 2 * DQ)
                        B.copy(vbr[t % nts][:, c0 - 2 * DQ:hi - 2 * DQ],
                               bankp[:, c0 - lo:nn], tag="qkvev")
                l2norm_xpos(t, qkl[:, 0:DQ], qp[:, 0:DQ], 0, 64, 0)
                for g0 in range(NDQ):
                    bank = pool.bf16()
                    B.transpose(bank[:, 0:128],
                                qp[:, 128 * g0:128 * (g0 + 1)], B.ident,
                                tag="qtr")
                    B.copy(qTf.chunk(g0, jt * 128, (jt + 1) * 128),
                           bank[:, 0:128], tag="qtrev")
                l2norm_xpos(t, qkl[:, DQ:2 * DQ], qp[:, 0:DQ], 128, 192, 1)
                for g0 in range(NDQ):
                    bank = pool.bf16()
                    B.transpose(bank[:, 0:128],
                                qp[:, 128 * g0:128 * (g0 + 1)], B.ident,
                                tag="ktr")
                    B.copy(kTf.chunk(g0, (t % nts) * 128,
                                     (t % nts) * 128 + 128),
                           bank[:, 0:128], tag="ktrev")
            stiles = list(range(max(0, (w - 1) * ntw), (w + 1) * ntw))
            jk0 = nts - len(stiles)
            nst = len(stiles)
            for h in range(H):
                hc, base = h // 2, 64 * (h % 2)

                def qr(c0, c1):
                    return qTf.sb[base:base + 64, hc * win + c0:hc * win + c1]

                def kr(ti):
                    sl = (ti % nts) * 128
                    return kTf.sb[base:base + 64,
                                  hc * (nts * 128) + sl:
                                  hc * (nts * 128) + sl + 128]
                for jq in range(ntw):
                    nchunks = (nst * 128 + 511) // 512
                    pss = []
                    for r0 in range(0, nst, 4):
                        grp = stiles[r0:r0 + 4]
                        ps = pool.f32()
                        pss.append((ps, r0, len(grp)))
                        for gi, ti in enumerate(grp):
                            B.mm_group(ps[:, gi * 128:(gi + 1) * 128],
                                       [(qr(jq * 128, (jq + 1) * 128),
                                         kr(ti))], tag="s1")
                        mv = mask1(jq)[:, (jk0 + r0) * 128:
                                       (jk0 + r0 + len(grp)) * 128]
                        B.tt(ps[:, 0:len(grp) * 128],
                             ps[:, 0:len(grp) * 128], mv, ALU.add, tag="s1m")
                        if r0 == 0:
                            B.red(rm[:, 0:1], ps[:, 0:len(grp) * 128],
                                  ALU.max, negate=True, tag="s1r")
                        else:
                            B.red(rm[:, 1:2], ps[:, 0:len(grp) * 128],
                                  ALU.max, negate=True, tag="s1r")
                            B.tt(rm[:, 0:1], rm[:, 0:1], rm[:, 1:2],
                                 ALU.min, tag="s1r")
                    for pi, (ps, r0, ng) in enumerate(pss):
                        B.act(Etok[:, r0 * 128:(r0 + ng) * 128],
                              ps[:, 0:ng * 128], AF.Exp, bias=rm[:, 0:1],
                              accum_out=dprt[:, pi:pi + 1], tag="s2e")
                    if len(pss) > 1:
                        B.tt(dprt[:, 0:1], dprt[:, 0:1], dprt[:, 1:2],
                             ALU.add, tag="dn")
                    B.recip(dcl[:, jq:jq + 1], dprt[:, 0:1])
                    for r in range(nst):
                        bank = pool.bf16()
                        B.transpose(bank[:, 0:128],
                                    Etok[:, r * 128:(r + 1) * 128], B.ident,
                                    tag="etr")
                        B.copy(ET[:, r * win + 128 * jq:
                                  r * win + 128 * jq + 128],
                               bank[:, 0:128], tag="etr")
                for jq in range(ntw):
                    ops_ = pool.f32()
                    B.mm_group(ops_[:, 0:64],
                               [(ET[:, r * win + 128 * jq:
                                    r * win + 128 * jq + 128],
                                 vbr[ti % nts][:, 64 * h:64 * h + 64])
                                for r, ti in enumerate(stiles)], tag="pv")
                    B.ts(aotk, ops_[:, 0:64], dcl[:, jq:jq + 1], None,
                         ALU.mult, tag="aoev")
                    bank = pool.bf16()
                    B.transpose(bank[base:base + 64, 0:128], aotk, B.ident,
                                tag="aotr")
                    B.copy(outTw.sb[base:base + 64,
                                    hc * win + 128 * jq:
                                    hc * win + 128 * jq + 128],
                           bank[base:base + 64, 0:128], tag="aotr")
            for jt in range(ntw):
                t = w * ntw + jt
                for n0 in (0, 512):
                    ps = pool.f32()
                    B.mm_group(ps[:, 0:512],
                               [(outTw.chunk(k, jt * 128, (jt + 1) * 128),
                                 wosb.chunk(k, n0, n0 + 512))
                                for k in range(NDQ)], tag="womm")
                    B.tt(xi.tile(t, n0, n0 + 512), ps[:, 0:512],
                         xi.tile(t, n0, n0 + 512), ALU.add, tag="wores")

    def fc(stat, sl=None):
        if sl is None:
            return fcols[:, stat * NT:(stat + 1) * NT]
        return fcols[:, stat * NT + sl:stat * NT + sl + 1]

    def ffn_layer(l):
        B.dma(stage[0:1, 0:2048], lglbd[0:1, 2048 * l:2048 * (l + 1)],
              ch="lglb")
        for dst, off in ((lgb, 0), (lbb, 1024)):
            for n0 in (0, 512):
                pb = pool.f32()
                B.mm_group(pb[:, 0:512],
                           [(B.onesf[0:1, 0:128],
                             stage[0:1, off + n0:off + n0 + 512])],
                           tag="lgbc")
                B.copy(dst[:, n0:n0 + 512], pb[:, 0:512], tag="lgbc")
        B.dma(stage[0:1, 2048:3072], b2rd[0:1, 1024 * l:1024 * (l + 1)],
              ch="b2r")
        for k in range(8):
            pb = pool.f32()
            B.transpose(pb[:, 0:1],
                        stage[0:1, 2048 + 128 * k:2048 + 128 * (k + 1)],
                        B.cs(1.0)[0:1, 0:1], tag="b2c")
            B.copy(b2c[:, k:k + 1], pb[:, 0:1], tag="b2c")
        B.dma(b1ac, b1ad[l], ch="b1a")
        B.dma(b1gc, b1gd[l], ch="b1g")
        for t in range(NT):
            B.red(fc(0, t), xi.tile(t, 0, 512), ALU.add, tag="fst")
            B.act(sq[:, 0:512], xi.tile(t, 0, 512), AF.Square,
                  accum_out=fc(1, t), tag="fst")
            B.red(fc(2, t), xi.tile(t, 512, 1024), ALU.add, tag="fst")
            B.act(sq[:, 0:512], xi.tile(t, 512, 1024), AF.Square,
                  accum_out=fc(3, t), tag="fst")
        for cc, dc in ((2, 4), (3, 5)):
            B.dma(fc(dc)[0:1, 0:1], B.cs(0.0)[0:1, 0:1], ch="fsh",
                  batch=True)
            B.dma(fc(dc)[1:128, :], fc(cc)[0:127, :], ch="fsh", batch=True)
            B.dma(fc(dc)[0:1, 1:NT], fc(cc)[127:128, 0:NT - 1], ch="fsh", batch=True)
        B.tt(fc(6), fc(0), fc(4), ALU.add, tag="fst")
        B.ts(fc(6), fc(6), -1.0 / 1024, None, ALU.mult, tag="fst")
        B.tt(fc(7), fc(1), fc(5), ALU.add, tag="fst")
        B.ts(fc(7), fc(7), 1.0 / 1024, None, ALU.mult, tag="fst")
        B.tt(fc(10), fc(6), fc(6), ALU.mult, tag="fst")
        B.tt(fc(7), fc(7), fc(10), ALU.subtract, tag="fst")
        B.act(fc(7), fc(7), AF.Sqrt, bias=B.cs(1e-5), tag="fst")
        B.recip(fc(7), fc(7))
        for cc, dc in ((6, 8), (7, 9)):
            B.dma(fc(dc)[0:127, :], fc(cc)[1:128, :], ch="fsh", batch=True)
            B.dma(fc(dc)[127:128, 0:NT - 1], fc(cc)[0:1, 1:NT], ch="fsh",
                  batch=True)
            B.dma(fc(dc)[127:128, NT - 1:NT], B.cs(0.0)[0:1, 0:1], ch="fsh",
                  batch=True)
        pb1 = pool.f32()
        B.mm_group(pb1[:, 0:1], [(B.onesf[0:1, 0:128],
                                  fcols[0:1, 6 * NT:6 * NT + 1])], tag="edge")
        B.copy(edgec[:, 0:1], pb1[:, 0:1], tag="edge")
        pb2 = pool.f32()
        B.mm_group(pb2[:, 0:1], [(B.onesf[0:1, 0:128],
                                  fcols[0:1, 7 * NT:7 * NT + 1])], tag="edge")
        B.copy(edgec[:, 1:2], pb2[:, 0:1], tag="edge")
        B.tt(edgec[:, 2:3], edgec[:, 0:1], edgec[:, 1:2], ALU.mult,
             tag="edge")
        for t in range(NT):
            B.ts(sq[:, 0:512], xi.tile(t, 0, 512), fc(6, t), fc(7, t),
                 ALU.add, ALU.mult, tag="f2")
            B.tt(sq[:, 0:512], sq[:, 0:512], lgb[:, 0:512], ALU.mult,
                 tag="f2")
            B.tt(h2n[:, 0:512], sq[:, 0:512], lbb[:, 0:512], ALU.add,
                 tag="f2")
            B.ts(sq[:, 512:1024], xi.tile(t, 512, 1024), fc(8, t), fc(9, t),
                 ALU.add, ALU.mult, tag="f2")
            B.tt(sq[:, 512:1024], sq[:, 512:1024], lgb[:, 512:1024],
                 ALU.mult, tag="f2")
            B.tt(h2n[:, 512:1024], sq[:, 512:1024], lbb[:, 512:1024],
                 ALU.add, tag="f2")
            bank = pool.bf16()
            for k in range(8):
                B.transpose(bank[:, 128 * k:128 * (k + 1)],
                            h2n[:, 128 * k:128 * (k + 1)], B.ident,
                            tag="f2tr")
            for k in range(4):
                B.copy(h2nT.chunk(k, t * 128, (t + 1) * 128),
                       bank[:, 128 * k:128 * (k + 1)], tag="f2trev")
            for k in range(4, 8):
                ncc = 127 if t == NT - 1 else 128
                B.copy(h2nT.chunk(k, t * 128 + 1, t * 128 + 1 + ncc),
                       bank[:, 128 * k:128 * k + ncc], tag="f2trev")
        for k in range(4, 8):
            lgcolp = pool.f32()
            B.transpose(lgcolp[:, 0:1], lgb[0:1, 128 * k:128 * (k + 1)],
                        B.cs(1.0)[0:1, 0:1], tag="edge")
            lbcolp = pool.f32()
            B.transpose(lbcolp[:, 0:1], lbb[0:1, 128 * k:128 * (k + 1)],
                        B.cs(1.0)[0:1, 0:1], tag="edge")
            B.copy(edgec[:, 3:4], lgcolp[:, 0:1], tag="edge")
            B.copy(edgec[:, 4:5], lbcolp[:, 0:1], tag="edge")
            B.stt(h2nT.chunk(k, 0, 1), edgec[:, 3:4], edgec[:, 2:3],
                  edgec[:, 4:5], ALU.mult, ALU.add, tag="edge")
        for b in range(4):
            t0 = 512 * b
            for j in range(NKC):
                rj = min(128, HID - 128 * j)
                sj = j % 2
                st = stage[:, 0:2048]
                B.dma(st.rearrange("p (k n) -> p k n", k=8),
                      w1d[l, :, 256 * j:256 * (j + 1)].rearrange(
                          "(k p) n -> p k n", p=128), ch="w1")
                B.copy(w1b[sj].sb, st, tag="wconv")
                psa = mmB(B, pool, w1b[sj], h2nT, 0, t0, 512, tag="w1a")
                psg = mmB(B, pool, w1b[sj], h2nT, 1, t0, 512, tag="w1g")
                B.act(glscr[0:rj, 0:512], psg[0:rj, 0:512], AF.Gelu,
                      bias=b1gc[0:rj, j:j + 1], tag="gelu")
                B.stt(uT.chunk(j), psa[0:rj, 0:512], b1ac[0:rj, j:j + 1],
                      glscr[0:rj, 0:512], ALU.add, ALU.mult, tag="umul")
            for m in range(8):
                sm = m % 2
                st = stage[:, 0:NKC * 128]
                B.dma(st.rearrange("p (k n) -> p k n", k=NKC)[:, 0:21, :],
                      w2d[l, 0:2688, 128 * m:128 * (m + 1)].rearrange(
                          "(k p) n -> p k n", p=128), ch="w2a")
                B.dma(st[0:42, 21 * 128:21 * 128 + 128],
                      w2d[l, 2688:2730, 128 * m:128 * (m + 1)], ch="w2b")
                B.copy(w2b[sm].sb, st, tag="wconv")
                psy = pool.f32()
                pairs = [(w2b[sm].chunk(k), uT.chunk(k)) for k in range(NKC)]
                B.mm_group(psy[:, 0:512], pairs, tag="w2mm")
                B.ts(yTs[:, 0:512], psy[:, 0:512], b2c[:, m:m + 1], None,
                     ALU.add, tag="w2ev")
                for jt in range(4):
                    t = 4 * b + jt
                    bank = pool.bf16()
                    B.transpose(bank[:, 0:128],
                                yTs[:, 128 * jt:128 * (jt + 1)], B.ident,
                                tag="w2tr")
                    B.tt(xi.tile(t, 128 * m, 128 * (m + 1)),
                         bank[:, 0:128],
                         xi.tile(t, 128 * m, 128 * (m + 1)), ALU.add,
                         tag="w2res")

    for l in range(nlayers):
        attn_layer(l)
        ffn_layer(l)

    for t in range(NT):
        xt = stage[:, (t % 2) * 1024:(t % 2) * 1024 + 1024]
        rt = stage[:, 2048:3072]
        B.dma(xt, xind[t], ch=f"x{t % 2}")
        B.tt(rt, xt, xi.tile(t), ALU.add, tag="fres")
        B.dma(xoutd[t], rt, ch="xo", on="scalar")



def prep_l3_attn(inp):
    f32 = np.float32
    tri = np.triu(np.ones((128, 128), np.float32))
    return {
        "wqT": np.ascontiguousarray(np.asarray(inp["L3_attn_Wq"], f32).T),
        "wkT": np.ascontiguousarray(np.asarray(inp["L3_attn_Wk"], f32).T),
        "wvT": np.ascontiguousarray(np.asarray(inp["L3_attn_Wv"], f32).T),
        "woT": np.ascontiguousarray(np.asarray(inp["L3_attn_Wo"], f32).T),
        "rows": np.asarray(inp["L3_attn_bo"], f32).reshape(1, 1024),
        "trimask": tri.astype(BF16NP),
    }


def dram_l3a(nc, pfx=""):
    def dt_(n, shape, dt=F32):
        return nc.dram_tensor(pfx + n, shape, dt, kind="ExternalInput").ap()
    return dict(wqd=dt_("wqT", [1024, 1024]), wkd=dt_("wkT", [1024, 1024]),
                wvd=dt_("wvT", [1024, 1024]), wod=dt_("woT", [1024, 1024]),
                rowsd=dt_("rows", [1, 1024]),
                trid=dt_("trimask", [128, 128], BF16))


def body_l3a(B, pool, d, xin, xout):
    wqd, wkd, wvd, wod, rowsd, trid = (d["wqd"], d["wkd"], d["wvd"],
                                       d["wod"], d["rowsd"], d["trid"])
    A = B.arena(40960)
    xind = xin.rearrange("(t p) c -> t p c", p=128)
    xoutd = xout.rearrange("(t p) c -> t p c", p=128)
    stage = B.sb("stage", [128, 1024], F32)
    tri = B.sb("tri", [128, 128], BF16)
    B.dma(tri, trid, ch="tri", batch=True)
    lncols = B.sb("lncols", [128, 4 * NT], F32)
    escr = B.sb("escr", [128, 2 * 512], F32)
    S_sb = B.sb("S_sb", [128, 8 * 65], F32)
    S_bf = B.sb("S_bf", [128, 8 * 65], BF16)
    At = B.sb("At", [128, 128], BF16)
    dcol = B.sb("dcolx", [128, 2], F32)

    xn = TMT(B, "xn", T_SEQ, D_MODEL, BF16, ap=A.bf16(0, 16384))
    for t in range(NT):
        xt = stage[:, 0:1024]
        B.dma(xt, xind[t], ch="x0")
        pre_ln_tile(B, xt, xn.tile(t), lncols[:, 4 * t:4 * t + 4],
                    escr[:, 0:1024])
    xnT = FMT(B, "xnT", D_MODEL, T_SEQ, BF16, ap=A.bf16(8192, 16384))
    tm_to_fm(B, xn, xnT, pool)

    phiqT = FMT(B, None, D_MODEL, T_SEQ, BF16, ap=A.bf16(16384, 16384))
    phikT = FMT(B, None, D_MODEL, T_SEQ, BF16, ap=A.bf16(24576, 16384))
    vT_tmp = None
    phik_roll = B.sb("phik_roll", [128, 1024], BF16)
    v = TMT(B, None, T_SEQ, D_MODEL, BF16, ap=A.bf16(32768, 16384))
    wcb = [B.sb(f"wcb{i}", [128, 8 * 128], BF16) for i in range(2)]

    def proj_convB(wd_, dst_fm, elu):
        for m in range(8):
            sm = m % 2
            st = stage[:, 0:1024]
            B.dma(st.rearrange("p (k n) -> p k n", k=8),
                  wd_[:, 128 * m:128 * (m + 1)].rearrange(
                      "(k p) n -> p k n", p=128), ch=f"w{sm}")
            B.copy(wcb[sm], st, tag="wconv")
            for t0 in range(0, T_SEQ, 512):
                ps = pool.f32()
                B.mm_group(ps[:, 0:512],
                           [(wcb[sm][:, 128 * k:128 * (k + 1)],
                             xnT.chunk(k, t0, t0 + 512)) for k in range(8)],
                           tag="proj")
                if elu:
                    B.ts(escr[:, 0:512], ps[:, 0:512], 0.0, None, ALU.min,
                         tag="elu")
                    B.act(escr[:, 512:1024], escr[:, 0:512], AF.Exp,
                          tag="elu")
                    B.stt(dst_fm.chunk(m, t0, t0 + 512), ps[:, 0:512], 0.0,
                          escr[:, 512:1024], ALU.max, ALU.add, tag="elu")
                else:
                    B.copy(dst_fm.chunk(m, t0, t0 + 512), ps[:, 0:512],
                           tag="projev")
    proj_convB(wqd, phiqT, True)
    proj_convB(wkd, phikT, True)
    proj_convB(wvd, FMT(B, None, D_MODEL, T_SEQ, BF16,
                        ap=A.bf16(0, 16384)), False)
    vT = FMT(B, None, D_MODEL, T_SEQ, BF16, ap=A.bf16(0, 16384))
    for t in range(NT):
        bank = pool.bf16()
        for k in range(8):
            B.transpose(bank[:, 128 * k:128 * (k + 1)],
                        vT.chunk(k, t * 128, (t + 1) * 128), B.ident,
                        tag="vtr")
        B.copy(v.tile(t), bank[:, 0:1024], tag="vtrev")

    B.memset(S_sb, 0.0, tag="sinit")
    B.memset(S_bf, 0.0, tag="sinit")
    ao = TMT(B, None, T_SEQ, D_MODEL, BF16, ap=A.bf16(0, 16384))
    EPS = 1e-5
    for c in range(NT):
        bank = pool.bf16()
        for k in range(8):
            B.transpose(bank[:, 128 * k:128 * (k + 1)],
                        phikT.chunk(k, c * 128, (c + 1) * 128), B.ident,
                        tag="pktr")
        B.copy(phik_roll, bank[:, 0:1024], tag="pktrev")
        for h in range(16):
            hc, base = h // 2, 64 * (h % 2)
            qrows = phiqT.sb[base:base + 64,
                             hc * T_SEQ + c * 128:hc * T_SEQ + (c + 1) * 128]
            krows = phikT.sb[base:base + 64,
                             hc * T_SEQ + c * 128:hc * T_SEQ + (c + 1) * 128]
            Ssl = S_sb[base:base + 64, 65 * hc:65 * hc + 65]
            Sbl = S_bf[base:base + 64, 65 * hc:65 * hc + 65]
            pa = pool.f32()
            B.mm_group(pa[:, 0:128], [(krows, qrows)], tag="at")
            B.tt(At, pa[:, 0:128], tri, ALU.mult, tag="atm")
            pn = pool.f32()
            B.mm_group(pn[:, 0:64],
                       [(At, v.tile(c, 64 * h, 64 * h + 64)),
                        (qrows, Sbl[:, 0:64])], tag="num")
            B.mm_group(pn[:, 64:65],
                       [(At, B.onesb[:, 0:1]),
                        (qrows, Sbl[:, 64:65])], tag="den")
            B.ts(dcol[:, 0:1], pn[:, 64:65], EPS, None, ALU.add, tag="div")
            B.recip(dcol[:, 1:2], dcol[:, 0:1])
            B.ts(ao.tile(c, 64 * h, 64 * h + 64), pn[:, 0:64],
                 dcol[:, 1:2], None, ALU.mult, tag="aoev")
            pd = pool.f32()
            B.mm_group(pd[base:base + 64, 0:64],
                       [(phik_roll[:, 64 * h:64 * h + 64],
                         v.tile(c, 64 * h, 64 * h + 64))], tag="supd")
            B.mm_group(pd[base:base + 64, 64:65],
                       [(phik_roll[:, 64 * h:64 * h + 64],
                         B.onesb[:, 0:1])], tag="supd")
            B.tt(Ssl, Ssl, pd[base:base + 64, 0:65], ALU.add, tag="sadd")
            B.copy(Sbl, Ssl, tag="sbf")
    Wo = FMT(B, None, 1024, 1024, BF16, ap=A.bf16(16384, 8192))
    for k in range(8):
        st = stage[:, 0:1024]
        B.dma(st, wod[128 * k:128 * (k + 1), :], ch=f"w{k % 2}")
        B.copy(Wo.chunk(k), st, tag="wconv")
    rows = A.row(16384 + 4096, 1024)
    B.dma(rows, rowsd, ch="rows", batch=True)
    aoT = FMT(B, None, D_MODEL, T_SEQ, BF16, ap=A.bf16(8192, 16384))
    tm_to_fm(B, ao, aoT, pool)
    for t in range(NT):
        xt = stage[:, 0:1024]
        rt = escr[:, 0:1024]
        B.dma(xt, xind[t], ch="x0")
        for n0 in (0, 512):
            ps = mmA(B, pool, aoT, Wo, t, n0, 512,
                     bias_row=rows[0:1, 0:1024], tag="womm")
            B.tt(rt[:, n0:n0 + 512], ps, xt[:, n0:n0 + 512], ALU.add,
                 tag="resid")
        B.dma(xoutd[t], rt, ch=f"xo{t % 2}", on="scalar")



def prep_l2_attn(inp):
    f32 = np.float32
    wx = np.asarray(inp["L2_attn_Wx"], f32)
    wxc = np.concatenate([wx[h] for h in range(16)], axis=1)
    rows = np.zeros((1, 2048), f32)
    rows[0, 0:1024] = np.asarray(inp["L2_attn_bx"], f32).reshape(-1)
    rows[0, 1024:2048] = np.asarray(inp["L2_attn_bp"], f32)
    return {
        "wxc": wxc,
        "wp": np.asarray(inp["L2_attn_Wp"], f32),
        "rows": rows,
    }


def dram_l2a(nc, pfx=""):
    def dt_(n, shape, dt=F32):
        return nc.dram_tensor(pfx + n, shape, dt, kind="ExternalInput").ap()
    return dict(wxd=dt_("wxc", [1024, 1024]), wpd=dt_("wp", [1024, 1024]),
                rowsd=dt_("rows", [1, 2048]))


def body_l2a(B, pool, d, xin, xout):
    wxd, wpd, rowsd = d["wxd"], d["wpd"], d["rowsd"]
    A = B.arena(32768)
    xind = xin.rearrange("(t p) c -> t p c", p=128)
    xoutd = xout.rearrange("(t p) c -> t p c", p=128)
    stage = B.sb("stage", [128, 1024], F32)
    lncols = B.sb("lncols", [128, 4 * NT], F32)
    scr = B.sb("scr", [128, 1024], F32)
    bxrow = B.sb("bxrow", [1, 1024], F32)
    B.dma(bxrow, rowsd[0:1, 0:1024], ch="bx", batch=True)
    rall = B.sb("rall", [128, NT], F32)
    rrow = B.sb("rrow", [1, T_SEQ], BF16)
    rb = B.sb("rb", [128, T_SEQ], BF16)
    nrm = B.sb("nrm", [128, 16], F32)
    dtmp = B.sb("dtmp", [128, 4], F32)
    E2 = B.sb("E_sb", [128, 8 * T_SEQ], BF16)
    E1 = A.bf16(8192, 8 * T_SEQ)

    def Etile(a, c0, c1):
        e = E1 if a < 8 else E2
        a2 = a % 8
        return e[:, a2 * T_SEQ + c0:a2 * T_SEQ + c1]

    xn = TMT(B, "xn", T_SEQ, D_MODEL, BF16, ap=A.bf16(0, 16384))
    for t in range(NT):
        xt = stage[:, 0:1024]
        B.dma(xt, xind[t], ch="x0")
        pre_ln_tile(B, xt, xn.tile(t), lncols[:, 4 * t:4 * t + 4],
                    scr[:, 0:1024])
    xnT = FMT(B, "xnT", D_MODEL, T_SEQ, BF16, ap=A.bf16(8192, 16384))
    tm_to_fm(B, xn, xnT, pool)
    Wx = FMT(B, None, 1024, 1024, BF16, ap=A.bf16(0, 8192))
    for k in range(8):
        st = stage[:, 0:1024]
        B.dma(st, wxd[128 * k:128 * (k + 1), :], ch="w0")
        B.copy(Wx.chunk(k), st, tag="wconv")
    H = TMT(B, None, T_SEQ, D_MODEL, BF16, ap=A.bf16(16384, 16384))
    hn = B.sb("hnscr", [128, 1024], BF16)
    hnT = FMT(B, None, D_MODEL, T_SEQ, BF16, ap=A.bf16(24576, 16384))
    for t in range(NT):
        for n0 in (0, 512):
            ps = mmA(B, pool, xnT, Wx, t, n0, 512,
                     bias_row=bxrow[0:1, 0:1024], tag="hmm")
            B.copy(H.tile(t, n0, n0 + 512), ps, tag="hev")
        B.act(scr, H.tile(t), AF.Square, tag="hn")
        B.red(nrm, scr.rearrange("p (h d) -> p h d", h=16), ALU.add,
              tag="hn")
        B.act(nrm, nrm, AF.Sqrt, tag="hn")
        B.recip(nrm, nrm)
        B.tt(hn.rearrange("p (h d) -> p h d", h=16),
             H.tile(t).rearrange("p (h d) -> p h d", h=16),
             nrm[:, :].unsqueeze(2).broadcast_to([128, 16, 64]), ALU.mult,
             tag="hn")
        bank = pool.bf16()
        for k in range(8):
            B.transpose(bank[:, 128 * k:128 * (k + 1)],
                        hn[:, 128 * k:128 * (k + 1)], B.ident, tag="hntr")
        for k in range(8):
            B.copy(hnT.chunk(k, t * 128, (t + 1) * 128),
                   bank[:, 128 * k:128 * (k + 1)], tag="hntrev")
    OT = FMT(B, None, D_MODEL, T_SEQ, BF16, ap=A.bf16(0, 16384))
    for h in range(16):
        hc, base = h // 2, 64 * (h % 2)

        def hrows(t0, t1):
            return hnT.sb[base:base + 64, hc * T_SEQ + t0:hc * T_SEQ + t1]
        for a in range(NT):
            for b0 in range(0, T_SEQ, 512):
                ps = pool.f32()
                B.mm_group(ps[:, 0:512],
                           [(hrows(a * 128, (a + 1) * 128),
                             hrows(b0, b0 + 512))], tag="g")
                B.act(Etile(a, b0, b0 + 512), ps[:, 0:512], AF.Exp,
                      accum_out=dtmp[:, (b0 // 512):(b0 // 512) + 1],
                      tag="ge")
            B.tt(dtmp[:, 0:1], dtmp[:, 0:1], dtmp[:, 1:2], ALU.add,
                 tag="dn")
            B.tt(dtmp[:, 2:3], dtmp[:, 2:3], dtmp[:, 3:4], ALU.add,
                 tag="dn")
            B.tt(dtmp[:, 0:1], dtmp[:, 0:1], dtmp[:, 2:3], ALU.add,
                 tag="dn")
            B.recip(rall[:, a:a + 1], dtmp[:, 0:1])
        for a in range(NT):
            B.copy(scr[:, 0:1].bitcast(BF16)[:, 0:1], rall[:, a:a + 1],
                   tag="rbc")
            bank = pool.bf16()
            B.transpose(bank[0:1, 0:128], scr[:, 0:1].bitcast(BF16)[:, 0:1],
                        B.ident, tag="rtr")
            B.copy(rrow[0:1, a * 128:(a + 1) * 128], bank[0:1, 0:128],
                   tag="rtr")
        for q0 in range(0, T_SEQ, 512):
            pb = pool.f32()
            B.mm_group(pb[:, 0:512],
                       [(B.onesb[0:1, 0:128], rrow[0:1, q0:q0 + 512])],
                       tag="rbb")
            B.copy(rb[:, q0:q0 + 512], pb[:, 0:512], tag="rbb")
        for q0 in range(0, T_SEQ, 512):
            ps = pool.f32()
            B.mm_group(ps[0:64, 0:512],
                       [(H.tile(k, 64 * h, 64 * h + 64),
                         Etile(k, q0, q0 + 512)) for k in range(NT)],
                       tag="pv")
            B.tt(OT.sb[base:base + 64,
                       hc * T_SEQ + q0:hc * T_SEQ + q0 + 512],
                 ps[0:64, 0:512], rb[base:base + 64, q0:q0 + 512],
                 ALU.mult, tag="pvev")
    Wp = FMT(B, None, 1024, 1024, BF16, ap=A.bf16(8192, 8192))
    for k in range(8):
        st = stage[:, 0:1024]
        B.dma(st, wpd[128 * k:128 * (k + 1), :], ch="w0")
        B.copy(Wp.chunk(k), st, tag="wconv")
    bprow = A.row(12288, 1024)
    B.dma(bprow, rowsd[0:1, 1024:2048], ch="bp", batch=True)
    res = B.sb("res", [128, 1024], F32)
    for t in range(NT):
        xt = stage[:, 0:1024]
        rt = res[:, 0:1024]
        B.dma(xt, xind[t], ch="x0")
        for n0 in (0, 512):
            ps = mmA(B, pool, OT, Wp, t, n0, 512,
                     bias_row=bprow[0:1, 0:1024], tag="wpmm")
            B.tt(rt[:, n0:n0 + 512], ps, xt[:, n0:n0 + 512], ALU.add,
                 tag="resid")
        B.dma(xoutd[t], rt, ch="xo", on="scalar")



def prep_l2_ffn(inp):
    f32 = np.float32
    w1 = np.asarray(inp["L2_ffn_w1"], f32)
    w2 = np.asarray(inp["L2_ffn_w2"], f32)
    w3 = np.asarray(inp["L2_ffn_w3"], f32)
    ws1 = np.asarray(inp["L2_ffn_ws1"], f32)
    ws2 = np.asarray(inp["L2_ffn_ws2"], f32)
    ws3 = np.asarray(inp["L2_ffn_ws3"], f32)
    w1r = np.zeros((17, 1024, 768), f32)
    w3r = np.zeros((17, 1024, 768), f32)
    w2r = np.zeros((17, 768, 1024), f32)
    for e in range(16):
        w1r[e] = w1[e].T
        w3r[e] = w3[e].T
        w2r[e] = w2[e].T
    w1r[16] = ws1.T
    w3r[16] = ws3.T
    w2r[16] = ws2.T
    return {
        "w1r": w1r, "w3r": w3r, "w2r": w2r,
        "gwT": np.ascontiguousarray(np.asarray(inp["L2_ffn_gate_w"],
                                               f32).T),
        "gbias": np.asarray(inp["L2_ffn_gate_bias"], f32).reshape(1, 16),
    }


def dram_l2f(nc, pfx=""):
    def dt_(n, shape, dt=F32):
        return nc.dram_tensor(pfx + n, shape, dt, kind="ExternalInput").ap()
    return dict(w1d=dt_("w1r", [17, 1024, 768]),
                w3d=dt_("w3r", [17, 1024, 768]),
                w2d=dt_("w2r", [17, 768, 1024]),
                gwd=dt_("gwT", [1024, 16]), gbd=dt_("gbias", [1, 16]))


def body_l2f(B, pool, d, xin, xout):
    w1d, w3d, w2d, gwd, gbd = (d["w1d"], d["w3d"], d["w2d"], d["gwd"],
                               d["gbd"])
    A = B.arena(32768)
    xind = xin.rearrange("(t p) c -> t p c", p=128)
    xoutd = xout.rearrange("(t p) c -> t p c", p=128)
    stage = B.sb("stage", [128, 2048], F32)
    lncols = B.sb("lncols", [128, 4 * NT], F32)
    scr = B.sb("scr", [128, 1024], F32)
    rt16 = B.sb("rt16", [128, 96], F32)
    ccols = B.sb("ccols", [128, 16 * NT], F32)
    gbb = B.sb("gbb", [128, 16], F32)
    gwsb = FMT(B, "gwsb", 1024, 16, BF16)
    gwst = B.sb("gwst", [128, 128], F32)
    B.dma(gwst, gwd.rearrange("(k p) e -> p k e", p=128)
          .rearrange("p k e -> p (k e)") if False else gwst, ch="nul",
          batch=True) if False else None
    B.dma(gwst.rearrange("p (k e) -> p k e", k=8),
          gwd.rearrange("(k p) e -> p k e", p=128), ch="gw", batch=True)
    B.copy(gwsb.sb, gwst, tag="wconv")
    gbs = B.sb("gbs", [1, 16], F32)
    B.dma(gbs, gbd, ch="gb", batch=True)
    gb16 = B.sb("gb16", [1, 16], BF16)
    B.copy(gb16, gbs, tag="gb")
    pb = pool.f32()
    B.mm_group(pb[:, 0:16], [(B.onesb[0:1, 0:128], gb16[0:1, :])], tag="gbb")
    B.copy(gbb, pb[:, 0:16], tag="gbb")

    xn = TMT(B, "xn", T_SEQ, D_MODEL, BF16, ap=A.bf16(0, 16384))
    for t in range(NT):
        xt = stage[:, 0:1024]
        B.dma(xt, xind[t], ch="x0")
        pre_ln_tile(B, xt, xn.tile(t), lncols[:, 4 * t:4 * t + 4],
                    scr[:, 0:1024])
    xnT = FMT(B, "xnT", D_MODEL, T_SEQ, BF16, ap=A.bf16(8192, 16384))
    tm_to_fm(B, xn, xnT, pool)

    for t in range(NT):
        sc = rt16[:, 0:16]
        sb_ = rt16[:, 16:32]
        m1 = rt16[:, 32:36]
        mm_ = rt16[:, 36:52]
        gsum = rt16[:, 52:56]
        gcol = rt16[:, 56:57]
        gcol2 = rt16[:, 57:58]
        gmask = rt16[:, 58:62]
        ml = rt16[:, 62:78]
        sel = rt16[:, 78:94]
        fbi = rt16[:, 36:52]
        ps = mmA(B, pool, xnT, gwsb, t, 0, 16, tag="logit")
        B.act(sc, ps, AF.Sigmoid, tag="rt")
        B.tt(sb_, sc, gbb, ALU.add, tag="rt")
        gv = sb_.rearrange("p (g e) -> p g e", g=4)
        B.red(m1, gv, ALU.max, tag="rt")
        B.tt(mm_.rearrange("p (g e) -> p g e", g=4), gv,
             m1[:, :].unsqueeze(2).broadcast_to([128, 4, 4]), ALU.is_ge,
             tag="rt")
        B.stt(mm_, mm_, -2e9, sb_, ALU.mult, ALU.add, tag="rt")
        B.red(gsum, mm_.rearrange("p (g e) -> p g e", g=4), ALU.max,
              tag="rt")
        B.tt(gsum, gsum, m1, ALU.add, tag="rt")
        B.red(gcol, gsum, ALU.max, tag="rt")
        B.ts(gmask, gsum, gcol, None, ALU.is_ge, tag="rt")
        B.stt(rt16[:, 36:40], gmask, -2e9, gsum, ALU.mult, ALU.add, tag="rt")
        B.red(gcol2, rt16[:, 36:40], ALU.max, tag="rt")
        B.ts(rt16[:, 40:44], rt16[:, 36:40], gcol2, None, ALU.is_ge,
             tag="rt")
        B.tt(gmask, gmask, rt16[:, 40:44], ALU.add, tag="rt")
        emv = gmask.unsqueeze(2).broadcast_to([128, 4, 4])
        B.tt(ml.rearrange("p (g e) -> p g e", g=4), sb_.rearrange(
            "p (g e) -> p g e", g=4), emv, ALU.mult, tag="rt")
        B.ts(rt16[:, 36:40], gmask, -1.0, 1e30, ALU.add, ALU.mult, tag="rt")
        B.tt(ml.rearrange("p (g e) -> p g e", g=4),
             ml.rearrange("p (g e) -> p g e", g=4),
             rt16[:, 36:40].unsqueeze(2).broadcast_to([128, 4, 4]),
             ALU.add, tag="rt")
        B.memset(sel, 0.0, tag="rt")
        for it in range(4):
            B.red(gcol, ml, ALU.max, tag="rt")
            B.ts(fbi, ml, gcol, None, ALU.is_ge, tag="rt")
            B.tt(sel, sel, fbi, ALU.add, tag="rt")
            B.stt(ml, fbi, -1e31, ml, ALU.mult, ALU.add, tag="rt")
        B.tt(sel, sel, sc, ALU.mult, tag="rt")
        B.red(gcol, sel, ALU.add, tag="rt")
        B.ts(gcol, gcol, 1e-20, None, ALU.add, tag="rt")
        B.recip(gcol2, gcol)
        B.ts(ccols[:, 16 * t:16 * t + 16], sel, gcol2, 2.5, ALU.mult,
             ALU.mult, tag="rt")
    facc = TMT(B, None, T_SEQ, D_MODEL, F32, ap=A.f32(16384, 16384))
    hactT = FMT(B, None, 768, T_SEQ, BF16, ap=A.bf16(0, 12288))
    w2sb = FMT(B, "w2sb", 768, 1024, BF16)
    wcb = [FMT(B, f"wcb{i}", 1024, 128, BF16) for i in range(4)]
    gl = B.sb("gl", [128, 512], BF16)
    for e in range(17):
        for m in range(6):
            s1, s3 = 2 * (m % 2), 2 * (m % 2) + 1
            st1 = stage[:, 0:1024]
            st3 = stage[:, 1024:2048]
            B.dma(st1.rearrange("p (k n) -> p k n", k=8),
                  w1d[e, :, 128 * m:128 * (m + 1)].rearrange(
                      "(k p) n -> p k n", p=128), ch="w1")
            B.copy(wcb[s1].sb, st1, tag="wconv")
            B.dma(st3.rearrange("p (k n) -> p k n", k=8),
                  w3d[e, :, 128 * m:128 * (m + 1)].rearrange(
                      "(k p) n -> p k n", p=128), ch="w3")
            B.copy(wcb[s3].sb, st3, tag="wconv")
            for t0 in range(0, T_SEQ, 512):
                pg = pool.f32()
                B.mm_group(pg[:, 0:512],
                           [(wcb[s1].chunk(k), xnT.chunk(k, t0, t0 + 512))
                            for k in range(8)], tag="w1mm")
                pu = pool.f32()
                B.mm_group(pu[:, 0:512],
                           [(wcb[s3].chunk(k), xnT.chunk(k, t0, t0 + 512))
                            for k in range(8)], tag="w3mm")
                B.act(gl, pg[:, 0:512], AF.Silu, tag="silu")
                B.tt(hactT.chunk(m, t0, t0 + 512), pu[:, 0:512], gl,
                     ALU.mult, tag="hact")
        for k in range(6):
            st = stage[:, 0:1024]
            B.dma(st, w2d[e, 128 * k:128 * (k + 1), :], ch="w2")
            B.copy(w2sb.chunk(k), st, tag="wconv")
        for t in range(NT):
            cc = ccols[:, 16 * t + e:16 * t + e + 1] if e < 16 else None
            for n0 in (0, 512):
                ps = pool.f32()
                B.mm_group(ps[:, 0:512],
                           [(hactT.chunk(k, t * 128, (t + 1) * 128),
                             w2sb.chunk(k, n0, n0 + 512))
                            for k in range(6)], tag="w2mm")
                ft = facc.tile(t, n0, n0 + 512)
                if e == 0:
                    B.ts(ft, ps[:, 0:512], cc, None, ALU.mult, tag="facc")
                elif e < 16:
                    B.stt(ft, ps[:, 0:512], cc, ft, ALU.mult, ALU.add,
                          tag="facc")
                else:
                    B.tt(ft, ps[:, 0:512], ft, ALU.add, tag="facc")
    res = B.sb("res", [128, 1024], F32)
    for t in range(NT):
        xt = stage[:, 0:1024]
        B.dma(xt, xind[t], ch="x0")
        B.tt(res, facc.tile(t), xt, ALU.add, tag="resid")
        B.dma(xoutd[t], res, ch="xo", on="scalar")



GA_COLS = 52300


def _standalone(nc, dram_fn, body_fn):
    d = dram_fn(nc)
    xin = nc.dram_tensor("xin", [T_SEQ, D_MODEL], F32,
                         kind="ExternalInput").ap()
    xout = nc.dram_tensor("xout", [T_SEQ, D_MODEL], F32,
                          kind="ExternalOutput").ap()
    B = Builder(nc)
    pool = PsumPool(B, n_f32=4)
    B.init_ga(GA_COLS)
    B.block_begin("blk")
    body_fn(B, pool, d, xin, xout)
    B.finalize()
    return B


def build_l0_attn(nc):
    return _standalone(nc, dram_l0a, body_l0a)


def build_l0_ffn(nc):
    return _standalone(nc, dram_l0f, body_l0f)


def build_l1_attn(nc):
    return _standalone(nc, dram_l1a, body_l1a)


def build_local_stack(nc, nlayers, heads, win):
    return _standalone(
        nc, lambda n_: dram_local(n_, nlayers, heads, win),
        lambda B, pool, d, xi_, xo_: body_local(B, pool, d, xi_, xo_,
                                                nlayers, heads, win))


def build_l3_attn(nc):
    return _standalone(nc, dram_l3a, body_l3a)


def build_l2_attn(nc):
    return _standalone(nc, dram_l2a, body_l2a)


def build_l2_ffn(nc):
    return _standalone(nc, dram_l2f, body_l2f)


_SPECS = [
    ("l0a", dram_l0a, body_l0a, prep_l0_attn),
    ("l0f", dram_l0f, body_l0f, prep_l0_ffn),
    ("l1a", dram_l1a, body_l1a, prep_l1_attn),
    ("l1f", lambda n_, pfx="": dram_local(n_, 2, 8, 512, pfx),
     lambda B, pool, d, xi_, xo_: body_local(B, pool, d, xi_, xo_, 2, 8,
                                             512),
     lambda i: prep_local_stack("L1", i, 2, 512)),
    ("l2a", dram_l2a, body_l2a, prep_l2_attn),
    ("l2f", dram_l2f, body_l2f, prep_l2_ffn),
    ("l3a", dram_l3a, body_l3a, prep_l3_attn),
    ("l3f", lambda n_, pfx="": dram_local(n_, 6, 16, 256, pfx),
     lambda B, pool, d, xi_, xo_: body_local(B, pool, d, xi_, xo_, 6, 16,
                                             256),
     lambda i: prep_local_stack("L3", i, 6, 256)),
]


def build_fused(nc):
    B = Builder(nc)
    pool = PsumPool(B, n_f32=4)
    B.init_ga(GA_COLS)
    xin = nc.dram_tensor("xin", [T_SEQ, D_MODEL], F32,
                         kind="ExternalInput").ap()
    xout = nc.dram_tensor("xout", [T_SEQ, D_MODEL], F32,
                          kind="ExternalOutput").ap()
    xm = [nc.dram_tensor(f"xmid{i}", [T_SEQ, D_MODEL], F32,
                         kind="Internal").ap() for i in range(2)]
    chain = [xin] + [xm[i % 2] for i in range(7)] + [xout]
    for i, spec in enumerate(_SPECS):
        pfx, dfn, bfn = spec[0], spec[1], spec[2]
        d = dfn(nc, pfx + "_")
        B.block_begin(pfx)
        bfn(B, pool, d, chain[i], chain[i + 1])
        if i < len(_SPECS) - 1:
            B.barrier()
    B.finalize()
    return B



_CACHE = {}


def kernel(**inputs):
    x = np.ascontiguousarray(np.asarray(inputs["L0_attn_x"], np.float32))
    n = x.shape[0]
    if "fused" not in _CACHE:
        nc = bass.Bass("TRN2", target_bir_lowering=False)
        Bd = build_fused(nc)
        _CACHE["fused"] = (nc, Bd)
    nc, Bd = _CACHE["fused"]
    consts = Bd.host_consts()
    wmap = dict(consts)
    for spec in _SPECS:
        w = spec[3](inputs)
        for k, v in w.items():
            wmap[spec[0] + "_" + k] = v
    in_maps = [dict(xin=np.ascontiguousarray(x[i]), **wmap)
               for i in range(n)]
    res = run_bass_kernel_spmd(nc, in_maps, core_ids=list(range(n)))
    return np.stack([np.asarray(r["xout"]) for r in res.results])
